# version 08
# baseline (speedup 1.0000x reference)
import numpy as np
import ml_dtypes
from contextlib import ExitStack

import concourse.bass as bass
import concourse.mybir as mybir
from concourse.bass_utils import run_bass_kernel_spmd

F32 = mybir.dt.float32
BF16 = mybir.dt.bfloat16
AF = mybir.ActivationFunctionType
ALU = mybir.AluOpType

S_FULL = 8192
D_FULL = 2048
H_FULL = 16
I_FULL = 8192
R = 4
WQ = 512
DH = 128

BF16NP = ml_dtypes.bfloat16



class Opr:
    __slots__ = ("eng", "fn", "dma", "deps", "needs_inc", "inc_val", "dma_val",
                 "tag")

    def __init__(self, eng, fn, dma, tag=""):
        self.eng = eng
        self.fn = fn
        self.dma = dma
        self.deps = []
        self.needs_inc = False
        self.inc_val = None
        self.dma_val = None
        self.tag = tag


class TBuf:
    __slots__ = ("last_w", "readers", "name")

    def __init__(self, name=""):
        self.last_w = None
        self.readers = []
        self.name = name

    def r(self, op):
        if self.last_w is not None:
            op.deps.append(self.last_w)
        self.readers.append(op)

    def w(self, op):
        if self.last_w is not None:
            op.deps.append(self.last_w)
        op.deps.extend(self.readers)
        self.last_w = op
        self.readers = []


class Sched:
    ENGINES = ("tensor", "scalar", "vector", "gpsimd", "sync")

    def __init__(self):
        self.ops = []
        self.last_by_eng = {}
        self.pending_barrier = None

    SERIALIZE = False

    def op(self, eng, fn, reads=(), writes=(), dma=False, deps=(), tag=""):
        o = Opr(eng, fn, dma, tag)
        if Sched.SERIALIZE and self.ops:
            o.deps.append(self.ops[-1])
        if self.pending_barrier is not None:
            o.deps.extend(self.pending_barrier.get("__all__", []))
            if not self.pending_barrier.get(eng, False):
                self.pending_barrier[eng] = True
            if all(self.pending_barrier.get(e, e not in self.last_by_eng)
                   for e in self.ENGINES):
                self.pending_barrier = None
        o.deps.extend(deps)
        for b in reads:
            b.r(o)
        for b in writes:
            b.w(o)
        self.ops.append(o)
        self.last_by_eng[eng] = o
        return o

    def barrier(self):
        lasts = [op for op in self.last_by_eng.values()]
        self.pending_barrier = {"__all__": lasts}

    NQSEM = 8

    def emit(self, nc, ctx):
        R = Sched.NQSEM
        qcount = {}
        for o in self.ops:
            if o.dma:
                n = qcount.get(o.eng, 0)
                o.dma_val = (n % R, 16 * (n // R + 1))
                qcount[o.eng] = n + 1
        for o in self.ops:
            for d in o.deps:
                if (not d.dma) and d.eng != o.eng:
                    d.needs_inc = True
        ecount = {}
        for o in self.ops:
            if (not o.dma) and o.needs_inc:
                ecount[o.eng] = ecount.get(o.eng, 0) + 1
                o.inc_val = ecount[o.eng]

        esem = {}
        qsem = {}
        for e in self.ENGINES:
            if ecount.get(e):
                esem[e] = ctx.enter_context(nc.semaphore(f"se_{e}"))
            if qcount.get(e):
                qsem[e] = [ctx.enter_context(nc.semaphore(f"sq_{e}{i}"))
                           for i in range(R)]

        by_eng = {e: [o for o in self.ops if o.eng == e] for e in self.ENGINES}

        def emit_engine(handle, ename):
            seen = {}
            for o in by_eng[ename]:
                waits = {}
                if o.dma:
                    sub, val = o.dma_val
                    if val > 16:
                        waits[("q", ename, sub)] = val - 16
                for d in o.deps:
                    if d.dma:
                        sub, val = d.dma_val
                        key = ("q", d.eng, sub)
                    else:
                        if d.eng == ename:
                            continue
                        if d.inc_val is None:
                            continue
                        key = ("e", d.eng, 0)
                        val = d.inc_val
                    if waits.get(key, 0) < val:
                        waits[key] = val
                for (kind, de, sub), val in sorted(waits.items()):
                    sem = qsem[de][sub] if kind == "q" else esem[de]
                    if seen.get((kind, de, sub), 0) < val:
                        handle.wait_ge(sem, val)
                        seen[(kind, de, sub)] = val
                ins = o.fn(handle)
                if o.dma:
                    ins.then_inc(qsem[ename][o.dma_val[0]], 16)
                elif o.needs_inc:
                    ins.then_inc(esem[ename], 1)

        with nc.Block() as block:
            if by_eng["sync"]:
                @block.sync
                def _(e):
                    emit_engine(e, "sync")
            if by_eng["tensor"]:
                @block.tensor
                def _(e):
                    emit_engine(e, "tensor")
            if by_eng["scalar"]:
                @block.scalar
                def _(e):
                    emit_engine(e, "scalar")
            if by_eng["vector"]:
                @block.vector
                def _(e):
                    emit_engine(e, "vector")
            if by_eng["gpsimd"]:
                @block.gpsimd
                def _(e):
                    emit_engine(e, "gpsimd")



def build_program(S, D, H, I):
    DC = D // 128
    TQ = S // R
    NW = TQ // WQ
    CPW = (R * WQ) // 128
    NT = S // 512
    NQT = TQ // 512
    NI = I // 128
    QUADS = I // 512
    scale = 1.0 / np.sqrt(DH)

    nc = bass.Bass("TRN2", target_bir_lowering=False)
    ctx = ExitStack()

    xT = nc.dram_tensor("xT", [D, S], F32, kind="ExternalInput")
    xqT = nc.dram_tensor("xqT", [D, TQ], F32, kind="ExternalInput")
    xq_rows = nc.dram_tensor("xq_rows", [TQ, D], F32, kind="ExternalInput")
    wq_h = nc.dram_tensor("wq", [D, D], F32, kind="ExternalInput")
    wk_h = nc.dram_tensor("wk", [D, D], F32, kind="ExternalInput")
    wv_h = nc.dram_tensor("wv", [D, D], F32, kind="ExternalInput")
    wo_h = nc.dram_tensor("wo", [D, D], F32, kind="ExternalInput")
    wgh = nc.dram_tensor("wgh", [NI, 128, D], F32, kind="ExternalInput")
    wuh = nc.dram_tensor("wuh", [NI, 128, D], F32, kind="ExternalInput")
    wdh = nc.dram_tensor("wdh", [NI, 128, D], F32, kind="ExternalInput")
    masks_h = nc.dram_tensor("masks", [CPW, 128, WQ], F32, kind="ExternalInput")

    out_h = nc.dram_tensor("out", [TQ, D], F32, kind="ExternalOutput")

    xn_dram = nc.dram_tensor("xn_dram", [DC, 128, S], BF16, kind="Internal")
    k_dram = nc.dram_tensor("k_dram", [H, 128, S], BF16, kind="Internal")
    v_dram = nc.dram_tensor("v_dram", [S // 128, 128, D], BF16, kind="Internal")
    q_dram = nc.dram_tensor("q_dram", [H, 128, TQ], BF16, kind="Internal")
    xnq_dram = nc.dram_tensor("xnq_dram", [DC, 128, TQ], BF16, kind="Internal")
    attn_dram = nc.dram_tensor("attn_dram", [TQ, D], F32, kind="Internal")
    hsp_dram = nc.dram_tensor("hsp_dram", [NI, 128, TQ], BF16, kind="Internal")
    wdb_dram = nc.dram_tensor("wdb_dram", [NI, 128, D], BF16, kind="Internal")

    sch = Sched()

    g = ExitStack()
    ones_b = g.enter_context(nc.sbuf_tensor("ones_b", [128, 2], BF16))
    ones_f = g.enter_context(nc.sbuf_tensor("ones_f", [1, 128], F32))
    ones_b_t = TBuf("ones_b")
    ones_f_t = TBuf("ones_f")
    sch.op("vector", lambda e: e.memset(ones_b[:, :], 1.0), writes=[ones_b_t])
    sch.op("vector", lambda e: e.memset(ones_f[:, :], 1.0), writes=[ones_f_t])
    eps_sb = g.enter_context(nc.sbuf_tensor("eps_sb", [1, 2], F32))
    eps_t = TBuf("eps")
    sch.op("vector", lambda e: e.memset(eps_sb[:, :], 1e-5), writes=[eps_t])
    onec_sb = g.enter_context(nc.sbuf_tensor("onec_sb", [128, 2], F32))
    onec_t = TBuf("onec")
    sch.op("vector", lambda e: e.memset(onec_sb[:, :], 1.0), writes=[onec_t])

    kmax = g.enter_context(nc.sbuf_tensor("kmax", [1, max(H, 2)], F32))
    qmax = g.enter_context(nc.sbuf_tensor("qmax", [1, max(H, 2)], F32))
    kmax_t = [TBuf(f"kmax{h}") for h in range(H)]
    qmax_t = [TBuf(f"qmax{h}") for h in range(H)]
    sch.op("vector", lambda e: e.memset(kmax[:, :], 0.0), writes=kmax_t)
    sch.op("vector", lambda e: e.memset(qmax[:, :], 0.0), writes=qmax_t)

    p = ExitStack()
    xb = [p.enter_context(nc.sbuf_tensor(f"xb{s}", [128, DC * 512], BF16))
          for s in range(2)]
    x2b = [p.enter_context(nc.sbuf_tensor(f"x2b{s}", [128, 512], BF16))
           for s in range(2)]
    xn_sb = [p.enter_context(nc.sbuf_tensor(f"xn{s}", [128, DC * 512], BF16))
             for s in range(2)]
    wkb = p.enter_context(nc.sbuf_tensor("wkb", [128, DC * D], BF16))
    wst = p.enter_context(nc.sbuf_tensor("wst", [128, 2 * D], F32))
    kst = [p.enter_context(nc.sbuf_tensor(f"kst{s}", [128, 512], BF16))
           for s in range(2)]
    k2b = [p.enter_context(nc.sbuf_tensor(f"k2b{s}", [128, 512], BF16))
           for s in range(2)]
    ab_s = [p.enter_context(nc.sbuf_tensor(f"ab{s}", [1, 3 * 512], F32))
            for s in range(1)]
    ps_st = p.enter_context(nc.psum_tensor("ps_st", [1, 1024], F32))
    ps_ab = p.enter_context(nc.psum_tensor("ps_ab", [128, 1024], F32))
    ps_k = [p.enter_context(nc.psum_tensor(f"ps_k{s}", [128, 512], F32))
            for s in range(2)]
    ps_n = p.enter_context(nc.psum_tensor("ps_n", [1, 512], F32))

    xb_t = [TBuf() for _ in range(2)]
    x2b_t = [TBuf() for _ in range(2)]
    xn_t = [TBuf() for _ in range(2)]
    wkb_t = TBuf("wkb")
    kst_t = [TBuf(), TBuf()]
    k2b_t = [TBuf() for _ in range(2)]
    ab_t = [TBuf() for _ in range(1)]
    ps_st_t = TBuf("ps_st")
    ps_ab_t = TBuf("ps_ab")
    ps_k_t = [TBuf() for _ in range(2)]
    ps_n_t = TBuf("ps_n")
    xn_dram_t = TBuf("xn_dram")
    k_dram_t = TBuf("k_dram")

    wstage = [wst[:, 0:D], wst[:, D:2 * D]]
    wstage_t = [TBuf(), TBuf()]

    def load_weight_cast(w_h, dst_sb, dst_t):
        for cch in range(DC):
            sl = cch % 2

            def ldf(e, cch=cch, sl=sl):
                return e.dma_start(
                    out=wstage[sl],
                    in_=w_h.ap()[cch * 128:(cch + 1) * 128, :])
            o = sch.op("sync", ldf, writes=[wstage_t[sl]], dma=True)

            def cst(e, cch=cch, sl=sl):
                return e.tensor_copy(
                    out=dst_sb[:, cch * D:(cch + 1) * D], in_=wstage[sl])
            sch.op("vector", cst, reads=[wstage_t[sl]], writes=[dst_t])

    load_weight_cast(wk_h, wkb, wkb_t)

    inv_d = 1.0 / D

    def emit_front1(tt):
        sl = tt % 2
        BCH = DC // 4
        for ph in range(4):
            xsl = (tt * 4 + ph) % 2

            def ldx(e, tt=tt, ph=ph, xsl=xsl, BCH=BCH):
                return e.dma_start(
                    out=wst.ap()[:, xsl * D: xsl * D + BCH * 512]
                        .rearrange("p (c n) -> p c n", c=BCH),
                    in_=xT.ap()[ph * BCH * 128:(ph + 1) * BCH * 128,
                                tt * 512:(tt + 1) * 512]
                        .rearrange("(c p) n -> p c n", p=128))
            sch.op("sync", ldx, writes=[wstage_t[xsl]], dma=True)

            for i in range(BCH):
                cch = ph * BCH + i

                def cstx(e, cch=cch, i=i, xsl=xsl, sl=sl):
                    return e.tensor_copy(
                        out=xb[sl][:, cch * 512:(cch + 1) * 512],
                        in_=wst[:, xsl * D + i * 512: xsl * D + i * 512 + 512])
                sch.op("vector", cstx, reads=[wstage_t[xsl]],
                       writes=[xb_t[sl]])

        for cch in range(DC):
            def sq(e, cch=cch, sl=sl):
                return e.activation(
                    out=x2b[sl][:, :],
                    in_=xb[sl][:, cch * 512:(cch + 1) * 512], func=AF.Square)
            sch.op("scalar", sq, reads=[xb_t[sl]], writes=[x2b_t[sl]])

            def mm_s(e, cch=cch, sl=sl):
                return e.matmul(ps_st[:, 0:512], ones_b[:, 0:1],
                                xb[sl][:, cch * 512:(cch + 1) * 512],
                                start=(cch == 0), stop=(cch == DC - 1))
            sch.op("tensor", mm_s, reads=[xb_t[sl], ones_b_t],
                   writes=[ps_st_t])

            def mm_q(e, cch=cch, sl=sl):
                return e.matmul(ps_st[:, 512:1024], ones_b[:, 0:1],
                                x2b[sl][:, :],
                                start=(cch == 0), stop=(cch == DC - 1))
            sch.op("tensor", mm_q, reads=[x2b_t[sl], ones_b_t],
                   writes=[ps_st_t])


    def emit_front2(tt):
        sl = tt % 2
        asl = 0
        m_v = ab_s[asl][:, 0:512]
        A_v = ab_s[asl][:, 512:1024]
        B_v = ab_s[asl][:, 1024:1536]

        def mmean(e, asl=asl):
            return e.mul(out=ab_s[asl][:, 0:512], in_=ps_st[:, 0:512], mul=inv_d)
        sch.op("scalar", mmean, reads=[ps_st_t], writes=[ab_t[asl]])

        def m2f(e, asl=asl):
            return e.tensor_tensor(out=ab_s[asl][:, 1024:1536],
                                   in0=ab_s[asl][:, 0:512],
                                   in1=ab_s[asl][:, 0:512], op=ALU.mult)
        sch.op("vector", m2f, reads=[ab_t[asl]], writes=[ab_t[asl]])

        def mvar2(e, asl=asl):
            return e.scalar_tensor_tensor(
                out=ab_s[asl][:, 512:1024], in0=ps_st[:, 512:1024],
                scalar=inv_d, in1=ab_s[asl][:, 1024:1536],
                op0=ALU.mult, op1=ALU.subtract)
        sch.op("vector", mvar2, reads=[ps_st_t, ab_t[asl]], writes=[ab_t[asl]])

        def lnv(e, asl=asl):
            return e.activation(out=ab_s[asl][:, 512:1024],
                                in_=ab_s[asl][:, 512:1024],
                                func=AF.Ln, bias=eps_sb[:, 0:1])
        sch.op("scalar", lnv, reads=[ab_t[asl], eps_t], writes=[ab_t[asl]])

        def expv(e, asl=asl):
            return e.activation(out=ab_s[asl][:, 512:1024],
                                in_=ab_s[asl][:, 512:1024],
                                func=AF.Exp, scale=-0.5)
        sch.op("scalar", expv, reads=[ab_t[asl]], writes=[ab_t[asl]])

        def bfn(e, asl=asl):
            return e.scalar_tensor_tensor(
                out=ab_s[asl][:, 1024:1536], in0=ab_s[asl][:, 0:512],
                scalar=-1.0, in1=ab_s[asl][:, 512:1024],
                op0=ALU.mult, op1=ALU.mult)
        sch.op("vector", bfn, reads=[ab_t[asl]], writes=[ab_t[asl]])

        def bca(e, asl=asl):
            return e.matmul(ps_ab[:, 0:512], ones_f[:, :],
                            ab_s[asl][:, 512:1024], start=True, stop=True)
        sch.op("tensor", bca, reads=[ab_t[asl], ones_f_t], writes=[ps_ab_t])

        def bcb(e, asl=asl):
            return e.matmul(ps_ab[:, 512:1024], ones_f[:, :],
                            ab_s[asl][:, 1024:1536], start=True, stop=True)
        sch.op("tensor", bcb, reads=[ab_t[asl], ones_f_t], writes=[ps_ab_t])

        for cch in range(DC):
            def xnm(e, cch=cch, sl=sl):
                return e.tensor_tensor(
                    out=xn_sb[sl][:, cch * 512:(cch + 1) * 512],
                    in0=xb[sl][:, cch * 512:(cch + 1) * 512],
                    in1=ps_ab[:, 0:512], op=ALU.mult)
            sch.op("vector", xnm, reads=[xb_t[sl], ps_ab_t], writes=[xn_t[sl]])

            def xna(e, cch=cch, sl=sl):
                return e.tensor_tensor(
                    out=xn_sb[sl][:, cch * 512:(cch + 1) * 512],
                    in0=xn_sb[sl][:, cch * 512:(cch + 1) * 512],
                    in1=ps_ab[:, 512:1024], op=ALU.add)
            sch.op("vector", xna, reads=[xn_t[sl], ps_ab_t], writes=[xn_t[sl]])

        def xnw(e, tt=tt, sl=sl):
            return e.dma_start(
                out=xn_dram.ap()[:, :, tt * 512:(tt + 1) * 512]
                    .rearrange("c p n -> p c n"),
                in_=xn_sb[sl].ap().rearrange("p (c n) -> p c n", c=DC))
        sch.op("gpsimd", xnw, reads=[xn_t[sl]], writes=[xn_dram_t], dma=True)


    def emit_K1a(tt):
        sl = tt % 2
        for jc in range(H):
            kp = jc % 2

            for cch in range(DC):
                def mk(e, jc=jc, sl=sl, kp=kp, cch=cch):
                    return e.matmul(
                        ps_k[kp][:, :],
                        wkb[:, cch * D + jc * 128: cch * D + jc * 128 + 128],
                        xn_sb[sl][:, cch * 512:(cch + 1) * 512],
                        start=(cch == 0), stop=(cch == DC - 1))
                sch.op("tensor", mk, reads=[wkb_t, xn_t[sl]],
                       writes=[ps_k_t[kp]])

            def evk(e, jc=jc, kp=kp):
                return e.copy(out=kst[kp][:, :], in_=ps_k[kp][:, :])
            sch.op("scalar", evk, reads=[ps_k_t[kp]], writes=[kst_t[kp]])

            def kwh(e, jc=jc, kp=kp, tt=tt):
                return e.dma_start(
                    out=k_dram.ap()[jc, :, tt * 512:(tt + 1) * 512],
                    in_=kst[kp][:, :])
            sch.op("gpsimd", kwh, reads=[kst_t[kp]], writes=[k_dram_t],
                   dma=True)

            def k2f(e, jc=jc, kp=kp):
                return e.tensor_tensor(out=k2b[kp][:, :],
                                       in0=kst[kp][:, :],
                                       in1=kst[kp][:, :],
                                       op=ALU.mult)
            sch.op("vector", k2f, reads=[kst_t[kp]], writes=[k2b_t[kp]])

            def nmm(e, jc=jc, kp=kp):
                return e.matmul(ps_n[:, :], ones_b[:, 0:1], k2b[kp][:, :],
                                start=True, stop=True)
            sch.op("tensor", nmm, reads=[k2b_t[kp], ones_b_t], writes=[ps_n_t])

            def nred2(e, jc=jc):
                return e.tensor_reduce(out=ab_s[0][:, 0:1], in_=ps_n[:, :],
                                       axis=mybir.AxisListType.X, op=ALU.max)
            sch.op("vector", nred2, reads=[ps_n_t], writes=[ab_t[0]], tag="kn")

            def nacc(e, jc=jc):
                return e.tensor_tensor(out=kmax[:, jc:jc + 1],
                                       in0=kmax[:, jc:jc + 1],
                                       in1=ab_s[0][:, 0:1], op=ALU.max)
            sch.op("vector", nacc, reads=[ab_t[0], kmax_t[jc]],
                   writes=[kmax_t[jc]])



    emit_front1(0)
    emit_front2(0)
    for tt in range(NT):
        if tt + 1 < NT:
            emit_front1(tt + 1)
        emit_K1a(tt)
        if tt + 1 < NT:
            emit_front2(tt + 1)

    p.close()
    sch.barrier()

    p = ExitStack()
    xnv = [p.enter_context(nc.sbuf_tensor(f"xnv{s}", [128, DC * 512], BF16))
           for s in range(2)]
    wvb = p.enter_context(nc.sbuf_tensor("wvb", [128, DC * D], BF16))
    wst2 = p.enter_context(nc.sbuf_tensor("wst2", [128, 2 * D], F32))
    vst = p.enter_context(nc.sbuf_tensor("vst", [128, 4 * D], BF16))
    ps_v = [p.enter_context(nc.psum_tensor(f"ps_v{s}", [128, 512], F32))
            for s in range(2)]

    xnv_t = [TBuf() for _ in range(2)]
    wvb_t = TBuf("wvb")
    vst_t = TBuf("vst")
    ps_v_t = [TBuf() for _ in range(2)]
    wst2_t = [TBuf(), TBuf()]
    v_dram_t = TBuf("v_dram")

    for cch in range(DC):
        sl2 = cch % 2

        def ldw2(e, cch=cch, sl2=sl2):
            return e.dma_start(out=wst2[:, sl2 * D:(sl2 + 1) * D],
                               in_=wv_h.ap()[cch * 128:(cch + 1) * 128, :])
        sch.op("sync", ldw2, writes=[wst2_t[sl2]], dma=True)

        def cstw2(e, cch=cch, sl2=sl2):
            return e.tensor_copy(out=wvb[:, cch * D:(cch + 1) * D],
                                 in_=wst2[:, sl2 * D:(sl2 + 1) * D])
        sch.op("vector", cstw2, reads=[wst2_t[sl2]], writes=[wvb_t])

    wdstg = [p.enter_context(nc.sbuf_tensor(f"wdstg1a{s}", [128, D], F32))
             for s in range(2)]
    wdbs = [p.enter_context(nc.sbuf_tensor(f"wdbs1a{s}", [128, D], BF16))
            for s in range(2)]
    wdstg1_t = [TBuf(), TBuf()]
    wdbs1_t = [TBuf(), TBuf()]
    wdb_dram_t = TBuf()

    def emit_wd_cast(ck):
        sl2 = ck % 2

        def ldwd(e, ck=ck, sl2=sl2):
            return e.dma_start(out=wdstg[sl2][:, :], in_=wdh.ap()[ck, :, :])
        sch.op("sync", ldwd, writes=[wdstg1_t[sl2]], dma=True)

        def cstwd(e, ck=ck, sl2=sl2):
            return e.tensor_copy(out=wdbs[sl2][:, :], in_=wdstg[sl2][:, :])
        sch.op("vector", cstwd, reads=[wdstg1_t[sl2]], writes=[wdbs1_t[sl2]])

        def stwd(e, ck=ck, sl2=sl2):
            return e.dma_start(out=wdb_dram.ap()[ck, :, :],
                               in_=wdbs[sl2][:, :])
        sch.op("scalar", stwd, reads=[wdbs1_t[sl2]], writes=[wdb_dram_t],
               dma=True)

    NJ = D // 512
    for tt in range(NT):
        sl = tt % 2

        def ldxn(e, tt=tt, sl=sl):
            return e.dma_start(
                out=xnv[sl].ap().rearrange("p (c n) -> p c n", c=DC),
                in_=xn_dram.ap()[:, :, tt * 512:(tt + 1) * 512]
                    .rearrange("c p n -> p c n"))
        sch.op("sync", ldxn, reads=[xn_dram_t], writes=[xnv_t[sl]], dma=True)

        for tb in range(4):
            for js in range(NJ):
                vp = (tb * NJ + js) % 2
                for cch in range(DC):
                    def mv(e, tt=tt, sl=sl, tb=tb, js=js, vp=vp, cch=cch):
                        return e.matmul(
                            ps_v[vp][:, :],
                            xnv[sl][:, cch * 512 + tb * 128:
                                    cch * 512 + tb * 128 + 128],
                            wvb[:, cch * D + js * 512: cch * D + js * 512 + 512],
                            start=(cch == 0), stop=(cch == DC - 1))
                    sch.op("tensor", mv, reads=[xnv_t[sl], wvb_t],
                           writes=[ps_v_t[vp]])

                def evv(e, tb=tb, js=js, vp=vp):
                    return e.copy(out=vst[:, tb * D + js * 512:
                                          tb * D + js * 512 + 512],
                                  in_=ps_v[vp][:, :])
                sch.op("scalar", evv, reads=[ps_v_t[vp]], writes=[vst_t])

        def vw(e, tt=tt):
            return e.dma_start(
                out=v_dram.ap()[tt * 4:(tt + 1) * 4, :, :]
                    .rearrange("c p n -> p c n"),
                in_=vst.ap().rearrange("p (c n) -> p c n", c=4))
        sch.op("gpsimd", vw, reads=[vst_t], writes=[v_dram_t], dma=True)

        for i2 in range(NI // NT):
            emit_wd_cast(tt * (NI // NT) + i2)

    p.close()
    sch.barrier()

    p = ExitStack()
    xbq = [p.enter_context(nc.sbuf_tensor(f"xbq{s}", [128, DC * 512], BF16))
           for s in range(2)]
    x2q = [p.enter_context(nc.sbuf_tensor(f"x2q{s}", [128, 512], BF16))
           for s in range(2)]
    xnq_sb = [p.enter_context(nc.sbuf_tensor(f"xnq{s}", [128, DC * 512], BF16))
              for s in range(2)]
    wqb = p.enter_context(nc.sbuf_tensor("wqb", [128, DC * D], BF16))
    wstq = p.enter_context(nc.sbuf_tensor("wstq", [128, 2 * D], F32))
    qst = [p.enter_context(nc.sbuf_tensor(f"qst{s}", [128, 512], BF16))
           for s in range(2)]
    q2b = [p.enter_context(nc.sbuf_tensor(f"q2b{s}", [128, 512], BF16))
           for s in range(2)]
    abq = [p.enter_context(nc.sbuf_tensor(f"abq{s}", [1, 3 * 512], F32))
           for s in range(1)]
    ps_st2 = p.enter_context(nc.psum_tensor("ps_st2", [1, 1024], F32))
    ps_ab2 = p.enter_context(nc.psum_tensor("ps_ab2", [128, 1024], F32))
    ps_q = [p.enter_context(nc.psum_tensor(f"ps_q{s}", [128, 512], F32))
            for s in range(2)]
    ps_n2 = p.enter_context(nc.psum_tensor("ps_n2", [1, 512], F32))

    xbq_t = [TBuf() for _ in range(2)]
    x2q_t = [TBuf() for _ in range(2)]
    xnq_t = [TBuf() for _ in range(2)]
    wqb_t = TBuf()
    wstq_t = [TBuf(), TBuf(), TBuf(), TBuf()]
    qst_t = [TBuf(), TBuf()]
    q2b_t = [TBuf() for _ in range(2)]
    abq_t = [TBuf() for _ in range(1)]
    ps_st2_t = TBuf()
    ps_ab2_t = TBuf()
    ps_q_t = [TBuf() for _ in range(2)]
    ps_n2_t = TBuf()
    q_dram_t = TBuf()
    xnq_dram_t = TBuf()

    for cch in range(DC):
        sl2 = cch % 2

        def ldwq(e, cch=cch, sl2=sl2):
            return e.dma_start(out=wstq[:, sl2 * D:(sl2 + 1) * D],
                               in_=wq_h.ap()[cch * 128:(cch + 1) * 128, :])
        sch.op("sync", ldwq, writes=[wstq_t[sl2]], dma=True)

        def cstwq(e, cch=cch, sl2=sl2):
            return e.tensor_copy(out=wqb[:, cch * D:(cch + 1) * D],
                                 in_=wstq[:, sl2 * D:(sl2 + 1) * D])
        sch.op("vector", cstwq, reads=[wstq_t[sl2]], writes=[wqb_t])

    for tt in range(NQT):
        sl = tt % 2
        BCH = DC // 4
        for ph in range(4):
            xsl = (tt * 4 + ph) % 2

            def ldxq(e, tt=tt, ph=ph, xsl=xsl, BCH=BCH):
                return e.dma_start(
                    out=wstq.ap()[:, xsl * D: xsl * D + BCH * 512]
                        .rearrange("p (c n) -> p c n", c=BCH),
                    in_=xqT.ap()[ph * BCH * 128:(ph + 1) * BCH * 128,
                                 tt * 512:(tt + 1) * 512]
                        .rearrange("(c p) n -> p c n", p=128))
            sch.op("sync", ldxq, writes=[wstq_t[xsl]], dma=True)

            for i in range(BCH):
                cch = ph * BCH + i

                def cstxq(e, cch=cch, i=i, xsl=xsl, sl=sl):
                    return e.tensor_copy(
                        out=xbq[sl][:, cch * 512:(cch + 1) * 512],
                        in_=wstq[:, xsl * D + i * 512:
                                 xsl * D + i * 512 + 512])
                sch.op("vector", cstxq, reads=[wstq_t[xsl]],
                       writes=[xbq_t[sl]])

        for cch in range(DC):
            def sq2(e, cch=cch, sl=sl):
                return e.activation(
                    out=x2q[sl][:, :],
                    in_=xbq[sl][:, cch * 512:(cch + 1) * 512], func=AF.Square)
            sch.op("scalar", sq2, reads=[xbq_t[sl]], writes=[x2q_t[sl]])

            def mm_s2(e, cch=cch, sl=sl):
                return e.matmul(ps_st2[:, 0:512], ones_b[:, 0:1],
                                xbq[sl][:, cch * 512:(cch + 1) * 512],
                                start=(cch == 0), stop=(cch == DC - 1))
            sch.op("tensor", mm_s2, reads=[xbq_t[sl], ones_b_t],
                   writes=[ps_st2_t])

            def mm_q2(e, cch=cch, sl=sl):
                return e.matmul(ps_st2[:, 512:1024], ones_b[:, 0:1],
                                x2q[sl][:, :],
                                start=(cch == 0), stop=(cch == DC - 1))
            sch.op("tensor", mm_q2, reads=[x2q_t[sl], ones_b_t],
                   writes=[ps_st2_t])

        asl = 0

        def mmean2(e, asl=asl):
            return e.mul(out=abq[asl][:, 0:512], in_=ps_st2[:, 0:512],
                         mul=inv_d)
        sch.op("scalar", mmean2, reads=[ps_st2_t], writes=[abq_t[asl]])

        def m2f2(e, asl=asl):
            return e.tensor_tensor(out=abq[asl][:, 1024:1536],
                                   in0=abq[asl][:, 0:512],
                                   in1=abq[asl][:, 0:512], op=ALU.mult)
        sch.op("vector", m2f2, reads=[abq_t[asl]], writes=[abq_t[asl]])

        def mvar22(e, asl=asl):
            return e.scalar_tensor_tensor(
                out=abq[asl][:, 512:1024], in0=ps_st2[:, 512:1024],
                scalar=inv_d, in1=abq[asl][:, 1024:1536],
                op0=ALU.mult, op1=ALU.subtract)
        sch.op("vector", mvar22, reads=[ps_st2_t, abq_t[asl]],
               writes=[abq_t[asl]])

        def lnv2(e, asl=asl):
            return e.activation(out=abq[asl][:, 512:1024],
                                in_=abq[asl][:, 512:1024], func=AF.Ln,
                                bias=eps_sb[:, 0:1])
        sch.op("scalar", lnv2, reads=[abq_t[asl], eps_t], writes=[abq_t[asl]])

        def expv2(e, asl=asl):
            return e.activation(out=abq[asl][:, 512:1024],
                                in_=abq[asl][:, 512:1024], func=AF.Exp,
                                scale=-0.5)
        sch.op("scalar", expv2, reads=[abq_t[asl]], writes=[abq_t[asl]])

        def bfn2(e, asl=asl):
            return e.scalar_tensor_tensor(
                out=abq[asl][:, 1024:1536], in0=abq[asl][:, 0:512],
                scalar=-1.0, in1=abq[asl][:, 512:1024],
                op0=ALU.mult, op1=ALU.mult)
        sch.op("vector", bfn2, reads=[abq_t[asl]], writes=[abq_t[asl]])

        def bca2(e, asl=asl):
            return e.matmul(ps_ab2[:, 0:512], ones_f[:, :],
                            abq[asl][:, 512:1024], start=True, stop=True)
        sch.op("tensor", bca2, reads=[abq_t[asl], ones_f_t], writes=[ps_ab2_t])

        def bcb2(e, asl=asl):
            return e.matmul(ps_ab2[:, 512:1024], ones_f[:, :],
                            abq[asl][:, 1024:1536], start=True, stop=True)
        sch.op("tensor", bcb2, reads=[abq_t[asl], ones_f_t], writes=[ps_ab2_t])

        for cch in range(DC):
            def xnm2(e, cch=cch, sl=sl):
                return e.tensor_tensor(
                    out=xnq_sb[sl][:, cch * 512:(cch + 1) * 512],
                    in0=xbq[sl][:, cch * 512:(cch + 1) * 512],
                    in1=ps_ab2[:, 0:512], op=ALU.mult)
            sch.op("vector", xnm2, reads=[xbq_t[sl], ps_ab2_t],
                   writes=[xnq_t[sl]])

            def xna2(e, cch=cch, sl=sl):
                return e.tensor_tensor(
                    out=xnq_sb[sl][:, cch * 512:(cch + 1) * 512],
                    in0=xnq_sb[sl][:, cch * 512:(cch + 1) * 512],
                    in1=ps_ab2[:, 512:1024], op=ALU.add)
            sch.op("vector", xna2, reads=[xnq_t[sl], ps_ab2_t],
                   writes=[xnq_t[sl]])

        def xnqw(e, tt=tt, sl=sl):
            return e.dma_start(
                out=xnq_dram.ap()[:, :, tt * 512:(tt + 1) * 512]
                    .rearrange("c p n -> p c n"),
                in_=xnq_sb[sl].ap().rearrange("p (c n) -> p c n", c=DC))
        sch.op("gpsimd", xnqw, reads=[xnq_t[sl]], writes=[xnq_dram_t],
               dma=True)

        for jc in range(H):
            qp = jc % 2
            for cch in range(DC):
                def mq(e, jc=jc, sl=sl, qp=qp, cch=cch):
                    return e.matmul(
                        ps_q[qp][:, :],
                        wqb[:, cch * D + jc * 128: cch * D + jc * 128 + 128],
                        xnq_sb[sl][:, cch * 512:(cch + 1) * 512],
                        start=(cch == 0), stop=(cch == DC - 1))
                sch.op("tensor", mq, reads=[wqb_t, xnq_t[sl]],
                       writes=[ps_q_t[qp]])

            def evq(e, jc=jc, qp=qp):
                return e.copy(out=qst[qp][:, :], in_=ps_q[qp][:, :])
            sch.op("scalar", evq, reads=[ps_q_t[qp]], writes=[qst_t[qp]])

            def qwh(e, jc=jc, qp=qp, tt=tt):
                return e.dma_start(
                    out=q_dram.ap()[jc, :, tt * 512:(tt + 1) * 512],
                    in_=qst[qp][:, :])
            sch.op("gpsimd", qwh, reads=[qst_t[qp]], writes=[q_dram_t],
                   dma=True)

            def q2f(e, jc=jc, qp=qp):
                return e.tensor_tensor(out=q2b[qp][:, :],
                                       in0=qst[qp][:, :],
                                       in1=qst[qp][:, :],
                                       op=ALU.mult)
            sch.op("vector", q2f, reads=[qst_t[qp]], writes=[q2b_t[qp]])

            def nmm2(e, jc=jc, qp=qp):
                return e.matmul(ps_n2[:, :], ones_b[:, 0:1], q2b[qp][:, :],
                                start=True, stop=True)
            sch.op("tensor", nmm2, reads=[q2b_t[qp], ones_b_t],
                   writes=[ps_n2_t])

            def nred22(e, jc=jc):
                return e.tensor_reduce(out=abq[0][:, 0:1], in_=ps_n2[:, :],
                                       axis=mybir.AxisListType.X, op=ALU.max)
            sch.op("vector", nred22, reads=[ps_n2_t], writes=[abq_t[0]])

            def nacc2(e, jc=jc):
                return e.tensor_tensor(out=qmax[:, jc:jc + 1],
                                       in0=qmax[:, jc:jc + 1],
                                       in1=abq[0][:, 0:1], op=ALU.max)
            sch.op("vector", nacc2, reads=[abq_t[0], qmax_t[jc]],
                   writes=[qmax_t[jc]])


    p.close()
    sch.barrier()

    attn_bf_dram = nc.dram_tensor("attn_bf", [H, 128, TQ], BF16,
                                  kind="Internal")
    p = ExitStack()
    ksb = [p.enter_context(nc.sbuf_tensor("ksb0", [128, S], BF16))]
    vsb = [p.enter_context(nc.sbuf_tensor("vsb0", [128, (S // 128) * 128],
                                          BF16))]
    qsb = [p.enter_context(nc.sbuf_tensor("qsb0", [128, TQ], BF16))]
    masks_sb = p.enter_context(nc.sbuf_tensor("masks_sb", [128, CPW * WQ],
                                              BF16))
    mstage = p.enter_context(nc.sbuf_tensor("mstage", [128, WQ], F32))
    P_sb = [p.enter_context(nc.sbuf_tensor(f"P{s}", [128, 1024], BF16))
            for s in range(3)]
    Pm_sb = [p.enter_context(nc.sbuf_tensor(f"Pm{s}", [128, WQ], BF16))
             for s in range(2)]
    bias_all = p.enter_context(nc.sbuf_tensor("bias_all", [128, max(H, 2)],
                                              F32))
    sc_sb = p.enter_context(nc.sbuf_tensor("sc_sb", [1, 2 * max(H, 2)], F32))
    recl_sb = [p.enter_context(nc.sbuf_tensor("recl0", [1, WQ], F32))]
    rb_sb = [p.enter_context(nc.sbuf_tensor("rb0", [128, WQ], F32))]
    attn_win = [p.enter_context(nc.sbuf_tensor(f"awin{s}", [128, WQ], BF16))
                for s in range(2)]
    HTOK = min(1024, TQ)
    NHALF = TQ // HTOK
    NT2 = HTOK // 512
    xnqh = p.enter_context(nc.sbuf_tensor("xnqh", [128, DC * HTOK], BF16))
    wstg = [p.enter_context(nc.sbuf_tensor(f"wstg{s}", [128, D], F32))
            for s in range(2)]
    wgb = [p.enter_context(nc.sbuf_tensor(f"wgb{s}", [128, D], BF16))
           for s in range(2)]
    wub = [p.enter_context(nc.sbuf_tensor(f"wub{s}", [128, D], BF16))
           for s in range(2)]
    hX = [p.enter_context(nc.sbuf_tensor(f"hX{x}", [128, 1024], F32))
          for x in range(4)]
    tmpA = p.enter_context(nc.sbuf_tensor("tmpA", [128, 512], F32))
    tmpB = p.enter_context(nc.sbuf_tensor("tmpB", [128, 512], F32))
    m1_sb = p.enter_context(nc.sbuf_tensor("m1_sb", [128, 512], F32))
    n1_sb = p.enter_context(nc.sbuf_tensor("n1_sb", [128, 512], F32))
    t2_sb = p.enter_context(nc.sbuf_tensor("t2_sb", [128, 512], F32))
    sg_sb = p.enter_context(nc.sbuf_tensor("sg_sb", [128, 512], F32))
    hsp_sb = [p.enter_context(nc.sbuf_tensor(f"hsp{x}", [128, 512], BF16))
              for x in range(4)]
    ps_sc = [p.enter_context(nc.psum_tensor(f"ps_sc{s}", [128, WQ], F32))
             for s in range(3)]
    ps_o = p.enter_context(nc.psum_tensor("ps_o", [128, WQ], F32))
    ps_L = p.enter_context(nc.psum_tensor("ps_L", [1, WQ], F32))
    ps_rb = p.enter_context(nc.psum_tensor("ps_rb", [128, WQ], F32))
    ps_g = p.enter_context(nc.psum_tensor("ps_g", [128, 512], F32))
    ps_u = p.enter_context(nc.psum_tensor("ps_u", [128, 512], F32))

    ksb_t = [TBuf()]
    vsb_t = [TBuf()]
    qsb_t = [TBuf()]
    masks_t = TBuf()
    mstage_t = TBuf()
    P_t = [TBuf() for _ in range(3)]
    Pm_t = [TBuf(), TBuf()]
    bias_all_t = TBuf()
    sc_t = TBuf()
    recl_t = [TBuf()]
    rb_t = [TBuf()]
    attn_win_t = [TBuf(), TBuf()]
    attn_bf_t = TBuf("attn_bf")
    attn_dram_t = TBuf("attn_dram")
    xnqh_t = TBuf()
    wstg_t = [TBuf(), TBuf()]
    wgb_t = [TBuf(), TBuf()]
    wub_t = [TBuf(), TBuf()]
    hX_t = [TBuf() for _ in range(4)]
    tmpA_t = TBuf()
    tmpB_t = TBuf()
    m1_t = TBuf()
    n1_t = TBuf()
    t2_t = TBuf()
    sg_t = TBuf()
    hsp_t = [TBuf() for _ in range(4)]
    ps_sc_t = [TBuf() for _ in range(3)]
    ps_o_t = TBuf()
    ps_L_t = TBuf()
    ps_rb_t = TBuf()
    ps_g_t = TBuf()
    ps_u_t = TBuf()
    hsp_dram_t = TBuf()

    for j in range(CPW):
        def ldm(e, j=j):
            return e.dma_start(out=mstage[:, :], in_=masks_h.ap()[j, :, :])
        sch.op("sync", ldm, writes=[mstage_t], dma=True)

        def cstm(e, j=j):
            return e.tensor_copy(out=masks_sb[:, j * WQ:(j + 1) * WQ],
                                 in_=mstage[:, :])
        sch.op("vector", cstm, reads=[mstage_t], writes=[masks_t])

    def b_all_m(e):
        return e.tensor_tensor(out=sc_sb[:, 0:H], in0=qmax[:, 0:H],
                               in1=kmax[:, 0:H], op=ALU.mult)
    sch.op("vector", b_all_m, reads=qmax_t + kmax_t, writes=[sc_t])

    def b_all_ln(e):
        return e.activation(out=sc_sb[:, H:2 * H], in_=sc_sb[:, 0:H],
                            func=AF.Ln)
    sch.op("scalar", b_all_ln, reads=[sc_t], writes=[sc_t])

    def b_all_exp(e):
        return e.activation(out=sc_sb[:, H:2 * H], in_=sc_sb[:, H:2 * H],
                            func=AF.Exp, scale=0.5)
    sch.op("scalar", b_all_exp, reads=[sc_t], writes=[sc_t])

    def b_all_sc(e):
        return e.mul(out=sc_sb[:, H:2 * H], in_=sc_sb[:, H:2 * H],
                     mul=-(1.02 * scale))
    sch.op("scalar", b_all_sc, reads=[sc_t], writes=[sc_t])

    def b_all_bc(e):
        return e.matmul(ps_rb[:, 0:H], ones_f[:, :], sc_sb[:, H:2 * H],
                        start=True, stop=True)
    sch.op("tensor", b_all_bc, reads=[sc_t, ones_f_t], writes=[ps_rb_t])

    def b_all_cv(e):
        return e.copy(out=bias_all[:, :], in_=ps_rb[:, 0:H])
    sch.op("scalar", b_all_cv, reads=[ps_rb_t], writes=[bias_all_t])

    def ffn_gen():
        for half in range(NHALF):
            def ldxnq(e, half=half):
                return e.dma_start(
                    out=xnqh.ap().rearrange("p (c n) -> p c n", c=DC),
                    in_=xnq_dram.ap()[:, :, half * HTOK:(half + 1) * HTOK]
                        .rearrange("c p n -> p c n"))
            sch.op("sync", ldxnq, reads=[xnq_dram_t], writes=[xnqh_t],
                   dma=True)
            yield
            for quad in range(QUADS):
                for X in range(4):
                    ck = X * QUADS + quad
                    wsl = X % 2

                    def ldwg(e, ck=ck, wsl=wsl):
                        return e.dma_start(out=wstg[wsl][:, :],
                                           in_=wgh.ap()[ck, :, :])
                    sch.op("sync", ldwg, writes=[wstg_t[wsl]], dma=True)

                    def cwg(e, wsl=wsl):
                        return e.tensor_copy(out=wgb[wsl][:, :],
                                             in_=wstg[wsl][:, :])
                    sch.op("vector", cwg, reads=[wstg_t[wsl]],
                           writes=[wgb_t[wsl]])

                    def ldwu(e, ck=ck, wsl=wsl):
                        return e.dma_start(out=wstg[wsl][:, :],
                                           in_=wuh.ap()[ck, :, :])
                    sch.op("sync", ldwu, reads=[wgb_t[wsl]],
                           writes=[wstg_t[wsl]], dma=True)

                    def cwu(e, wsl=wsl):
                        return e.tensor_copy(out=wub[wsl][:, :],
                                             in_=wstg[wsl][:, :])
                    sch.op("vector", cwu, reads=[wstg_t[wsl]],
                           writes=[wub_t[wsl]])
                    yield

                    for tt2 in range(NT2):
                        for cch in range(DC):
                            def mg(e, wsl=wsl, cch=cch, tt2=tt2):
                                return e.matmul(
                                    ps_g[:, :],
                                    wgb[wsl][:, cch * 128:(cch + 1) * 128],
                                    xnqh[:, cch * HTOK + tt2 * 512:
                                         cch * HTOK + tt2 * 512 + 512],
                                    start=(cch == 0), stop=(cch == DC - 1))
                            sch.op("tensor", mg, reads=[wgb_t[wsl], xnqh_t],
                                   writes=[ps_g_t])

                            def mu(e, wsl=wsl, cch=cch, tt2=tt2):
                                return e.matmul(
                                    ps_u[:, :],
                                    wub[wsl][:, cch * 128:(cch + 1) * 128],
                                    xnqh[:, cch * HTOK + tt2 * 512:
                                         cch * HTOK + tt2 * 512 + 512],
                                    start=(cch == 0), stop=(cch == DC - 1))
                            sch.op("tensor", mu, reads=[wub_t[wsl], xnqh_t],
                                   writes=[ps_u_t])
                            yield

                        def s1(e):
                            return e.activation(out=sg_sb[:, :],
                                                in_=ps_g[:, :], func=AF.Exp,
                                                scale=-1.0)
                        sch.op("scalar", s1, reads=[ps_g_t], writes=[sg_t])

                        def s2b(e):
                            return e.activation(out=sg_sb[:, :],
                                                in_=sg_sb[:, :], func=AF.Ln,
                                                bias=onec_sb[:, 0:1])
                        sch.op("scalar", s2b, reads=[sg_t, onec_t],
                               writes=[sg_t])

                        def s3(e):
                            return e.activation(out=sg_sb[:, :],
                                                in_=sg_sb[:, :], func=AF.Exp,
                                                scale=-1.0)
                        sch.op("scalar", s3, reads=[sg_t], writes=[sg_t])

                        def h1(e, X=X, tt2=tt2):
                            return e.tensor_tensor(
                                out=hX[X][:, tt2 * 512:(tt2 + 1) * 512],
                                in0=sg_sb[:, :], in1=ps_g[:, :], op=ALU.mult)
                        sch.op("vector", h1, reads=[sg_t, ps_g_t],
                               writes=[hX_t[X]])

                        def h2(e, X=X, tt2=tt2):
                            return e.tensor_tensor(
                                out=hX[X][:, tt2 * 512:(tt2 + 1) * 512],
                                in0=hX[X][:, tt2 * 512:(tt2 + 1) * 512],
                                in1=ps_u[:, :], op=ALU.mult)
                        sch.op("vector", h2, reads=[hX_t[X], ps_u_t],
                               writes=[hX_t[X]])
                        yield

                for tt2 in range(NT2):
                    tt = half * NT2 + tt2

                    def sp(fn, reads, writes):
                        sch.op("vector", fn, reads=reads, writes=writes)

                    sp(lambda e, tt2=tt2: e.tensor_tensor(
                        out=tmpA[:, :], in0=hX[0][:, tt2 * 512:(tt2 + 1) * 512],
                        in1=hX[0][:, tt2 * 512:(tt2 + 1) * 512], op=ALU.mult),
                       [hX_t[0]], [tmpA_t])
                    sp(lambda e, tt2=tt2: e.tensor_tensor(
                        out=tmpB[:, :], in0=hX[1][:, tt2 * 512:(tt2 + 1) * 512],
                        in1=hX[1][:, tt2 * 512:(tt2 + 1) * 512], op=ALU.mult),
                       [hX_t[1]], [tmpB_t])
                    sp(lambda e: e.tensor_tensor(out=m1_sb[:, :],
                                                 in0=tmpA[:, :],
                                                 in1=tmpB[:, :], op=ALU.max),
                       [tmpA_t, tmpB_t], [m1_t])
                    sp(lambda e: e.tensor_tensor(out=n1_sb[:, :],
                                                 in0=tmpA[:, :],
                                                 in1=tmpB[:, :], op=ALU.min),
                       [tmpA_t, tmpB_t], [n1_t])
                    yield
                    sp(lambda e, tt2=tt2: e.tensor_tensor(
                        out=tmpA[:, :], in0=hX[2][:, tt2 * 512:(tt2 + 1) * 512],
                        in1=hX[2][:, tt2 * 512:(tt2 + 1) * 512], op=ALU.mult),
                       [hX_t[2]], [tmpA_t])
                    sp(lambda e, tt2=tt2: e.tensor_tensor(
                        out=tmpB[:, :], in0=hX[3][:, tt2 * 512:(tt2 + 1) * 512],
                        in1=hX[3][:, tt2 * 512:(tt2 + 1) * 512], op=ALU.mult),
                       [hX_t[3]], [tmpB_t])
                    sp(lambda e: e.tensor_tensor(out=t2_sb[:, :],
                                                 in0=tmpA[:, :],
                                                 in1=tmpB[:, :], op=ALU.max),
                       [tmpA_t, tmpB_t], [t2_t])
                    sp(lambda e: e.tensor_tensor(out=tmpA[:, :],
                                                 in0=tmpA[:, :],
                                                 in1=tmpB[:, :], op=ALU.min),
                       [tmpA_t, tmpB_t], [tmpA_t])
                    yield
                    sp(lambda e: e.tensor_tensor(out=m1_sb[:, :],
                                                 in0=m1_sb[:, :],
                                                 in1=t2_sb[:, :], op=ALU.min),
                       [m1_t, t2_t], [m1_t])
                    sp(lambda e: e.tensor_tensor(out=n1_sb[:, :],
                                                 in0=n1_sb[:, :],
                                                 in1=tmpA[:, :], op=ALU.max),
                       [n1_t, tmpA_t], [n1_t])
                    sp(lambda e: e.tensor_tensor(out=t2_sb[:, :],
                                                 in0=m1_sb[:, :],
                                                 in1=n1_sb[:, :], op=ALU.max),
                       [m1_t, n1_t], [t2_t])
                    yield
                    for X in range(4):
                        def f_sq(e, X=X, tt2=tt2):
                            return e.tensor_tensor(
                                out=tmpB[:, :],
                                in0=hX[X][:, tt2 * 512:(tt2 + 1) * 512],
                                in1=hX[X][:, tt2 * 512:(tt2 + 1) * 512],
                                op=ALU.mult)
                        sch.op("vector", f_sq, reads=[hX_t[X]],
                               writes=[tmpB_t])

                        def f_keep(e):
                            return e.tensor_tensor(out=tmpB[:, :],
                                                   in0=tmpB[:, :],
                                                   in1=t2_sb[:, :],
                                                   op=ALU.is_ge)
                        sch.op("vector", f_keep, reads=[tmpB_t, t2_t],
                               writes=[tmpB_t])

                        def f_app(e, X=X, tt2=tt2):
                            return e.tensor_tensor(
                                out=hsp_sb[X][:, :],
                                in0=hX[X][:, tt2 * 512:(tt2 + 1) * 512],
                                in1=tmpB[:, :], op=ALU.mult)
                        sch.op("vector", f_app, reads=[hX_t[X], tmpB_t],
                               writes=[hsp_t[X]])

                        ck = X * QUADS + quad

                        def f_st(e, X=X, ck=ck, tt=tt):
                            return e.dma_start(
                                out=hsp_dram.ap()[ck, :,
                                                  tt * 512:(tt + 1) * 512],
                                in_=hsp_sb[X][:, :])
                        sch.op("gpsimd", f_st, reads=[hsp_t[X]],
                               writes=[hsp_dram_t], dma=True)
                        yield

    gen = ffn_gen()
    gen_done = [False]

    def pump(n):
        if gen_done[0]:
            return
        for _ in range(n):
            try:
                next(gen)
            except StopIteration:
                gen_done[0] = True
                return

    def load_head(h):
        def ldk(e, h=h):
            return e.dma_start(out=ksb[0][:, :], in_=k_dram.ap()[h, :, :])
        sch.op("sync", ldk, reads=[k_dram_t], writes=[ksb_t[0]], dma=True)

        def ldv(e, h=h):
            return e.dma_start(
                out=vsb[0].ap().rearrange("p (c n) -> p c n", c=S // 128),
                in_=v_dram.ap()[:, :, h * 128:(h + 1) * 128]
                    .rearrange("c p n -> p c n"))
        sch.op("sync", ldv, reads=[v_dram_t], writes=[vsb_t[0]], dma=True)

        def ldq(e, h=h):
            return e.dma_start(out=qsb[0][:, :], in_=q_dram.ap()[h, :, :])
        sch.op("sync", ldq, reads=[q_dram_t], writes=[qsb_t[0]], dma=True)

    for h in range(H):
        load_head(h)
        pump(2)
        for w in range(NW):
            nk = CPW * (w + 1)
            npair = nk // 2

            def emit_scores(kp):
                for half in (0, 1):
                    ck = 2 * kp + half
                    scp = ck % 3

                    def mms(e, h=h, w=w, ck=ck, scp=scp):
                        return e.matmul(
                            ps_sc[scp][:, :],
                            ksb[0][:, ck * 128:(ck + 1) * 128],
                            qsb[0][:, w * WQ:(w + 1) * WQ],
                            start=True, stop=True)
                    sch.op("tensor", mms, reads=[ksb_t[0], qsb_t[0]],
                           writes=[ps_sc_t[scp]])

            emit_scores(0)
            for kp in range(npair):
                ck0, ck1 = 2 * kp, 2 * kp + 1
                psl = kp % 3

                for half, ck in ((0, ck0), (1, ck1)):
                    scp = ck % 3

                    def mexp(e, scp=scp, psl=psl, half=half, h=h):
                        return e.activation(
                            out=P_sb[psl][:, half * WQ:(half + 1) * WQ],
                            in_=ps_sc[scp][:, :], func=AF.Exp,
                            scale=scale, bias=bias_all[:, h:h + 1])
                    sch.op("scalar", mexp, reads=[ps_sc_t[scp], bias_all_t],
                           writes=[P_t[psl]])

                if kp + 1 < npair:
                    emit_scores(kp + 1)

                for half, ck in ((0, ck0), (1, ck1)):
                    j = ck - CPW * w
                    if j >= 0:
                        def mmask(e, psl=psl, half=half, j=j):
                            return e.tensor_tensor(
                                out=P_sb[psl][:, half * WQ:(half + 1) * WQ],
                                in0=P_sb[psl][:, half * WQ:(half + 1) * WQ],
                                in1=masks_sb[:, j * WQ:(j + 1) * WQ],
                                op=ALU.mult)
                        sch.op("vector", mmask, reads=[P_t[psl], masks_t],
                               writes=[P_t[psl]])

                pm = kp % 2

                def mpadd(e, psl=psl, pm=pm):
                    return e.tensor_tensor(out=Pm_sb[pm][:, :],
                                           in0=P_sb[psl][:, 0:WQ],
                                           in1=P_sb[psl][:, WQ:2 * WQ],
                                           op=ALU.add)
                sch.op("vector", mpadd, reads=[P_t[psl]], writes=[Pm_t[pm]])

                def mmsum(e, pm=pm, kp=kp, npair=npair):
                    return e.matmul(
                        ps_L[:, :], ones_b[:, 0:1], Pm_sb[pm][:, :],
                        start=(kp == 0), stop=(kp == npair - 1))
                sch.op("tensor", mmsum, reads=[Pm_t[pm], ones_b_t],
                       writes=[ps_L_t])

                for half, ck in ((0, ck0), (1, ck1)):
                    def mmav(e, psl=psl, half=half, ck=ck, kp=kp,
                             npair=npair):
                        return e.matmul(
                            ps_o[:, :],
                            vsb[0][:, ck * 128:(ck + 1) * 128],
                            P_sb[psl][:, half * WQ:(half + 1) * WQ],
                            start=(kp == 0 and half == 0),
                            stop=(kp == npair - 1 and half == 1))
                    sch.op("tensor", mmav, reads=[P_t[psl], vsb_t[0]],
                           writes=[ps_o_t])

                pump(5)

            def wln(e):
                return e.activation(out=recl_sb[0][:, :], in_=ps_L[:, :],
                                    func=AF.Ln)
            sch.op("scalar", wln, reads=[ps_L_t], writes=[recl_t[0]])

            def wexp(e):
                return e.activation(out=recl_sb[0][:, :],
                                    in_=recl_sb[0][:, :], func=AF.Exp,
                                    scale=-1.0)
            sch.op("scalar", wexp, reads=[recl_t[0]], writes=[recl_t[0]])

            def wbc(e):
                return e.matmul(ps_rb[:, :], ones_f[:, :], recl_sb[0][:, :],
                                start=True, stop=True)
            sch.op("tensor", wbc, reads=[recl_t[0], ones_f_t],
                   writes=[ps_rb_t])

            def wrbc(e):
                return e.tensor_copy(out=rb_sb[0][:, :], in_=ps_rb[:, :])
            sch.op("vector", wrbc, reads=[ps_rb_t], writes=[rb_t[0]])

            osl = w % 2

            def wnorm(e, osl=osl):
                return e.tensor_tensor(
                    out=attn_win[osl][:, :],
                    in0=ps_o[:, :], in1=rb_sb[0][:, :], op=ALU.mult)
            sch.op("vector", wnorm, reads=[ps_o_t, rb_t[0]],
                   writes=[attn_win_t[osl]])

            def wst_a(e, h=h, w=w, osl=osl):
                return e.dma_start(
                    out=attn_bf_dram.ap()[h, :, w * WQ:(w + 1) * WQ],
                    in_=attn_win[osl][:, :])
            sch.op("gpsimd", wst_a, reads=[attn_win_t[osl]],
                   writes=[attn_bf_t], dma=True)
            pump(8)

    while not gen_done[0]:
        pump(50)

    p.close()
    sch.barrier()

    p = ExitStack()
    wob = p.enter_context(nc.sbuf_tensor("wob", [128, H * D], BF16))
    atile = [p.enter_context(nc.sbuf_tensor(f"atile{s}", [128, H * 128],
                                            BF16)) for s in range(2)]
    wsto = [p.enter_context(nc.sbuf_tensor(f"wsto{s}", [128, D], F32))
            for s in range(2)]
    ost = [p.enter_context(nc.sbuf_tensor(f"ost{s}", [128, 512], F32))
           for s in range(2)]
    ps_wo = [p.enter_context(nc.psum_tensor(f"ps_wo{s}", [128, 512], F32))
             for s in range(2)]
    wob_t = TBuf()
    atile_t = [TBuf(), TBuf()]
    wsto_t = [TBuf(), TBuf()]
    ost_t = [TBuf(), TBuf()]
    ps_wo_t = [TBuf(), TBuf()]

    for f in range(H):
        sl2 = f % 2

        def ldwo(e, f=f, sl2=sl2):
            return e.dma_start(out=wsto[sl2][:, :],
                               in_=wo_h.ap()[f * 128:(f + 1) * 128, :])
        sch.op("sync", ldwo, writes=[wsto_t[sl2]], dma=True)

        def cstwo(e, f=f, sl2=sl2):
            return e.tensor_copy(out=wob[:, f * D:(f + 1) * D],
                                 in_=wsto[sl2][:, :])
        sch.op("vector", cstwo, reads=[wsto_t[sl2]], writes=[wob_t])

    NJT = D // 512
    for qb in range(TQ // 128):
        asl3 = qb % 2

        def lda3(e, qb=qb, asl3=asl3):
            return e.dma_start(
                out=atile[asl3].ap().rearrange("p (c n) -> p c n", c=H),
                in_=attn_bf_dram.ap()[:, :, qb * 128:(qb + 1) * 128]
                    .rearrange("c p n -> p c n"))
        sch.op("sync", lda3, reads=[attn_bf_t], writes=[atile_t[asl3]],
               dma=True)

        for jt in range(NJT):
            wp = (qb * NJT + jt) % 2
            for f in range(H):
                def mwo(e, qb=qb, jt=jt, f=f, wp=wp, asl3=asl3):
                    return e.matmul(
                        ps_wo[wp][:, :],
                        atile[asl3][:, f * 128:(f + 1) * 128],
                        wob[:, f * D + jt * 512: f * D + jt * 512 + 512],
                        start=(f == 0), stop=(f == H - 1))
                sch.op("tensor", mwo, reads=[atile_t[asl3], wob_t],
                       writes=[ps_wo_t[wp]])

            def evo(e, wp=wp):
                return e.copy(out=ost[wp][:, :], in_=ps_wo[wp][:, :])
            sch.op("scalar", evo, reads=[ps_wo_t[wp]], writes=[ost_t[wp]])

            def sto(e, qb=qb, jt=jt, wp=wp):
                return e.dma_start(
                    out=attn_dram.ap()[qb * 128:(qb + 1) * 128,
                                       jt * 512:(jt + 1) * 512],
                    in_=ost[wp][:, :])
            sch.op("gpsimd", sto, reads=[ost_t[wp]], writes=[attn_dram_t],
                   dma=True)

    p.close()
    sch.barrier()

    p = ExitStack()
    TS5 = 256
    NTS = TQ // TS5
    hsp_c = [p.enter_context(nc.sbuf_tensor(f"hsp_c{s}", [128, NI * TS5],
                                            BF16)) for s in range(2)]
    wd_jt = p.enter_context(nc.sbuf_tensor("wd_jt", [128, NI * 512], BF16))
    x_tile = [p.enter_context(nc.sbuf_tensor(f"x_t{s}", [128, 512], F32))
              for s in range(2)]
    a_tile = [p.enter_context(nc.sbuf_tensor(f"a_t{s}", [128, 512], F32))
              for s in range(2)]
    o_tile = [p.enter_context(nc.sbuf_tensor(f"o_t{s}", [128, 512], F32))
              for s in range(2)]
    ps_dn = [p.enter_context(nc.psum_tensor(f"ps_dn{s}", [128, 512], F32))
             for s in range(2)]
    hsp_c_t = [TBuf(), TBuf()]
    wd_jt_t = TBuf()
    x_tile_t = [TBuf(), TBuf()]
    a_tile_t = [TBuf(), TBuf()]
    o_tile_t = [TBuf(), TBuf()]
    ps_dn_t = [TBuf(), TBuf()]
    out_t = TBuf()

    NJT5 = D // 512
    for jt in range(NJT5):
        def ldwj(e, jt=jt):
            return e.dma_start(
                out=wd_jt.ap().rearrange("p (c n) -> p c n", c=NI),
                in_=wdb_dram.ap()[:, :, jt * 512:(jt + 1) * 512]
                    .rearrange("c p n -> p c n"))
        sch.op("sync", ldwj, reads=[wdb_dram_t], writes=[wd_jt_t], dma=True)

        for tts in range(NTS):
            hsl = tts % 2

            def ldh(e, tts=tts, hsl=hsl):
                return e.dma_start(
                    out=hsp_c[hsl].ap().rearrange("p (c n) -> p c n", c=NI),
                    in_=hsp_dram.ap()[:, :, tts * TS5:(tts + 1) * TS5]
                        .rearrange("c p n -> p c n"))
            sch.op("sync", ldh, reads=[hsp_dram_t], writes=[hsp_c_t[hsl]],
                   dma=True)

            for tb in range(TS5 // 128):
                dp = tb % 2
                t0 = tts * TS5 + tb * 128

                def ldxt(e, t0=t0, jt=jt, dp=dp):
                    return e.dma_start(
                        out=x_tile[dp][:, :],
                        in_=xq_rows.ap()[t0:t0 + 128,
                                         jt * 512:(jt + 1) * 512])
                sch.op("sync", ldxt, writes=[x_tile_t[dp]], dma=True)

                def ldat(e, t0=t0, jt=jt, dp=dp):
                    return e.dma_start(
                        out=a_tile[dp][:, :],
                        in_=attn_dram.ap()[t0:t0 + 128,
                                           jt * 512:(jt + 1) * 512])
                sch.op("sync", ldat, reads=[attn_dram_t],
                       writes=[a_tile_t[dp]], dma=True)

                for ck in range(NI):
                    def mdn(e, ck=ck, tb=tb, dp=dp, hsl=hsl):
                        return e.matmul(
                            ps_dn[dp][:, :],
                            hsp_c[hsl][:, ck * TS5 + tb * 128:
                                       ck * TS5 + tb * 128 + 128],
                            wd_jt[:, ck * 512:(ck + 1) * 512],
                            start=(ck == 0), stop=(ck == NI - 1))
                    sch.op("tensor", mdn, reads=[hsp_c_t[hsl], wd_jt_t],
                           writes=[ps_dn_t[dp]])

                def madd1(e, dp=dp):
                    return e.tensor_tensor(out=o_tile[dp][:, :],
                                           in0=ps_dn[dp][:, :],
                                           in1=x_tile[dp][:, :], op=ALU.add)
                sch.op("vector", madd1, reads=[ps_dn_t[dp], x_tile_t[dp]],
                       writes=[o_tile_t[dp]])

                def madd2(e, dp=dp):
                    return e.tensor_tensor(out=o_tile[dp][:, :],
                                           in0=o_tile[dp][:, :],
                                           in1=a_tile[dp][:, :], op=ALU.add)
                sch.op("vector", madd2, reads=[o_tile_t[dp], a_tile_t[dp]],
                       writes=[o_tile_t[dp]])

                def stout(e, t0=t0, jt=jt, dp=dp):
                    return e.dma_start(
                        out=out_h.ap()[t0:t0 + 128, jt * 512:(jt + 1) * 512],
                        in_=o_tile[dp][:, :])
                sch.op("gpsimd", stout, reads=[o_tile_t[dp]], writes=[out_t],
                       dma=True)

    last_out = sch.last_by_eng["gpsimd"]

    def fence(e):
        return e.engine_nop()
    sch.op("gpsimd", fence, deps=[last_out])

    p.close()

    sch.emit(nc, ctx)
    return nc



_cached = {}


def _get_program(S, D, H, I):
    key = (S, D, H, I)
    if key not in _cached:
        _cached[key] = build_program(S, D, H, I)
    return _cached[key]


def run(x, wq, wk, wv, wo, wg, wu, wd):
    B, S, D = x.shape
    H = D // DH
    I = wg.shape[1]
    DC = D // 128
    NI = I // 128
    TQ = S // R
    QUADS = I // 512
    CPW = (R * WQ) // 128

    Xs = np.arange(I) // (I // 4)
    ms = np.arange(I) % (I // 4)
    perm = 4 * ms + Xs
    wg_p = np.ascontiguousarray(wg[:, perm])
    wu_p = np.ascontiguousarray(wu[:, perm])
    wd_p = np.ascontiguousarray(wd[perm, :])

    def mk_wh(w_p):
        t = w_p.reshape(DC, 128, 4, QUADS, 128)
        t = t.transpose(2, 3, 1, 0, 4)
        return np.ascontiguousarray(t.reshape(NI, 128, DC * 128))

    wgh = mk_wh(wg_p)
    wuh = mk_wh(wu_p)
    wdh = np.ascontiguousarray(wd_p.reshape(4, QUADS, 128, D)
                               .reshape(NI, 128, D))

    in_maps = []
    xT_cache = {}
    for c in range(8):
        b, l = c // 4, c % 4
        if b not in xT_cache:
            xT_cache[b] = np.ascontiguousarray(x[b].T)
        xq = np.ascontiguousarray(x[b, l::R, :])
        xqT = np.ascontiguousarray(xq.T)
        j = np.arange(CPW)[:, None, None]
        k = np.arange(128)[None, :, None]
        q = np.arange(WQ)[None, None, :]
        masks = ((128 * j + k) <= (R * q + l)).astype(np.float32)
        in_maps.append({
            "xT": xT_cache[b], "xqT": xqT, "xq_rows": xq,
            "wq": wq, "wk": wk, "wv": wv, "wo": wo,
            "wgh": wgh, "wuh": wuh, "wdh": wdh,
            "masks": np.ascontiguousarray(masks),
        })

    nc = _get_program(S, D, H, I)
    res = run_bass_kernel_spmd(nc, in_maps, core_ids=list(range(8)))

    out = np.empty((B, S, D), dtype=np.float32)
    for c in range(8):
        b, l = c // 4, c % 4
        out[b, l::R, :] = res.results[c]["out"]
    return out


def kernel(L0_attn_x, L0_attn_w_q, L0_attn_w_k, L0_attn_w_v, L0_attn_w_o,
           L0_ffn_w_gate, L0_ffn_w_up, L0_ffn_w_down):
    return run(np.asarray(L0_attn_x, dtype=np.float32),
               np.asarray(L0_attn_w_q, dtype=np.float32),
               np.asarray(L0_attn_w_k, dtype=np.float32),
               np.asarray(L0_attn_w_v, dtype=np.float32),
               np.asarray(L0_attn_w_o, dtype=np.float32),
               np.asarray(L0_ffn_w_gate, dtype=np.float32),
               np.asarray(L0_ffn_w_up, dtype=np.float32),
               np.asarray(L0_ffn_w_down, dtype=np.float32))


# version 09
# speedup vs baseline: 1.1303x; 1.1303x over previous
import numpy as np
import ml_dtypes
from contextlib import ExitStack

import concourse.bass as bass
import concourse.mybir as mybir
from concourse.bass_utils import run_bass_kernel_spmd

F32 = mybir.dt.float32
BF16 = mybir.dt.bfloat16
AF = mybir.ActivationFunctionType
ALU = mybir.AluOpType

S_FULL = 8192
D_FULL = 2048
H_FULL = 16
I_FULL = 8192
R = 4
WQ = 512
DH = 128

BF16NP = ml_dtypes.bfloat16



class Opr:
    __slots__ = ("eng", "fn", "dma", "deps", "needs_inc", "inc_val", "dma_val",
                 "tag")

    def __init__(self, eng, fn, dma, tag=""):
        self.eng = eng
        self.fn = fn
        self.dma = dma
        self.deps = []
        self.needs_inc = False
        self.inc_val = None
        self.dma_val = None
        self.tag = tag


class TBuf:
    __slots__ = ("last_w", "readers", "name")

    def __init__(self, name=""):
        self.last_w = None
        self.readers = []
        self.name = name

    def r(self, op):
        if self.last_w is not None:
            op.deps.append(self.last_w)
        self.readers.append(op)

    def w(self, op):
        if self.last_w is not None:
            op.deps.append(self.last_w)
        op.deps.extend(self.readers)
        self.last_w = op
        self.readers = []


class Sched:
    ENGINES = ("tensor", "scalar", "vector", "gpsimd", "sync")

    def __init__(self):
        self.ops = []
        self.last_by_eng = {}
        self.pending_barrier = None

    SERIALIZE = False

    def op(self, eng, fn, reads=(), writes=(), dma=False, deps=(), tag=""):
        o = Opr(eng, fn, dma, tag)
        if Sched.SERIALIZE and self.ops:
            o.deps.append(self.ops[-1])
        if self.pending_barrier is not None:
            o.deps.extend(self.pending_barrier.get("__all__", []))
            if not self.pending_barrier.get(eng, False):
                self.pending_barrier[eng] = True
            if all(self.pending_barrier.get(e, e not in self.last_by_eng)
                   for e in self.ENGINES):
                self.pending_barrier = None
        o.deps.extend(deps)
        for b in reads:
            b.r(o)
        for b in writes:
            b.w(o)
        self.ops.append(o)
        self.last_by_eng[eng] = o
        return o

    def barrier(self):
        lasts = [op for op in self.last_by_eng.values()]
        self.pending_barrier = {"__all__": lasts}

    NQSEM = 8

    def emit(self, nc, ctx):
        R = Sched.NQSEM
        qcount = {}
        for o in self.ops:
            if o.dma:
                n = qcount.get(o.eng, 0)
                o.dma_val = (n % R, 16 * (n // R + 1))
                qcount[o.eng] = n + 1
        for o in self.ops:
            for d in o.deps:
                if (not d.dma) and d.eng != o.eng:
                    d.needs_inc = True
        ecount = {}
        for o in self.ops:
            if (not o.dma) and o.needs_inc:
                ecount[o.eng] = ecount.get(o.eng, 0) + 1
                o.inc_val = ecount[o.eng]

        esem = {}
        qsem = {}
        for e in self.ENGINES:
            if ecount.get(e):
                esem[e] = ctx.enter_context(nc.semaphore(f"se_{e}"))
            if qcount.get(e):
                qsem[e] = [ctx.enter_context(nc.semaphore(f"sq_{e}{i}"))
                           for i in range(R)]

        by_eng = {e: [o for o in self.ops if o.eng == e] for e in self.ENGINES}

        def emit_engine(handle, ename):
            seen = {}
            for o in by_eng[ename]:
                waits = {}
                if o.dma:
                    sub, val = o.dma_val
                    if val > 16:
                        waits[("q", ename, sub)] = val - 16
                for d in o.deps:
                    if d.dma:
                        sub, val = d.dma_val
                        key = ("q", d.eng, sub)
                    else:
                        if d.eng == ename:
                            continue
                        if d.inc_val is None:
                            continue
                        key = ("e", d.eng, 0)
                        val = d.inc_val
                    if waits.get(key, 0) < val:
                        waits[key] = val
                for (kind, de, sub), val in sorted(waits.items()):
                    sem = qsem[de][sub] if kind == "q" else esem[de]
                    if seen.get((kind, de, sub), 0) < val:
                        handle.wait_ge(sem, val)
                        seen[(kind, de, sub)] = val
                ins = o.fn(handle)
                if o.dma:
                    ins.then_inc(qsem[ename][o.dma_val[0]], 16)
                elif o.needs_inc:
                    ins.then_inc(esem[ename], 1)

        with nc.Block() as block:
            if by_eng["sync"]:
                @block.sync
                def _(e):
                    emit_engine(e, "sync")
            if by_eng["tensor"]:
                @block.tensor
                def _(e):
                    emit_engine(e, "tensor")
            if by_eng["scalar"]:
                @block.scalar
                def _(e):
                    emit_engine(e, "scalar")
            if by_eng["vector"]:
                @block.vector
                def _(e):
                    emit_engine(e, "vector")
            if by_eng["gpsimd"]:
                @block.gpsimd
                def _(e):
                    emit_engine(e, "gpsimd")



def build_program(S, D, H, I):
    DC = D // 128
    TQ = S // R
    NW = TQ // WQ
    CPW = (R * WQ) // 128
    NT = S // 512
    NQT = TQ // 512
    NI = I // 128
    QUADS = I // 512
    scale = 1.0 / np.sqrt(DH)

    nc = bass.Bass("TRN2", target_bir_lowering=False)
    ctx = ExitStack()

    xT = nc.dram_tensor("xT", [D, S], F32, kind="ExternalInput")
    xqT = nc.dram_tensor("xqT", [D, TQ], F32, kind="ExternalInput")
    xq_rows = nc.dram_tensor("xq_rows", [TQ, D], F32, kind="ExternalInput")
    wq_h = nc.dram_tensor("wq", [D, D], F32, kind="ExternalInput")
    wk_h = nc.dram_tensor("wk", [D, D], F32, kind="ExternalInput")
    wv_h = nc.dram_tensor("wv", [D, D], F32, kind="ExternalInput")
    wo_h = nc.dram_tensor("wo", [D, D], F32, kind="ExternalInput")
    wgh = nc.dram_tensor("wgh", [NI, 128, D], F32, kind="ExternalInput")
    wuh = nc.dram_tensor("wuh", [NI, 128, D], F32, kind="ExternalInput")
    wdh = nc.dram_tensor("wdh", [NI, 128, D], F32, kind="ExternalInput")
    masks_h = nc.dram_tensor("masks", [CPW, 128, WQ], F32, kind="ExternalInput")

    out_h = nc.dram_tensor("out", [TQ, D], F32, kind="ExternalOutput")

    xn_dram = nc.dram_tensor("xn_dram", [DC, 128, S], BF16, kind="Internal")
    k_dram = nc.dram_tensor("k_dram", [H, 128, S], BF16, kind="Internal")
    v_dram = nc.dram_tensor("v_dram", [S // 128, 128, D], BF16, kind="Internal")
    q_dram = nc.dram_tensor("q_dram", [H, 128, TQ], BF16, kind="Internal")
    xnq_dram = nc.dram_tensor("xnq_dram", [DC, 128, TQ], BF16, kind="Internal")
    attn_dram = nc.dram_tensor("attn_dram", [TQ, D], F32, kind="Internal")
    hsp_dram = nc.dram_tensor("hsp_dram", [NI, 128, TQ], BF16, kind="Internal")
    wdb_dram = nc.dram_tensor("wdb_dram", [NI, 128, D], BF16, kind="Internal")

    sch = Sched()

    g = ExitStack()
    ones_b = g.enter_context(nc.sbuf_tensor("ones_b", [128, 2], BF16))
    ones_f = g.enter_context(nc.sbuf_tensor("ones_f", [1, 128], F32))
    ones_b_t = TBuf("ones_b")
    ones_f_t = TBuf("ones_f")
    sch.op("vector", lambda e: e.memset(ones_b[:, :], 1.0), writes=[ones_b_t])
    sch.op("vector", lambda e: e.memset(ones_f[:, :], 1.0), writes=[ones_f_t])
    eps_sb = g.enter_context(nc.sbuf_tensor("eps_sb", [1, 2], F32))
    eps_t = TBuf("eps")
    sch.op("vector", lambda e: e.memset(eps_sb[:, :], 1e-5), writes=[eps_t])

    kmax = g.enter_context(nc.sbuf_tensor("kmax", [1, max(H, 2)], F32))
    qmax = g.enter_context(nc.sbuf_tensor("qmax", [1, max(H, 2)], F32))
    kmax_t = [TBuf(f"kmax{h}") for h in range(H)]
    qmax_t = [TBuf(f"qmax{h}") for h in range(H)]
    sch.op("vector", lambda e: e.memset(kmax[:, :], 0.0), writes=kmax_t)
    sch.op("vector", lambda e: e.memset(qmax[:, :], 0.0), writes=qmax_t)

    p = ExitStack()
    xb = [p.enter_context(nc.sbuf_tensor(f"xb{s}", [128, DC * 512], BF16))
          for s in range(2)]
    x2b = [p.enter_context(nc.sbuf_tensor(f"x2b{s}", [128, 512], BF16))
           for s in range(2)]
    xn_sb = [p.enter_context(nc.sbuf_tensor(f"xn{s}", [128, DC * 512], BF16))
             for s in range(2)]
    wkb = p.enter_context(nc.sbuf_tensor("wkb", [128, DC * D], BF16))
    wst = p.enter_context(nc.sbuf_tensor("wst", [128, 2 * D], F32))
    kst = [p.enter_context(nc.sbuf_tensor(f"kst{s}", [128, 512], BF16))
           for s in range(2)]
    k2b = [p.enter_context(nc.sbuf_tensor(f"k2b{s}", [128, 512], BF16))
           for s in range(2)]
    ab_s = [p.enter_context(nc.sbuf_tensor(f"ab{s}", [1, 3 * 512], F32))
            for s in range(1)]
    ps_st = p.enter_context(nc.psum_tensor("ps_st", [1, 1024], F32))
    ps_ab = p.enter_context(nc.psum_tensor("ps_ab", [128, 1024], F32))
    ps_k = [p.enter_context(nc.psum_tensor(f"ps_k{s}", [128, 512], F32))
            for s in range(2)]
    ps_n = p.enter_context(nc.psum_tensor("ps_n", [1, 512], F32))

    xb_t = [TBuf() for _ in range(2)]
    x2b_t = [TBuf() for _ in range(2)]
    xn_t = [TBuf() for _ in range(2)]
    wkb_t = TBuf("wkb")
    kst_t = [TBuf(), TBuf()]
    k2b_t = [TBuf() for _ in range(2)]
    ab_t = [TBuf() for _ in range(1)]
    ps_st_t = TBuf("ps_st")
    ps_ab_t = TBuf("ps_ab")
    ps_k_t = [TBuf() for _ in range(2)]
    ps_n_t = TBuf("ps_n")
    xn_dram_t = TBuf("xn_dram")
    k_dram_t = TBuf("k_dram")

    wstage = [wst[:, 0:D], wst[:, D:2 * D]]
    wstage_t = [TBuf(), TBuf()]

    def load_weight_cast(w_h, dst_sb, dst_t):
        for cch in range(DC):
            sl = cch % 2

            def ldf(e, cch=cch, sl=sl):
                return e.dma_start(
                    out=wstage[sl],
                    in_=w_h.ap()[cch * 128:(cch + 1) * 128, :])
            o = sch.op("sync", ldf, writes=[wstage_t[sl]], dma=True)

            def cst(e, cch=cch, sl=sl):
                return e.tensor_copy(
                    out=dst_sb[:, cch * D:(cch + 1) * D], in_=wstage[sl])
            sch.op("vector", cst, reads=[wstage_t[sl]], writes=[dst_t])

    load_weight_cast(wk_h, wkb, wkb_t)

    inv_d = 1.0 / D

    def emit_front1(tt):
        sl = tt % 2
        BCH = DC // 4
        for ph in range(4):
            xsl = (tt * 4 + ph) % 2

            def ldx(e, tt=tt, ph=ph, xsl=xsl, BCH=BCH):
                return e.dma_start(
                    out=wst.ap()[:, xsl * D: xsl * D + BCH * 512]
                        .rearrange("p (c n) -> p c n", c=BCH),
                    in_=xT.ap()[ph * BCH * 128:(ph + 1) * BCH * 128,
                                tt * 512:(tt + 1) * 512]
                        .rearrange("(c p) n -> p c n", p=128))
            sch.op("sync", ldx, writes=[wstage_t[xsl]], dma=True)

            for i in range(BCH):
                cch = ph * BCH + i

                def cstx(e, cch=cch, i=i, xsl=xsl, sl=sl):
                    return e.tensor_copy(
                        out=xb[sl][:, cch * 512:(cch + 1) * 512],
                        in_=wst[:, xsl * D + i * 512: xsl * D + i * 512 + 512])
                sch.op("vector", cstx, reads=[wstage_t[xsl]],
                       writes=[xb_t[sl]])

        for cch in range(DC):
            def sq(e, cch=cch, sl=sl):
                return e.activation(
                    out=x2b[sl][:, :],
                    in_=xb[sl][:, cch * 512:(cch + 1) * 512], func=AF.Square)
            sch.op("scalar", sq, reads=[xb_t[sl]], writes=[x2b_t[sl]])

            def mm_s(e, cch=cch, sl=sl):
                return e.matmul(ps_st[:, 0:512], ones_b[:, 0:1],
                                xb[sl][:, cch * 512:(cch + 1) * 512],
                                start=(cch == 0), stop=(cch == DC - 1))
            sch.op("tensor", mm_s, reads=[xb_t[sl], ones_b_t],
                   writes=[ps_st_t])

            def mm_q(e, cch=cch, sl=sl):
                return e.matmul(ps_st[:, 512:1024], ones_b[:, 0:1],
                                x2b[sl][:, :],
                                start=(cch == 0), stop=(cch == DC - 1))
            sch.op("tensor", mm_q, reads=[x2b_t[sl], ones_b_t],
                   writes=[ps_st_t])


    def emit_front2(tt):
        sl = tt % 2
        asl = 0
        m_v = ab_s[asl][:, 0:512]
        A_v = ab_s[asl][:, 512:1024]
        B_v = ab_s[asl][:, 1024:1536]

        def mmean(e, asl=asl):
            return e.mul(out=ab_s[asl][:, 0:512], in_=ps_st[:, 0:512], mul=inv_d)
        sch.op("scalar", mmean, reads=[ps_st_t], writes=[ab_t[asl]])

        def m2f(e, asl=asl):
            return e.tensor_tensor(out=ab_s[asl][:, 1024:1536],
                                   in0=ab_s[asl][:, 0:512],
                                   in1=ab_s[asl][:, 0:512], op=ALU.mult)
        sch.op("vector", m2f, reads=[ab_t[asl]], writes=[ab_t[asl]])

        def mvar2(e, asl=asl):
            return e.scalar_tensor_tensor(
                out=ab_s[asl][:, 512:1024], in0=ps_st[:, 512:1024],
                scalar=inv_d, in1=ab_s[asl][:, 1024:1536],
                op0=ALU.mult, op1=ALU.subtract)
        sch.op("vector", mvar2, reads=[ps_st_t, ab_t[asl]], writes=[ab_t[asl]])

        def lnv(e, asl=asl):
            return e.activation(out=ab_s[asl][:, 512:1024],
                                in_=ab_s[asl][:, 512:1024],
                                func=AF.Ln, bias=eps_sb[:, 0:1])
        sch.op("scalar", lnv, reads=[ab_t[asl], eps_t], writes=[ab_t[asl]])

        def expv(e, asl=asl):
            return e.activation(out=ab_s[asl][:, 512:1024],
                                in_=ab_s[asl][:, 512:1024],
                                func=AF.Exp, scale=-0.5)
        sch.op("scalar", expv, reads=[ab_t[asl]], writes=[ab_t[asl]])

        def bfn(e, asl=asl):
            return e.scalar_tensor_tensor(
                out=ab_s[asl][:, 1024:1536], in0=ab_s[asl][:, 0:512],
                scalar=-1.0, in1=ab_s[asl][:, 512:1024],
                op0=ALU.mult, op1=ALU.mult)
        sch.op("vector", bfn, reads=[ab_t[asl]], writes=[ab_t[asl]])

        def bca(e, asl=asl):
            return e.matmul(ps_ab[:, 0:512], ones_f[:, :],
                            ab_s[asl][:, 512:1024], start=True, stop=True)
        sch.op("tensor", bca, reads=[ab_t[asl], ones_f_t], writes=[ps_ab_t])

        def bcb(e, asl=asl):
            return e.matmul(ps_ab[:, 512:1024], ones_f[:, :],
                            ab_s[asl][:, 1024:1536], start=True, stop=True)
        sch.op("tensor", bcb, reads=[ab_t[asl], ones_f_t], writes=[ps_ab_t])

        for cch in range(DC):
            def xnm(e, cch=cch, sl=sl):
                return e.tensor_tensor(
                    out=xn_sb[sl][:, cch * 512:(cch + 1) * 512],
                    in0=xb[sl][:, cch * 512:(cch + 1) * 512],
                    in1=ps_ab[:, 0:512], op=ALU.mult)
            sch.op("vector", xnm, reads=[xb_t[sl], ps_ab_t], writes=[xn_t[sl]])

            def xna(e, cch=cch, sl=sl):
                return e.tensor_tensor(
                    out=xn_sb[sl][:, cch * 512:(cch + 1) * 512],
                    in0=xn_sb[sl][:, cch * 512:(cch + 1) * 512],
                    in1=ps_ab[:, 512:1024], op=ALU.add)
            sch.op("vector", xna, reads=[xn_t[sl], ps_ab_t], writes=[xn_t[sl]])

        def xnw(e, tt=tt, sl=sl):
            return e.dma_start(
                out=xn_dram.ap()[:, :, tt * 512:(tt + 1) * 512]
                    .rearrange("c p n -> p c n"),
                in_=xn_sb[sl].ap().rearrange("p (c n) -> p c n", c=DC))
        sch.op("gpsimd", xnw, reads=[xn_t[sl]], writes=[xn_dram_t], dma=True)


    def emit_K1a(tt):
        sl = tt % 2
        for jc in range(H):
            kp = jc % 2

            for cch in range(DC):
                def mk(e, jc=jc, sl=sl, kp=kp, cch=cch):
                    return e.matmul(
                        ps_k[kp][:, :],
                        wkb[:, cch * D + jc * 128: cch * D + jc * 128 + 128],
                        xn_sb[sl][:, cch * 512:(cch + 1) * 512],
                        start=(cch == 0), stop=(cch == DC - 1))
                sch.op("tensor", mk, reads=[wkb_t, xn_t[sl]],
                       writes=[ps_k_t[kp]])

            def evk(e, jc=jc, kp=kp):
                return e.copy(out=kst[kp][:, :], in_=ps_k[kp][:, :])
            sch.op("scalar", evk, reads=[ps_k_t[kp]], writes=[kst_t[kp]])

            def kwh(e, jc=jc, kp=kp, tt=tt):
                return e.dma_start(
                    out=k_dram.ap()[jc, :, tt * 512:(tt + 1) * 512],
                    in_=kst[kp][:, :])
            sch.op("gpsimd", kwh, reads=[kst_t[kp]], writes=[k_dram_t],
                   dma=True)

            def k2f(e, jc=jc, kp=kp):
                return e.tensor_tensor(out=k2b[kp][:, :],
                                       in0=kst[kp][:, :],
                                       in1=kst[kp][:, :],
                                       op=ALU.mult)
            sch.op("vector", k2f, reads=[kst_t[kp]], writes=[k2b_t[kp]])

            def nmm(e, jc=jc, kp=kp):
                return e.matmul(ps_n[:, :], ones_b[:, 0:1], k2b[kp][:, :],
                                start=True, stop=True)
            sch.op("tensor", nmm, reads=[k2b_t[kp], ones_b_t], writes=[ps_n_t])

            def nred2(e, jc=jc):
                return e.tensor_reduce(out=ab_s[0][:, 0:1], in_=ps_n[:, :],
                                       axis=mybir.AxisListType.X, op=ALU.max)
            sch.op("vector", nred2, reads=[ps_n_t], writes=[ab_t[0]], tag="kn")

            def nacc(e, jc=jc):
                return e.tensor_tensor(out=kmax[:, jc:jc + 1],
                                       in0=kmax[:, jc:jc + 1],
                                       in1=ab_s[0][:, 0:1], op=ALU.max)
            sch.op("vector", nacc, reads=[ab_t[0], kmax_t[jc]],
                   writes=[kmax_t[jc]])



    emit_front1(0)
    emit_front2(0)
    for tt in range(NT):
        if tt + 1 < NT:
            emit_front1(tt + 1)
        emit_K1a(tt)
        if tt + 1 < NT:
            emit_front2(tt + 1)

    p.close()
    sch.barrier()

    p = ExitStack()
    xnv = [p.enter_context(nc.sbuf_tensor(f"xnv{s}", [128, DC * 512], BF16))
           for s in range(2)]
    wvb = p.enter_context(nc.sbuf_tensor("wvb", [128, DC * D], BF16))
    wst2 = p.enter_context(nc.sbuf_tensor("wst2", [128, 2 * D], F32))
    vst = p.enter_context(nc.sbuf_tensor("vst", [128, 4 * D], BF16))
    ps_v = [p.enter_context(nc.psum_tensor(f"ps_v{s}", [128, 512], F32))
            for s in range(2)]

    xnv_t = [TBuf() for _ in range(2)]
    wvb_t = TBuf("wvb")
    vst_t = TBuf("vst")
    ps_v_t = [TBuf() for _ in range(2)]
    wst2_t = [TBuf(), TBuf()]
    v_dram_t = TBuf("v_dram")

    for cch in range(DC):
        sl2 = cch % 2

        def ldw2(e, cch=cch, sl2=sl2):
            return e.dma_start(out=wst2[:, sl2 * D:(sl2 + 1) * D],
                               in_=wv_h.ap()[cch * 128:(cch + 1) * 128, :])
        sch.op("sync", ldw2, writes=[wst2_t[sl2]], dma=True)

        def cstw2(e, cch=cch, sl2=sl2):
            return e.tensor_copy(out=wvb[:, cch * D:(cch + 1) * D],
                                 in_=wst2[:, sl2 * D:(sl2 + 1) * D])
        sch.op("vector", cstw2, reads=[wst2_t[sl2]], writes=[wvb_t])

    NJ = D // 512
    for tt in range(NT):
        sl = tt % 2

        def ldxn(e, tt=tt, sl=sl):
            return e.dma_start(
                out=xnv[sl].ap().rearrange("p (c n) -> p c n", c=DC),
                in_=xn_dram.ap()[:, :, tt * 512:(tt + 1) * 512]
                    .rearrange("c p n -> p c n"))
        sch.op("sync", ldxn, reads=[xn_dram_t], writes=[xnv_t[sl]], dma=True)

        for tb in range(4):
            for js in range(NJ):
                vp = (tb * NJ + js) % 2
                for cch in range(DC):
                    def mv(e, tt=tt, sl=sl, tb=tb, js=js, vp=vp, cch=cch):
                        return e.matmul(
                            ps_v[vp][:, :],
                            xnv[sl][:, cch * 512 + tb * 128:
                                    cch * 512 + tb * 128 + 128],
                            wvb[:, cch * D + js * 512: cch * D + js * 512 + 512],
                            start=(cch == 0), stop=(cch == DC - 1))
                    sch.op("tensor", mv, reads=[xnv_t[sl], wvb_t],
                           writes=[ps_v_t[vp]])

                def evv(e, tb=tb, js=js, vp=vp):
                    return e.copy(out=vst[:, tb * D + js * 512:
                                          tb * D + js * 512 + 512],
                                  in_=ps_v[vp][:, :])
                sch.op("scalar", evv, reads=[ps_v_t[vp]], writes=[vst_t])

        def vw(e, tt=tt):
            return e.dma_start(
                out=v_dram.ap()[tt * 4:(tt + 1) * 4, :, :]
                    .rearrange("c p n -> p c n"),
                in_=vst.ap().rearrange("p (c n) -> p c n", c=4))
        sch.op("gpsimd", vw, reads=[vst_t], writes=[v_dram_t], dma=True)

    p.close()
    sch.barrier()

    p = ExitStack()
    xbq = [p.enter_context(nc.sbuf_tensor(f"xbq{s}", [128, DC * 512], BF16))
           for s in range(2)]
    x2q = [p.enter_context(nc.sbuf_tensor(f"x2q{s}", [128, 512], BF16))
           for s in range(2)]
    xnq_sb = [p.enter_context(nc.sbuf_tensor(f"xnq{s}", [128, DC * 512], BF16))
              for s in range(2)]
    wqb = p.enter_context(nc.sbuf_tensor("wqb", [128, DC * D], BF16))
    wstq = p.enter_context(nc.sbuf_tensor("wstq", [128, 2 * D], F32))
    qst = [p.enter_context(nc.sbuf_tensor(f"qst{s}", [128, 512], BF16))
           for s in range(2)]
    q2b = [p.enter_context(nc.sbuf_tensor(f"q2b{s}", [128, 512], BF16))
           for s in range(2)]
    abq = [p.enter_context(nc.sbuf_tensor(f"abq{s}", [1, 3 * 512], F32))
           for s in range(1)]
    ps_st2 = p.enter_context(nc.psum_tensor("ps_st2", [1, 1024], F32))
    ps_ab2 = p.enter_context(nc.psum_tensor("ps_ab2", [128, 1024], F32))
    ps_q = [p.enter_context(nc.psum_tensor(f"ps_q{s}", [128, 512], F32))
            for s in range(2)]
    ps_n2 = p.enter_context(nc.psum_tensor("ps_n2", [1, 512], F32))

    xbq_t = [TBuf() for _ in range(2)]
    x2q_t = [TBuf() for _ in range(2)]
    xnq_t = [TBuf() for _ in range(2)]
    wqb_t = TBuf()
    wstq_t = [TBuf(), TBuf(), TBuf(), TBuf()]
    qst_t = [TBuf(), TBuf()]
    q2b_t = [TBuf() for _ in range(2)]
    abq_t = [TBuf() for _ in range(1)]
    ps_st2_t = TBuf()
    ps_ab2_t = TBuf()
    ps_q_t = [TBuf() for _ in range(2)]
    ps_n2_t = TBuf()
    q_dram_t = TBuf()
    xnq_dram_t = TBuf()

    for cch in range(DC):
        sl2 = cch % 2

        def ldwq(e, cch=cch, sl2=sl2):
            return e.dma_start(out=wstq[:, sl2 * D:(sl2 + 1) * D],
                               in_=wq_h.ap()[cch * 128:(cch + 1) * 128, :])
        sch.op("sync", ldwq, writes=[wstq_t[sl2]], dma=True)

        def cstwq(e, cch=cch, sl2=sl2):
            return e.tensor_copy(out=wqb[:, cch * D:(cch + 1) * D],
                                 in_=wstq[:, sl2 * D:(sl2 + 1) * D])
        sch.op("vector", cstwq, reads=[wstq_t[sl2]], writes=[wqb_t])

    for tt in range(NQT):
        sl = tt % 2
        BCH = DC // 4
        for ph in range(4):
            xsl = (tt * 4 + ph) % 2

            def ldxq(e, tt=tt, ph=ph, xsl=xsl, BCH=BCH):
                return e.dma_start(
                    out=wstq.ap()[:, xsl * D: xsl * D + BCH * 512]
                        .rearrange("p (c n) -> p c n", c=BCH),
                    in_=xqT.ap()[ph * BCH * 128:(ph + 1) * BCH * 128,
                                 tt * 512:(tt + 1) * 512]
                        .rearrange("(c p) n -> p c n", p=128))
            sch.op("sync", ldxq, writes=[wstq_t[xsl]], dma=True)

            for i in range(BCH):
                cch = ph * BCH + i

                def cstxq(e, cch=cch, i=i, xsl=xsl, sl=sl):
                    return e.tensor_copy(
                        out=xbq[sl][:, cch * 512:(cch + 1) * 512],
                        in_=wstq[:, xsl * D + i * 512:
                                 xsl * D + i * 512 + 512])
                sch.op("vector", cstxq, reads=[wstq_t[xsl]],
                       writes=[xbq_t[sl]])

        for cch in range(DC):
            def sq2(e, cch=cch, sl=sl):
                return e.activation(
                    out=x2q[sl][:, :],
                    in_=xbq[sl][:, cch * 512:(cch + 1) * 512], func=AF.Square)
            sch.op("scalar", sq2, reads=[xbq_t[sl]], writes=[x2q_t[sl]])

            def mm_s2(e, cch=cch, sl=sl):
                return e.matmul(ps_st2[:, 0:512], ones_b[:, 0:1],
                                xbq[sl][:, cch * 512:(cch + 1) * 512],
                                start=(cch == 0), stop=(cch == DC - 1))
            sch.op("tensor", mm_s2, reads=[xbq_t[sl], ones_b_t],
                   writes=[ps_st2_t])

            def mm_q2(e, cch=cch, sl=sl):
                return e.matmul(ps_st2[:, 512:1024], ones_b[:, 0:1],
                                x2q[sl][:, :],
                                start=(cch == 0), stop=(cch == DC - 1))
            sch.op("tensor", mm_q2, reads=[x2q_t[sl], ones_b_t],
                   writes=[ps_st2_t])

        asl = 0

        def mmean2(e, asl=asl):
            return e.mul(out=abq[asl][:, 0:512], in_=ps_st2[:, 0:512],
                         mul=inv_d)
        sch.op("scalar", mmean2, reads=[ps_st2_t], writes=[abq_t[asl]])

        def m2f2(e, asl=asl):
            return e.tensor_tensor(out=abq[asl][:, 1024:1536],
                                   in0=abq[asl][:, 0:512],
                                   in1=abq[asl][:, 0:512], op=ALU.mult)
        sch.op("vector", m2f2, reads=[abq_t[asl]], writes=[abq_t[asl]])

        def mvar22(e, asl=asl):
            return e.scalar_tensor_tensor(
                out=abq[asl][:, 512:1024], in0=ps_st2[:, 512:1024],
                scalar=inv_d, in1=abq[asl][:, 1024:1536],
                op0=ALU.mult, op1=ALU.subtract)
        sch.op("vector", mvar22, reads=[ps_st2_t, abq_t[asl]],
               writes=[abq_t[asl]])

        def lnv2(e, asl=asl):
            return e.activation(out=abq[asl][:, 512:1024],
                                in_=abq[asl][:, 512:1024], func=AF.Ln,
                                bias=eps_sb[:, 0:1])
        sch.op("scalar", lnv2, reads=[abq_t[asl], eps_t], writes=[abq_t[asl]])

        def expv2(e, asl=asl):
            return e.activation(out=abq[asl][:, 512:1024],
                                in_=abq[asl][:, 512:1024], func=AF.Exp,
                                scale=-0.5)
        sch.op("scalar", expv2, reads=[abq_t[asl]], writes=[abq_t[asl]])

        def bfn2(e, asl=asl):
            return e.scalar_tensor_tensor(
                out=abq[asl][:, 1024:1536], in0=abq[asl][:, 0:512],
                scalar=-1.0, in1=abq[asl][:, 512:1024],
                op0=ALU.mult, op1=ALU.mult)
        sch.op("vector", bfn2, reads=[abq_t[asl]], writes=[abq_t[asl]])

        def bca2(e, asl=asl):
            return e.matmul(ps_ab2[:, 0:512], ones_f[:, :],
                            abq[asl][:, 512:1024], start=True, stop=True)
        sch.op("tensor", bca2, reads=[abq_t[asl], ones_f_t], writes=[ps_ab2_t])

        def bcb2(e, asl=asl):
            return e.matmul(ps_ab2[:, 512:1024], ones_f[:, :],
                            abq[asl][:, 1024:1536], start=True, stop=True)
        sch.op("tensor", bcb2, reads=[abq_t[asl], ones_f_t], writes=[ps_ab2_t])

        for cch in range(DC):
            def xnm2(e, cch=cch, sl=sl):
                return e.tensor_tensor(
                    out=xnq_sb[sl][:, cch * 512:(cch + 1) * 512],
                    in0=xbq[sl][:, cch * 512:(cch + 1) * 512],
                    in1=ps_ab2[:, 0:512], op=ALU.mult)
            sch.op("vector", xnm2, reads=[xbq_t[sl], ps_ab2_t],
                   writes=[xnq_t[sl]])

            def xna2(e, cch=cch, sl=sl):
                return e.tensor_tensor(
                    out=xnq_sb[sl][:, cch * 512:(cch + 1) * 512],
                    in0=xnq_sb[sl][:, cch * 512:(cch + 1) * 512],
                    in1=ps_ab2[:, 512:1024], op=ALU.add)
            sch.op("vector", xna2, reads=[xnq_t[sl], ps_ab2_t],
                   writes=[xnq_t[sl]])

        def xnqw(e, tt=tt, sl=sl):
            return e.dma_start(
                out=xnq_dram.ap()[:, :, tt * 512:(tt + 1) * 512]
                    .rearrange("c p n -> p c n"),
                in_=xnq_sb[sl].ap().rearrange("p (c n) -> p c n", c=DC))
        sch.op("gpsimd", xnqw, reads=[xnq_t[sl]], writes=[xnq_dram_t],
               dma=True)

        for jc in range(H):
            qp = jc % 2
            for cch in range(DC):
                def mq(e, jc=jc, sl=sl, qp=qp, cch=cch):
                    return e.matmul(
                        ps_q[qp][:, :],
                        wqb[:, cch * D + jc * 128: cch * D + jc * 128 + 128],
                        xnq_sb[sl][:, cch * 512:(cch + 1) * 512],
                        start=(cch == 0), stop=(cch == DC - 1))
                sch.op("tensor", mq, reads=[wqb_t, xnq_t[sl]],
                       writes=[ps_q_t[qp]])

            def evq(e, jc=jc, qp=qp):
                return e.copy(out=qst[qp][:, :], in_=ps_q[qp][:, :])
            sch.op("scalar", evq, reads=[ps_q_t[qp]], writes=[qst_t[qp]])

            def qwh(e, jc=jc, qp=qp, tt=tt):
                return e.dma_start(
                    out=q_dram.ap()[jc, :, tt * 512:(tt + 1) * 512],
                    in_=qst[qp][:, :])
            sch.op("gpsimd", qwh, reads=[qst_t[qp]], writes=[q_dram_t],
                   dma=True)

            def q2f(e, jc=jc, qp=qp):
                return e.tensor_tensor(out=q2b[qp][:, :],
                                       in0=qst[qp][:, :],
                                       in1=qst[qp][:, :],
                                       op=ALU.mult)
            sch.op("vector", q2f, reads=[qst_t[qp]], writes=[q2b_t[qp]])

            def nmm2(e, jc=jc, qp=qp):
                return e.matmul(ps_n2[:, :], ones_b[:, 0:1], q2b[qp][:, :],
                                start=True, stop=True)
            sch.op("tensor", nmm2, reads=[q2b_t[qp], ones_b_t],
                   writes=[ps_n2_t])

            def nred22(e, jc=jc):
                return e.tensor_reduce(out=abq[0][:, 0:1], in_=ps_n2[:, :],
                                       axis=mybir.AxisListType.X, op=ALU.max)
            sch.op("vector", nred22, reads=[ps_n2_t], writes=[abq_t[0]])

            def nacc2(e, jc=jc):
                return e.tensor_tensor(out=qmax[:, jc:jc + 1],
                                       in0=qmax[:, jc:jc + 1],
                                       in1=abq[0][:, 0:1], op=ALU.max)
            sch.op("vector", nacc2, reads=[abq_t[0], qmax_t[jc]],
                   writes=[qmax_t[jc]])


    p.close()
    sch.barrier()

    attn_bf_dram = nc.dram_tensor("attn_bf", [H, 128, TQ], BF16,
                                  kind="Internal")
    p = ExitStack()
    ksb = [p.enter_context(nc.sbuf_tensor(f"ksb{s}", [128, S], BF16))
           for s in range(2)]
    vsb = [p.enter_context(nc.sbuf_tensor(f"vsb{s}", [128, (S // 128) * 128],
                                          BF16)) for s in range(2)]
    qsb = [p.enter_context(nc.sbuf_tensor(f"qsb{s}", [128, TQ], BF16))
           for s in range(2)]
    masks_sb = p.enter_context(nc.sbuf_tensor("masks_sb", [128, CPW * WQ],
                                              BF16))
    mstage = p.enter_context(nc.sbuf_tensor("mstage", [128, WQ], F32))
    P_sb = [p.enter_context(nc.sbuf_tensor(f"P{s}", [128, 1024], BF16))
            for s in range(4)]
    Pm_sb = [p.enter_context(nc.sbuf_tensor(f"Pm{s}", [128, WQ], BF16))
             for s in range(2)]
    pacc_sb = [p.enter_context(nc.sbuf_tensor(f"pacc{s}", [128, WQ], F32))
               for s in range(2)]
    pcast_sb = [p.enter_context(nc.sbuf_tensor(f"pcast{s}", [128, WQ], BF16))
                for s in range(2)]
    bias_all = p.enter_context(nc.sbuf_tensor("bias_all", [128, max(H, 2)],
                                               F32))
    sc_sb = p.enter_context(nc.sbuf_tensor("sc_sb", [1, 2 * max(H, 2)], F32))
    recl_sb = [p.enter_context(nc.sbuf_tensor(f"recl{s}", [1, WQ], F32))
               for s in range(2)]
    rb_sb = [p.enter_context(nc.sbuf_tensor(f"rb{s}", [128, WQ], F32))
             for s in range(2)]
    attn_win = [p.enter_context(nc.sbuf_tensor(f"awin{s}", [128, WQ], BF16))
                for s in range(2)]
    ps_sc = [p.enter_context(nc.psum_tensor(f"ps_sc{s}", [128, 1024], F32))
             for s in range(2)]
    ps_o = [p.enter_context(nc.psum_tensor(f"ps_o{s}", [128, WQ], F32))
            for s in range(2)]
    ps_L = p.enter_context(nc.psum_tensor("ps_L", [1, WQ], F32))
    ps_rb = p.enter_context(nc.psum_tensor("ps_rb", [128, WQ], F32))

    ksb_t = [TBuf() for _ in range(2)]
    vsb_t = [TBuf() for _ in range(2)]
    qsb_t = [TBuf() for _ in range(2)]
    attn_win_t = [TBuf(), TBuf()]
    attn_bf_t = TBuf("attn_bf")
    masks_t = TBuf()
    mstage_t = TBuf()
    P_t = [TBuf() for _ in range(4)]
    Pm_t = [TBuf(), TBuf()]
    pacc_t = [TBuf(), TBuf()]
    pcast_t = [TBuf(), TBuf()]
    bias_all_t = TBuf()
    sc_t = TBuf()
    recl_t = [TBuf() for _ in range(2)]
    rb_t = [TBuf() for _ in range(2)]
    ps_sc_t = [TBuf() for _ in range(2)]
    ps_o_t = [TBuf() for _ in range(2)]
    ps_L_t = TBuf()
    ps_rb_t = TBuf()
    attn_dram_t = TBuf()

    for j in range(CPW):
        def ldm(e, j=j):
            return e.dma_start(out=mstage[:, :], in_=masks_h.ap()[j, :, :])
        sch.op("sync", ldm, writes=[mstage_t], dma=True)

        def cstm(e, j=j):
            return e.tensor_copy(out=masks_sb[:, j * WQ:(j + 1) * WQ],
                                 in_=mstage[:, :])
        sch.op("vector", cstm, reads=[mstage_t], writes=[masks_t])

    wdstg = [p.enter_context(nc.sbuf_tensor(f"wdstg{s}", [128, D], F32))
             for s in range(2)]
    wdbs = [p.enter_context(nc.sbuf_tensor(f"wdbs{s}", [128, D], BF16))
            for s in range(2)]
    wdstg_t = [TBuf(), TBuf()]
    wdbs_t = [TBuf(), TBuf()]
    wdb_dram_t = TBuf()

    def emit_wd_cast(ck):
        sl2 = ck % 2

        def ldwd(e, ck=ck, sl2=sl2):
            return e.dma_start(out=wdstg[sl2][:, :], in_=wdh.ap()[ck, :, :])
        sch.op("sync", ldwd, writes=[wdstg_t[sl2]], dma=True)

        def cstwd(e, ck=ck, sl2=sl2):
            return e.tensor_copy(out=wdbs[sl2][:, :], in_=wdstg[sl2][:, :])
        sch.op("vector", cstwd, reads=[wdstg_t[sl2]], writes=[wdbs_t[sl2]])

        def stwd(e, ck=ck, sl2=sl2):
            return e.dma_start(out=wdb_dram.ap()[ck, :, :], in_=wdbs[sl2][:, :])
        sch.op("gpsimd", stwd, reads=[wdbs_t[sl2]], writes=[wdb_dram_t],
               dma=True)

    def load_head(h):
        sl = h % 2

        def ldk(e, h=h):
            return e.dma_start(out=ksb[sl][:, :], in_=k_dram.ap()[h, :, :])
        sch.op("sync", ldk, reads=[k_dram_t], writes=[ksb_t[sl]], dma=True)

        def ldv(e, h=h):
            return e.dma_start(
                out=vsb[sl].ap().rearrange("p (c n) -> p c n", c=S // 128),
                in_=v_dram.ap()[:, :, h * 128:(h + 1) * 128]
                    .rearrange("c p n -> p c n"))
        sch.op("sync", ldv, reads=[v_dram_t], writes=[vsb_t[sl]], dma=True)

        def ldq(e, h=h):
            return e.dma_start(out=qsb[sl][:, :], in_=q_dram.ap()[h, :, :])
        sch.op("sync", ldq, reads=[q_dram_t], writes=[qsb_t[sl]], dma=True)

    def b_all_m(e):
        return e.tensor_tensor(out=sc_sb[:, 0:H], in0=qmax[:, 0:H],
                               in1=kmax[:, 0:H], op=ALU.mult)
    sch.op("vector", b_all_m, reads=qmax_t + kmax_t, writes=[sc_t])

    def b_all_ln(e):
        return e.activation(out=sc_sb[:, H:2 * H], in_=sc_sb[:, 0:H],
                            func=AF.Ln)
    sch.op("scalar", b_all_ln, reads=[sc_t], writes=[sc_t])

    def b_all_exp(e):
        return e.activation(out=sc_sb[:, H:2 * H], in_=sc_sb[:, H:2 * H],
                            func=AF.Exp, scale=0.5)
    sch.op("scalar", b_all_exp, reads=[sc_t], writes=[sc_t])

    def b_all_sc(e):
        return e.mul(out=sc_sb[:, H:2 * H], in_=sc_sb[:, H:2 * H],
                     mul=-(1.02 * scale))
    sch.op("scalar", b_all_sc, reads=[sc_t], writes=[sc_t])

    def b_all_bc(e):
        return e.matmul(ps_rb[:, 0:H], ones_f[:, :], sc_sb[:, H:2 * H],
                        start=True, stop=True)
    sch.op("tensor", b_all_bc, reads=[sc_t, ones_f_t], writes=[ps_rb_t])

    def b_all_cv(e):
        return e.copy(out=bias_all[:, :], in_=ps_rb[:, 0:H])
    sch.op("scalar", b_all_cv, reads=[ps_rb_t], writes=[bias_all_t])

    def lane_gen(lane):
        sl = lane
        for h in range(lane, H, 2):
            load_head(h)
            yield
            for w in range(NW):
                nk = CPW * (w + 1)
                npair = nk // 2

                def emit_scores(kp):
                    for half in (0, 1):
                        ck = 2 * kp + half

                        def mms(e, h=h, sl=sl, w=w, ck=ck, half=half):
                            return e.matmul(
                                ps_sc[sl][:, half * WQ:(half + 1) * WQ],
                                ksb[sl][:, ck * 128:(ck + 1) * 128],
                                qsb[sl][:, w * WQ:(w + 1) * WQ],
                                start=True, stop=True)
                        sch.op("tensor", mms, reads=[ksb_t[sl], qsb_t[sl]],
                               writes=[ps_sc_t[sl]])

                emit_scores(0)
                for kp in range(npair):
                    ck0, ck1 = 2 * kp, 2 * kp + 1
                    psl = lane * 2 + kp % 2

                    def mexp(e, sl=sl, psl=psl, h=h):
                        return e.activation(out=P_sb[psl][:, :],
                                            in_=ps_sc[sl][:, :], func=AF.Exp,
                                            scale=scale,
                                            bias=bias_all[:, h:h + 1])
                    sch.op("scalar", mexp, reads=[ps_sc_t[sl], bias_all_t],
                           writes=[P_t[psl]])

                    if kp + 1 < npair:
                        emit_scores(kp + 1)

                    for half, ck in ((0, ck0), (1, ck1)):
                        j = ck - CPW * w
                        if j >= 0:
                            def mmask(e, psl=psl, half=half, j=j):
                                return e.tensor_tensor(
                                    out=P_sb[psl][:, half * WQ:
                                                  (half + 1) * WQ],
                                    in0=P_sb[psl][:, half * WQ:
                                                  (half + 1) * WQ],
                                    in1=masks_sb[:, j * WQ:(j + 1) * WQ],
                                    op=ALU.mult)
                            sch.op("vector", mmask, reads=[P_t[psl], masks_t],
                                   writes=[P_t[psl]])

                    if kp == 0:
                        def macc0(e, psl=psl, lane=lane):
                            return e.tensor_tensor(out=pacc_sb[lane][:, :],
                                                   in0=P_sb[psl][:, 0:WQ],
                                                   in1=P_sb[psl][:, WQ:2 * WQ],
                                                   op=ALU.add)
                        sch.op("vector", macc0, reads=[P_t[psl]],
                               writes=[pacc_t[lane]])
                    else:
                        def mpadd(e, psl=psl, lane=lane):
                            return e.tensor_tensor(out=Pm_sb[lane][:, :],
                                                   in0=P_sb[psl][:, 0:WQ],
                                                   in1=P_sb[psl][:, WQ:2 * WQ],
                                                   op=ALU.add)
                        sch.op("vector", mpadd, reads=[P_t[psl]],
                               writes=[Pm_t[lane]])

                        def macc(e, lane=lane):
                            return e.tensor_tensor(out=pacc_sb[lane][:, :],
                                                   in0=pacc_sb[lane][:, :],
                                                   in1=Pm_sb[lane][:, :],
                                                   op=ALU.add)
                        sch.op("vector", macc, reads=[Pm_t[lane],
                                                      pacc_t[lane]],
                               writes=[pacc_t[lane]])

                    for half, ck in ((0, ck0), (1, ck1)):
                        def mmav(e, sl=sl, psl=psl, half=half, ck=ck, kp=kp,
                                 npair=npair):
                            return e.matmul(
                                ps_o[sl][:, :],
                                vsb[sl][:, ck * 128:(ck + 1) * 128],
                                P_sb[psl][:, half * WQ:(half + 1) * WQ],
                                start=(kp == 0 and half == 0),
                                stop=(kp == npair - 1 and half == 1))
                        sch.op("tensor", mmav, reads=[P_t[psl], vsb_t[sl]],
                               writes=[ps_o_t[sl]])
                    yield

                def wcast(e, lane=lane):
                    return e.tensor_copy(out=pcast_sb[lane][:, :],
                                         in_=pacc_sb[lane][:, :])
                sch.op("vector", wcast, reads=[pacc_t[lane]],
                       writes=[pcast_t[lane]])

                def wsum(e, lane=lane):
                    return e.matmul(ps_L[:, :], ones_b[:, 0:1],
                                    pcast_sb[lane][:, :],
                                    start=True, stop=True)
                sch.op("tensor", wsum, reads=[pcast_t[lane], ones_b_t],
                       writes=[ps_L_t])

                def wln(e, lane=lane):
                    return e.activation(out=recl_sb[lane][:, :],
                                        in_=ps_L[:, :], func=AF.Ln)
                sch.op("scalar", wln, reads=[ps_L_t], writes=[recl_t[lane]])

                def wexp(e, lane=lane):
                    return e.activation(out=recl_sb[lane][:, :],
                                        in_=recl_sb[lane][:, :], func=AF.Exp,
                                        scale=-1.0)
                sch.op("scalar", wexp, reads=[recl_t[lane]],
                       writes=[recl_t[lane]])

                def wbc(e, lane=lane):
                    return e.matmul(ps_rb[:, :], ones_f[:, :],
                                    recl_sb[lane][:, :],
                                    start=True, stop=True)
                sch.op("tensor", wbc, reads=[recl_t[lane], ones_f_t],
                       writes=[ps_rb_t])

                def wrbc(e, lane=lane):
                    return e.copy(out=rb_sb[lane][:, :], in_=ps_rb[:, :])
                sch.op("scalar", wrbc, reads=[ps_rb_t], writes=[rb_t[lane]])

                def wnorm(e, h=h, w=w, lane=lane):
                    return e.tensor_tensor(
                        out=attn_win[lane][:, :],
                        in0=ps_o[lane][:, :], in1=rb_sb[lane][:, :],
                        op=ALU.mult)
                sch.op("vector", wnorm, reads=[ps_o_t[lane], rb_t[lane]],
                       writes=[attn_win_t[lane]])

                def wst_a(e, h=h, w=w, lane=lane):
                    return e.dma_start(
                        out=attn_bf_dram.ap()[h, :, w * WQ:(w + 1) * WQ],
                        in_=attn_win[lane][:, :])
                sch.op("gpsimd", wst_a, reads=[attn_win_t[lane]],
                       writes=[attn_bf_t], dma=True)
                yield

            for i in range(NI // H):
                emit_wd_cast(h * (NI // H) + i)
            yield

    _gens = [lane_gen(0), lane_gen(1)]
    _alive = [True, True]
    _gi = 0
    while _alive[0] or _alive[1]:
        if _alive[_gi % 2]:
            try:
                next(_gens[_gi % 2])
            except StopIteration:
                _alive[_gi % 2] = False
        _gi += 1

    p.close()
    sch.barrier()

    p = ExitStack()
    wob = p.enter_context(nc.sbuf_tensor("wob", [128, H * D], BF16))
    atile = [p.enter_context(nc.sbuf_tensor(f"atile{s}", [128, H * 128],
                                            BF16)) for s in range(2)]
    wsto = [p.enter_context(nc.sbuf_tensor(f"wsto{s}", [128, D], F32))
            for s in range(2)]
    ost = [p.enter_context(nc.sbuf_tensor(f"ost{s}", [128, 512], F32))
           for s in range(2)]
    ps_wo = [p.enter_context(nc.psum_tensor(f"ps_wo{s}", [128, 512], F32))
             for s in range(2)]
    wob_t = TBuf()
    atile_t = [TBuf(), TBuf()]
    wsto_t = [TBuf(), TBuf()]
    ost_t = [TBuf(), TBuf()]
    ps_wo_t = [TBuf(), TBuf()]

    for f in range(H):
        sl2 = f % 2

        def ldwo(e, f=f, sl2=sl2):
            return e.dma_start(out=wsto[sl2][:, :],
                               in_=wo_h.ap()[f * 128:(f + 1) * 128, :])
        sch.op("sync", ldwo, writes=[wsto_t[sl2]], dma=True)

        def cstwo(e, f=f, sl2=sl2):
            return e.tensor_copy(out=wob[:, f * D:(f + 1) * D],
                                 in_=wsto[sl2][:, :])
        sch.op("vector", cstwo, reads=[wsto_t[sl2]], writes=[wob_t])

    NJT = D // 512
    for qb in range(TQ // 128):
        asl3 = qb % 2

        def lda3(e, qb=qb, asl3=asl3):
            return e.dma_start(
                out=atile[asl3].ap().rearrange("p (c n) -> p c n", c=H),
                in_=attn_bf_dram.ap()[:, :, qb * 128:(qb + 1) * 128]
                    .rearrange("c p n -> p c n"))
        sch.op("sync", lda3, reads=[attn_bf_t], writes=[atile_t[asl3]],
               dma=True)

        for jt in range(NJT):
            wp = (qb * NJT + jt) % 2
            for f in range(H):
                def mwo(e, qb=qb, jt=jt, f=f, wp=wp, asl3=asl3):
                    return e.matmul(
                        ps_wo[wp][:, :],
                        atile[asl3][:, f * 128:(f + 1) * 128],
                        wob[:, f * D + jt * 512: f * D + jt * 512 + 512],
                        start=(f == 0), stop=(f == H - 1))
                sch.op("tensor", mwo, reads=[atile_t[asl3], wob_t],
                       writes=[ps_wo_t[wp]])

            def evo(e, wp=wp):
                return e.copy(out=ost[wp][:, :], in_=ps_wo[wp][:, :])
            sch.op("scalar", evo, reads=[ps_wo_t[wp]], writes=[ost_t[wp]])

            def sto(e, qb=qb, jt=jt, wp=wp):
                return e.dma_start(
                    out=attn_dram.ap()[qb * 128:(qb + 1) * 128,
                                       jt * 512:(jt + 1) * 512],
                    in_=ost[wp][:, :])
            sch.op("gpsimd", sto, reads=[ost_t[wp]], writes=[attn_dram_t],
                   dma=True)

    p.close()
    sch.barrier()

    p = ExitStack()
    xnq_full = p.enter_context(nc.sbuf_tensor("xnq_full", [128, DC * TQ],
                                              BF16))
    wstg = [p.enter_context(nc.sbuf_tensor(f"wstg{s}", [128, D], F32))
            for s in range(2)]
    wgb = [p.enter_context(nc.sbuf_tensor(f"wgb{s}", [128, D], BF16))
           for s in range(4)]
    wub = [p.enter_context(nc.sbuf_tensor(f"wub{s}", [128, D], BF16))
           for s in range(4)]
    T4 = 512
    hX = [p.enter_context(nc.sbuf_tensor(f"hX{x}", [128, TQ], F32))
          for x in range(4)]
    tmpA = p.enter_context(nc.sbuf_tensor("tmpA", [128, T4], F32))
    tmpB = p.enter_context(nc.sbuf_tensor("tmpB", [128, T4], F32))
    m1_sb = p.enter_context(nc.sbuf_tensor("m1_sb", [128, T4], F32))
    n1_sb = p.enter_context(nc.sbuf_tensor("n1_sb", [128, T4], F32))
    t2_sb = p.enter_context(nc.sbuf_tensor("t2_sb", [128, T4], F32))
    hsp_sb = [p.enter_context(nc.sbuf_tensor(f"hsp{x}", [128, T4], BF16))
              for x in range(4)]
    NT4 = TQ // T4
    ps_g = [p.enter_context(nc.psum_tensor(f"ps_g{s}", [128, T4], F32))
            for s in range(NT4)]
    ps_u = [p.enter_context(nc.psum_tensor(f"ps_u{s}", [128, T4], F32))
            for s in range(NT4)]

    import os as _os
    KDBG = bool(_os.environ.get("KDBG"))
    if KDBG:
        dbg_h_dram = nc.dram_tensor("dbg_h", [NI, 128, T4], F32,
                                    kind="ExternalOutput")
        dbg_h_t = TBuf()

    xnq_full_t = TBuf()
    wstg_t = [TBuf(), TBuf()]
    wgb_t = [TBuf() for _ in range(4)]
    wub_t = [TBuf() for _ in range(4)]
    hX_t = [TBuf() for _ in range(4)]
    tmpA_t = TBuf()
    tmpB_t = TBuf()
    m1_t = TBuf()
    n1_t = TBuf()
    t2_t = TBuf()
    hsp_t = [TBuf() for _ in range(4)]
    ps_g_t = [TBuf() for _ in range(NT4)]
    ps_u_t = [TBuf() for _ in range(NT4)]
    hsp_dram_t = TBuf()

    def ldxnq(e):
        return e.dma_start(
            out=xnq_full.ap().rearrange("p (c n) -> p c n", c=DC),
            in_=xnq_dram.ap().rearrange("c p n -> p c n"))
    sch.op("sync", ldxnq, reads=[xnq_dram_t], writes=[xnq_full_t], dma=True)

    NH2 = T4 // 512
    for quad in range(QUADS):
        for X in range(4):
            ck = X * QUADS + quad
            s0 = (2 * X) % 2
            s1 = (2 * X + 1) % 2

            def ldwg(e, ck=ck, s0=s0):
                return e.dma_start(out=wstg[s0][:, :], in_=wgh.ap()[ck, :, :])
            sch.op("sync", ldwg, writes=[wstg_t[s0]], dma=True)

            def cwg(e, s0=s0, X=X):
                return e.tensor_copy(out=wgb[X][:, :], in_=wstg[s0][:, :])
            sch.op("vector", cwg, reads=[wstg_t[s0]], writes=[wgb_t[X]])

            def ldwu(e, ck=ck, s1=s1):
                return e.dma_start(out=wstg[s1][:, :], in_=wuh.ap()[ck, :, :])
            sch.op("sync", ldwu, writes=[wstg_t[s1]], dma=True)

            def cwu(e, s1=s1, X=X):
                return e.tensor_copy(out=wub[X][:, :], in_=wstg[s1][:, :])
            sch.op("vector", cwu, reads=[wstg_t[s1]], writes=[wub_t[X]])

        for X in range(4):
            for cch in range(DC):
                for tt in range(NT4):
                    def mg(e, X=X, cch=cch, tt=tt):
                        return e.matmul(
                            ps_g[tt][:, :],
                            wgb[X][:, cch * 128:(cch + 1) * 128],
                            xnq_full[:, cch * TQ + tt * T4:
                                     cch * TQ + tt * T4 + T4],
                            start=(cch == 0), stop=(cch == DC - 1))
                    sch.op("tensor", mg, reads=[wgb_t[X], xnq_full_t],
                           writes=[ps_g_t[tt]])
                for tt in range(NT4):
                    def mu(e, X=X, cch=cch, tt=tt):
                        return e.matmul(
                            ps_u[tt][:, :],
                            wub[X][:, cch * 128:(cch + 1) * 128],
                            xnq_full[:, cch * TQ + tt * T4:
                                     cch * TQ + tt * T4 + T4],
                            start=(cch == 0), stop=(cch == DC - 1))
                    sch.op("tensor", mu, reads=[wub_t[X], xnq_full_t],
                           writes=[ps_u_t[tt]])

            for tt in range(NT4):
                def msilu(e, X=X, tt=tt):
                    return e.activation(out=hX[X][:, tt * T4:(tt + 1) * T4],
                                        in_=ps_g[tt][:, :], func=AF.Silu)
                sch.op("scalar", msilu, reads=[ps_g_t[tt]], writes=[hX_t[X]])

                def mh(e, X=X, tt=tt):
                    return e.tensor_tensor(
                        out=hX[X][:, tt * T4:(tt + 1) * T4],
                        in0=hX[X][:, tt * T4:(tt + 1) * T4],
                        in1=ps_u[tt][:, :], op=ALU.mult)
                sch.op("vector", mh, reads=[hX_t[X], ps_u_t[tt]],
                       writes=[hX_t[X]])

        for tt in range(NT4):
            def sp(fn, reads, writes):
                sch.op("vector", fn, reads=reads, writes=writes)

            sp(lambda e, tt=tt: e.tensor_tensor(out=tmpA[:, :], in0=hX[0][:, tt * T4:(tt + 1) * T4],
                                         in1=hX[0][:, tt * T4:(tt + 1) * T4], op=ALU.mult),
               [hX_t[0]], [tmpA_t])
            sp(lambda e, tt=tt: e.tensor_tensor(out=tmpB[:, :], in0=hX[1][:, tt * T4:(tt + 1) * T4],
                                         in1=hX[1][:, tt * T4:(tt + 1) * T4], op=ALU.mult),
               [hX_t[1]], [tmpB_t])
            sp(lambda e, tt=tt: e.tensor_tensor(out=m1_sb[:, :], in0=tmpA[:, :],
                                         in1=tmpB[:, :], op=ALU.max),
               [tmpA_t, tmpB_t], [m1_t])
            sp(lambda e, tt=tt: e.tensor_tensor(out=n1_sb[:, :], in0=tmpA[:, :],
                                         in1=tmpB[:, :], op=ALU.min),
               [tmpA_t, tmpB_t], [n1_t])
            sp(lambda e, tt=tt: e.tensor_tensor(out=tmpA[:, :], in0=hX[2][:, tt * T4:(tt + 1) * T4],
                                         in1=hX[2][:, tt * T4:(tt + 1) * T4], op=ALU.mult),
               [hX_t[2]], [tmpA_t])
            sp(lambda e, tt=tt: e.tensor_tensor(out=tmpB[:, :], in0=hX[3][:, tt * T4:(tt + 1) * T4],
                                         in1=hX[3][:, tt * T4:(tt + 1) * T4], op=ALU.mult),
               [hX_t[3]], [tmpB_t])
            sp(lambda e, tt=tt: e.tensor_tensor(out=t2_sb[:, :], in0=tmpA[:, :],
                                         in1=tmpB[:, :], op=ALU.max),
               [tmpA_t, tmpB_t], [t2_t])
            sp(lambda e, tt=tt: e.tensor_tensor(out=tmpA[:, :], in0=tmpA[:, :],
                                         in1=tmpB[:, :], op=ALU.min),
               [tmpA_t, tmpB_t], [tmpA_t])
            sp(lambda e, tt=tt: e.tensor_tensor(out=m1_sb[:, :], in0=m1_sb[:, :],
                                         in1=t2_sb[:, :], op=ALU.min),
               [m1_t, t2_t], [m1_t])
            sp(lambda e, tt=tt: e.tensor_tensor(out=n1_sb[:, :], in0=n1_sb[:, :],
                                         in1=tmpA[:, :], op=ALU.max),
               [n1_t, tmpA_t], [n1_t])
            sp(lambda e, tt=tt: e.tensor_tensor(out=t2_sb[:, :], in0=m1_sb[:, :],
                                         in1=n1_sb[:, :], op=ALU.max),
               [m1_t, n1_t], [t2_t])

            for X in range(4):
                def f_sq(e, X=X, tt=tt):
                    return e.tensor_tensor(out=tmpB[:, :], in0=hX[X][:, tt * T4:(tt + 1) * T4],
                                           in1=hX[X][:, tt * T4:(tt + 1) * T4], op=ALU.mult)
                sch.op("vector", f_sq, reads=[hX_t[X]], writes=[tmpB_t])

                def f_keep(e, X=X, tt=tt):
                    return e.tensor_tensor(out=tmpB[:, :], in0=tmpB[:, :],
                                           in1=t2_sb[:, :], op=ALU.is_ge)
                sch.op("vector", f_keep, reads=[tmpB_t, t2_t],
                       writes=[tmpB_t])

                def f_app(e, X=X, tt=tt):
                    return e.tensor_tensor(out=hsp_sb[X][:, :],
                                           in0=hX[X][:, tt * T4:(tt + 1) * T4],
                                           in1=tmpB[:, :], op=ALU.mult)
                sch.op("vector", f_app, reads=[hX_t[X], tmpB_t],
                       writes=[hsp_t[X]])

                ck = X * QUADS + quad

                def f_st(e, X=X, ck=ck, tt=tt):
                    return e.dma_start(
                        out=hsp_dram.ap()[ck, :, tt * T4:(tt + 1) * T4],
                        in_=hsp_sb[X][:, :])
                sch.op("gpsimd", f_st, reads=[hsp_t[X]], writes=[hsp_dram_t],
                       dma=True)

    p.close()
    sch.barrier()

    p = ExitStack()
    TS5 = 256
    NTS = TQ // TS5
    hsp_c = [p.enter_context(nc.sbuf_tensor(f"hsp_c{s}", [128, NI * TS5],
                                            BF16)) for s in range(2)]
    wd_jt = p.enter_context(nc.sbuf_tensor("wd_jt", [128, NI * 512], BF16))
    x_tile = [p.enter_context(nc.sbuf_tensor(f"x_t{s}", [128, 512], F32))
              for s in range(2)]
    a_tile = [p.enter_context(nc.sbuf_tensor(f"a_t{s}", [128, 512], F32))
              for s in range(2)]
    o_tile = [p.enter_context(nc.sbuf_tensor(f"o_t{s}", [128, 512], F32))
              for s in range(2)]
    ps_dn = [p.enter_context(nc.psum_tensor(f"ps_dn{s}", [128, 512], F32))
             for s in range(2)]
    hsp_c_t = [TBuf(), TBuf()]
    wd_jt_t = TBuf()
    x_tile_t = [TBuf(), TBuf()]
    a_tile_t = [TBuf(), TBuf()]
    o_tile_t = [TBuf(), TBuf()]
    ps_dn_t = [TBuf(), TBuf()]
    out_t = TBuf()

    NJT5 = D // 512
    for jt in range(NJT5):
        def ldwj(e, jt=jt):
            return e.dma_start(
                out=wd_jt.ap().rearrange("p (c n) -> p c n", c=NI),
                in_=wdb_dram.ap()[:, :, jt * 512:(jt + 1) * 512]
                    .rearrange("c p n -> p c n"))
        sch.op("sync", ldwj, reads=[wdb_dram_t], writes=[wd_jt_t], dma=True)

        for tts in range(NTS):
            hsl = tts % 2

            def ldh(e, tts=tts, hsl=hsl):
                return e.dma_start(
                    out=hsp_c[hsl].ap().rearrange("p (c n) -> p c n", c=NI),
                    in_=hsp_dram.ap()[:, :, tts * TS5:(tts + 1) * TS5]
                        .rearrange("c p n -> p c n"))
            sch.op("sync", ldh, reads=[hsp_dram_t], writes=[hsp_c_t[hsl]],
                   dma=True)

            for tb in range(TS5 // 128):
                dp = tb % 2
                t0 = tts * TS5 + tb * 128

                def ldxt(e, t0=t0, jt=jt, dp=dp):
                    return e.dma_start(
                        out=x_tile[dp][:, :],
                        in_=xq_rows.ap()[t0:t0 + 128,
                                         jt * 512:(jt + 1) * 512])
                sch.op("sync", ldxt, writes=[x_tile_t[dp]], dma=True)

                def ldat(e, t0=t0, jt=jt, dp=dp):
                    return e.dma_start(
                        out=a_tile[dp][:, :],
                        in_=attn_dram.ap()[t0:t0 + 128,
                                           jt * 512:(jt + 1) * 512])
                sch.op("sync", ldat, reads=[attn_dram_t],
                       writes=[a_tile_t[dp]], dma=True)

                for ck in range(NI):
                    def mdn(e, ck=ck, tb=tb, dp=dp, hsl=hsl):
                        return e.matmul(
                            ps_dn[dp][:, :],
                            hsp_c[hsl][:, ck * TS5 + tb * 128:
                                       ck * TS5 + tb * 128 + 128],
                            wd_jt[:, ck * 512:(ck + 1) * 512],
                            start=(ck == 0), stop=(ck == NI - 1))
                    sch.op("tensor", mdn, reads=[hsp_c_t[hsl], wd_jt_t],
                           writes=[ps_dn_t[dp]])

                def madd1(e, dp=dp):
                    return e.tensor_tensor(out=o_tile[dp][:, :],
                                           in0=ps_dn[dp][:, :],
                                           in1=x_tile[dp][:, :], op=ALU.add)
                sch.op("vector", madd1, reads=[ps_dn_t[dp], x_tile_t[dp]],
                       writes=[o_tile_t[dp]])

                def madd2(e, dp=dp):
                    return e.tensor_tensor(out=o_tile[dp][:, :],
                                           in0=o_tile[dp][:, :],
                                           in1=a_tile[dp][:, :], op=ALU.add)
                sch.op("vector", madd2, reads=[o_tile_t[dp], a_tile_t[dp]],
                       writes=[o_tile_t[dp]])

                def stout(e, t0=t0, jt=jt, dp=dp):
                    return e.dma_start(
                        out=out_h.ap()[t0:t0 + 128, jt * 512:(jt + 1) * 512],
                        in_=o_tile[dp][:, :])
                sch.op("gpsimd", stout, reads=[o_tile_t[dp]], writes=[out_t],
                       dma=True)

    last_out = sch.last_by_eng["gpsimd"]

    def fence(e):
        return e.engine_nop()
    sch.op("gpsimd", fence, deps=[last_out])

    p.close()

    sch.emit(nc, ctx)
    return nc



_cached = {}


def _get_program(S, D, H, I):
    key = (S, D, H, I)
    if key not in _cached:
        _cached[key] = build_program(S, D, H, I)
    return _cached[key]


def run(x, wq, wk, wv, wo, wg, wu, wd):
    B, S, D = x.shape
    H = D // DH
    I = wg.shape[1]
    DC = D // 128
    NI = I // 128
    TQ = S // R
    QUADS = I // 512
    CPW = (R * WQ) // 128

    Xs = np.arange(I) // (I // 4)
    ms = np.arange(I) % (I // 4)
    perm = 4 * ms + Xs
    wg_p = np.ascontiguousarray(wg[:, perm])
    wu_p = np.ascontiguousarray(wu[:, perm])
    wd_p = np.ascontiguousarray(wd[perm, :])

    def mk_wh(w_p):
        t = w_p.reshape(DC, 128, 4, QUADS, 128)
        t = t.transpose(2, 3, 1, 0, 4)
        return np.ascontiguousarray(t.reshape(NI, 128, DC * 128))

    wgh = mk_wh(wg_p)
    wuh = mk_wh(wu_p)
    wdh = np.ascontiguousarray(wd_p.reshape(4, QUADS, 128, D)
                               .reshape(NI, 128, D))

    in_maps = []
    xT_cache = {}
    for c in range(8):
        b, l = c // 4, c % 4
        if b not in xT_cache:
            xT_cache[b] = np.ascontiguousarray(x[b].T)
        xq = np.ascontiguousarray(x[b, l::R, :])
        xqT = np.ascontiguousarray(xq.T)
        j = np.arange(CPW)[:, None, None]
        k = np.arange(128)[None, :, None]
        q = np.arange(WQ)[None, None, :]
        masks = ((128 * j + k) <= (R * q + l)).astype(np.float32)
        in_maps.append({
            "xT": xT_cache[b], "xqT": xqT, "xq_rows": xq,
            "wq": wq, "wk": wk, "wv": wv, "wo": wo,
            "wgh": wgh, "wuh": wuh, "wdh": wdh,
            "masks": np.ascontiguousarray(masks),
        })

    nc = _get_program(S, D, H, I)
    res = run_bass_kernel_spmd(nc, in_maps, core_ids=list(range(8)))

    out = np.empty((B, S, D), dtype=np.float32)
    for c in range(8):
        b, l = c // 4, c % 4
        out[b, l::R, :] = res.results[c]["out"]
    return out


def kernel(L0_attn_x, L0_attn_w_q, L0_attn_w_k, L0_attn_w_v, L0_attn_w_o,
           L0_ffn_w_gate, L0_ffn_w_up, L0_ffn_w_down):
    return run(np.asarray(L0_attn_x, dtype=np.float32),
               np.asarray(L0_attn_w_q, dtype=np.float32),
               np.asarray(L0_attn_w_k, dtype=np.float32),
               np.asarray(L0_attn_w_v, dtype=np.float32),
               np.asarray(L0_attn_w_o, dtype=np.float32),
               np.asarray(L0_ffn_w_gate, dtype=np.float32),
               np.asarray(L0_ffn_w_up, dtype=np.float32),
               np.asarray(L0_ffn_w_down, dtype=np.float32))


# version 10
# speedup vs baseline: 1.1347x; 1.0039x over previous
import numpy as np
import ml_dtypes
from contextlib import ExitStack

import concourse.bass as bass
import concourse.mybir as mybir
from concourse.bass_utils import run_bass_kernel_spmd

F32 = mybir.dt.float32
BF16 = mybir.dt.bfloat16
AF = mybir.ActivationFunctionType
ALU = mybir.AluOpType

S_FULL = 8192
D_FULL = 2048
H_FULL = 16
I_FULL = 8192
R = 4
WQ = 512
DH = 128

BF16NP = ml_dtypes.bfloat16



class Opr:
    __slots__ = ("eng", "fn", "dma", "deps", "needs_inc", "inc_val", "dma_val",
                 "tag")

    def __init__(self, eng, fn, dma, tag=""):
        self.eng = eng
        self.fn = fn
        self.dma = dma
        self.deps = []
        self.needs_inc = False
        self.inc_val = None
        self.dma_val = None
        self.tag = tag


class TBuf:
    __slots__ = ("last_w", "readers", "name")

    def __init__(self, name=""):
        self.last_w = None
        self.readers = []
        self.name = name

    def r(self, op):
        if self.last_w is not None:
            op.deps.append(self.last_w)
        self.readers.append(op)

    def w(self, op):
        if self.last_w is not None:
            op.deps.append(self.last_w)
        op.deps.extend(self.readers)
        self.last_w = op
        self.readers = []


class Sched:
    ENGINES = ("tensor", "scalar", "vector", "gpsimd", "sync")

    def __init__(self):
        self.ops = []
        self.last_by_eng = {}
        self.pending_barrier = None

    SERIALIZE = False

    def op(self, eng, fn, reads=(), writes=(), dma=False, deps=(), tag=""):
        o = Opr(eng, fn, dma, tag)
        if Sched.SERIALIZE and self.ops:
            o.deps.append(self.ops[-1])
        if self.pending_barrier is not None:
            o.deps.extend(self.pending_barrier.get("__all__", []))
            if not self.pending_barrier.get(eng, False):
                self.pending_barrier[eng] = True
            if all(self.pending_barrier.get(e, e not in self.last_by_eng)
                   for e in self.ENGINES):
                self.pending_barrier = None
        o.deps.extend(deps)
        for b in reads:
            b.r(o)
        for b in writes:
            b.w(o)
        self.ops.append(o)
        self.last_by_eng[eng] = o
        return o

    def barrier(self):
        lasts = [op for op in self.last_by_eng.values()]
        self.pending_barrier = {"__all__": lasts}

    NQSEM = 8

    def emit(self, nc, ctx):
        R = Sched.NQSEM
        qcount = {}
        for o in self.ops:
            if o.dma:
                n = qcount.get(o.eng, 0)
                o.dma_val = (n % R, 16 * (n // R + 1))
                qcount[o.eng] = n + 1
        for o in self.ops:
            for d in o.deps:
                if (not d.dma) and d.eng != o.eng:
                    d.needs_inc = True
        ecount = {}
        for o in self.ops:
            if (not o.dma) and o.needs_inc:
                ecount[o.eng] = ecount.get(o.eng, 0) + 1
                o.inc_val = ecount[o.eng]

        esem = {}
        qsem = {}
        for e in self.ENGINES:
            if ecount.get(e):
                esem[e] = ctx.enter_context(nc.semaphore(f"se_{e}"))
            if qcount.get(e):
                qsem[e] = [ctx.enter_context(nc.semaphore(f"sq_{e}{i}"))
                           for i in range(R)]

        by_eng = {e: [o for o in self.ops if o.eng == e] for e in self.ENGINES}

        def emit_engine(handle, ename):
            seen = {}
            for o in by_eng[ename]:
                waits = {}
                if o.dma:
                    sub, val = o.dma_val
                    if val > 16:
                        waits[("q", ename, sub)] = val - 16
                for d in o.deps:
                    if d.dma:
                        sub, val = d.dma_val
                        key = ("q", d.eng, sub)
                    else:
                        if d.eng == ename:
                            continue
                        if d.inc_val is None:
                            continue
                        key = ("e", d.eng, 0)
                        val = d.inc_val
                    if waits.get(key, 0) < val:
                        waits[key] = val
                for (kind, de, sub), val in sorted(waits.items()):
                    sem = qsem[de][sub] if kind == "q" else esem[de]
                    if seen.get((kind, de, sub), 0) < val:
                        handle.wait_ge(sem, val)
                        seen[(kind, de, sub)] = val
                ins = o.fn(handle)
                if o.dma:
                    ins.then_inc(qsem[ename][o.dma_val[0]], 16)
                elif o.needs_inc:
                    ins.then_inc(esem[ename], 1)

        with nc.Block() as block:
            if by_eng["sync"]:
                @block.sync
                def _(e):
                    emit_engine(e, "sync")
            if by_eng["tensor"]:
                @block.tensor
                def _(e):
                    emit_engine(e, "tensor")
            if by_eng["scalar"]:
                @block.scalar
                def _(e):
                    emit_engine(e, "scalar")
            if by_eng["vector"]:
                @block.vector
                def _(e):
                    emit_engine(e, "vector")
            if by_eng["gpsimd"]:
                @block.gpsimd
                def _(e):
                    emit_engine(e, "gpsimd")



def build_program(S, D, H, I):
    DC = D // 128
    TQ = S // R
    NW = TQ // WQ
    CPW = (R * WQ) // 128
    NT = S // 512
    NQT = TQ // 512
    NI = I // 128
    QUADS = I // 512
    scale = 1.0 / np.sqrt(DH)

    nc = bass.Bass("TRN2", target_bir_lowering=False)
    ctx = ExitStack()

    xT = nc.dram_tensor("xT", [D, S], F32, kind="ExternalInput")
    xqT = nc.dram_tensor("xqT", [D, TQ], F32, kind="ExternalInput")
    xq_rows = nc.dram_tensor("xq_rows", [TQ, D], F32, kind="ExternalInput")
    wq_h = nc.dram_tensor("wq", [D, D], F32, kind="ExternalInput")
    wk_h = nc.dram_tensor("wk", [D, D], F32, kind="ExternalInput")
    wv_h = nc.dram_tensor("wv", [D, D], F32, kind="ExternalInput")
    wo_h = nc.dram_tensor("wo", [D, D], F32, kind="ExternalInput")
    wgh = nc.dram_tensor("wgh", [NI, 128, D], F32, kind="ExternalInput")
    wuh = nc.dram_tensor("wuh", [NI, 128, D], F32, kind="ExternalInput")
    wdh = nc.dram_tensor("wdh", [NI, 128, D], F32, kind="ExternalInput")
    masks_h = nc.dram_tensor("masks", [CPW, 128, WQ], F32, kind="ExternalInput")

    out_h = nc.dram_tensor("out", [TQ, D], F32, kind="ExternalOutput")

    xn_dram = nc.dram_tensor("xn_dram", [DC, 128, S], BF16, kind="Internal")
    k_dram = nc.dram_tensor("k_dram", [H, 128, S], BF16, kind="Internal")
    v_dram = nc.dram_tensor("v_dram", [S // 128, 128, D], BF16, kind="Internal")
    q_dram = nc.dram_tensor("q_dram", [H, 128, TQ], BF16, kind="Internal")
    xnq_dram = nc.dram_tensor("xnq_dram", [DC, 128, TQ], BF16, kind="Internal")
    attn_dram = nc.dram_tensor("attn_dram", [TQ, D], F32, kind="Internal")
    hsp_dram = nc.dram_tensor("hsp_dram", [NI, 128, TQ], BF16, kind="Internal")
    wdb_dram = nc.dram_tensor("wdb_dram", [NI, 128, D], BF16, kind="Internal")

    sch = Sched()

    g = ExitStack()
    ones_b = g.enter_context(nc.sbuf_tensor("ones_b", [128, 2], BF16))
    ones_f = g.enter_context(nc.sbuf_tensor("ones_f", [1, 128], F32))
    ones_b_t = TBuf("ones_b")
    ones_f_t = TBuf("ones_f")
    sch.op("vector", lambda e: e.memset(ones_b[:, :], 1.0), writes=[ones_b_t])
    sch.op("vector", lambda e: e.memset(ones_f[:, :], 1.0), writes=[ones_f_t])
    eps_sb = g.enter_context(nc.sbuf_tensor("eps_sb", [1, 2], F32))
    eps_t = TBuf("eps")
    sch.op("vector", lambda e: e.memset(eps_sb[:, :], 1e-5), writes=[eps_t])

    kmax = g.enter_context(nc.sbuf_tensor("kmax", [1, max(H, 2)], F32))
    qmax = g.enter_context(nc.sbuf_tensor("qmax", [1, max(H, 2)], F32))
    kmax_t = [TBuf(f"kmax{h}") for h in range(H)]
    qmax_t = [TBuf(f"qmax{h}") for h in range(H)]
    sch.op("vector", lambda e: e.memset(kmax[:, :], 0.0), writes=kmax_t)
    sch.op("vector", lambda e: e.memset(qmax[:, :], 0.0), writes=qmax_t)

    p = ExitStack()
    xb = [p.enter_context(nc.sbuf_tensor(f"xb{s}", [128, DC * 512], BF16))
          for s in range(2)]
    x2b = [p.enter_context(nc.sbuf_tensor(f"x2b{s}", [128, 512], BF16))
           for s in range(2)]
    xn_sb = [p.enter_context(nc.sbuf_tensor(f"xn{s}", [128, DC * 512], BF16))
             for s in range(2)]
    wkb = p.enter_context(nc.sbuf_tensor("wkb", [128, DC * D], BF16))
    wst = p.enter_context(nc.sbuf_tensor("wst", [128, 2 * D], F32))
    kst = [p.enter_context(nc.sbuf_tensor(f"kst{s}", [128, 512], BF16))
           for s in range(2)]
    k2b = [p.enter_context(nc.sbuf_tensor(f"k2b{s}", [128, 512], BF16))
           for s in range(2)]
    ab_s = [p.enter_context(nc.sbuf_tensor(f"ab{s}", [1, 3 * 512], F32))
            for s in range(1)]
    ps_st = p.enter_context(nc.psum_tensor("ps_st", [1, 1024], F32))
    ps_ab = p.enter_context(nc.psum_tensor("ps_ab", [128, 1024], F32))
    ps_k = [p.enter_context(nc.psum_tensor(f"ps_k{s}", [128, 512], F32))
            for s in range(2)]
    ps_n = p.enter_context(nc.psum_tensor("ps_n", [1, 512], F32))

    xb_t = [TBuf() for _ in range(2)]
    x2b_t = [TBuf() for _ in range(2)]
    xn_t = [TBuf() for _ in range(2)]
    wkb_t = TBuf("wkb")
    kst_t = [TBuf(), TBuf()]
    k2b_t = [TBuf() for _ in range(2)]
    ab_t = [TBuf() for _ in range(1)]
    ps_st_t = TBuf("ps_st")
    ps_ab_t = TBuf("ps_ab")
    ps_k_t = [TBuf() for _ in range(2)]
    ps_n_t = TBuf("ps_n")
    xn_dram_t = TBuf("xn_dram")
    k_dram_t = TBuf("k_dram")

    wstage = [wst[:, 0:D], wst[:, D:2 * D]]
    wstage_t = [TBuf(), TBuf()]

    def load_weight_cast(w_h, dst_sb, dst_t):
        for cch in range(DC):
            sl = cch % 2

            def ldf(e, cch=cch, sl=sl):
                return e.dma_start(
                    out=wstage[sl],
                    in_=w_h.ap()[cch * 128:(cch + 1) * 128, :])
            o = sch.op("sync", ldf, writes=[wstage_t[sl]], dma=True)

            def cst(e, cch=cch, sl=sl):
                return e.tensor_copy(
                    out=dst_sb[:, cch * D:(cch + 1) * D], in_=wstage[sl])
            sch.op("vector", cst, reads=[wstage_t[sl]], writes=[dst_t])

    load_weight_cast(wk_h, wkb, wkb_t)

    inv_d = 1.0 / D

    def emit_front1(tt):
        sl = tt % 2
        BCH = DC // 4
        for ph in range(4):
            xsl = (tt * 4 + ph) % 2

            def ldx(e, tt=tt, ph=ph, xsl=xsl, BCH=BCH):
                return e.dma_start(
                    out=wst.ap()[:, xsl * D: xsl * D + BCH * 512]
                        .rearrange("p (c n) -> p c n", c=BCH),
                    in_=xT.ap()[ph * BCH * 128:(ph + 1) * BCH * 128,
                                tt * 512:(tt + 1) * 512]
                        .rearrange("(c p) n -> p c n", p=128))
            sch.op("sync", ldx, writes=[wstage_t[xsl]], dma=True)

            for i in range(BCH):
                cch = ph * BCH + i

                def cstx(e, cch=cch, i=i, xsl=xsl, sl=sl):
                    return e.tensor_copy(
                        out=xb[sl][:, cch * 512:(cch + 1) * 512],
                        in_=wst[:, xsl * D + i * 512: xsl * D + i * 512 + 512])
                sch.op("vector", cstx, reads=[wstage_t[xsl]],
                       writes=[xb_t[sl]])

        for cch in range(DC):
            def sq(e, cch=cch, sl=sl):
                return e.activation(
                    out=x2b[sl][:, :],
                    in_=xb[sl][:, cch * 512:(cch + 1) * 512], func=AF.Square)
            sch.op("scalar", sq, reads=[xb_t[sl]], writes=[x2b_t[sl]])

            def mm_s(e, cch=cch, sl=sl):
                return e.matmul(ps_st[:, 0:512], ones_b[:, 0:1],
                                xb[sl][:, cch * 512:(cch + 1) * 512],
                                start=(cch == 0), stop=(cch == DC - 1))
            sch.op("tensor", mm_s, reads=[xb_t[sl], ones_b_t],
                   writes=[ps_st_t])

            def mm_q(e, cch=cch, sl=sl):
                return e.matmul(ps_st[:, 512:1024], ones_b[:, 0:1],
                                x2b[sl][:, :],
                                start=(cch == 0), stop=(cch == DC - 1))
            sch.op("tensor", mm_q, reads=[x2b_t[sl], ones_b_t],
                   writes=[ps_st_t])


    def emit_front2(tt):
        sl = tt % 2
        asl = 0
        m_v = ab_s[asl][:, 0:512]
        A_v = ab_s[asl][:, 512:1024]
        B_v = ab_s[asl][:, 1024:1536]

        def mmean(e, asl=asl):
            return e.mul(out=ab_s[asl][:, 0:512], in_=ps_st[:, 0:512], mul=inv_d)
        sch.op("scalar", mmean, reads=[ps_st_t], writes=[ab_t[asl]])

        def m2f(e, asl=asl):
            return e.tensor_tensor(out=ab_s[asl][:, 1024:1536],
                                   in0=ab_s[asl][:, 0:512],
                                   in1=ab_s[asl][:, 0:512], op=ALU.mult)
        sch.op("vector", m2f, reads=[ab_t[asl]], writes=[ab_t[asl]])

        def mvar2(e, asl=asl):
            return e.scalar_tensor_tensor(
                out=ab_s[asl][:, 512:1024], in0=ps_st[:, 512:1024],
                scalar=inv_d, in1=ab_s[asl][:, 1024:1536],
                op0=ALU.mult, op1=ALU.subtract)
        sch.op("vector", mvar2, reads=[ps_st_t, ab_t[asl]], writes=[ab_t[asl]])

        def lnv(e, asl=asl):
            return e.activation(out=ab_s[asl][:, 512:1024],
                                in_=ab_s[asl][:, 512:1024],
                                func=AF.Ln, bias=eps_sb[:, 0:1])
        sch.op("scalar", lnv, reads=[ab_t[asl], eps_t], writes=[ab_t[asl]])

        def expv(e, asl=asl):
            return e.activation(out=ab_s[asl][:, 512:1024],
                                in_=ab_s[asl][:, 512:1024],
                                func=AF.Exp, scale=-0.5)
        sch.op("scalar", expv, reads=[ab_t[asl]], writes=[ab_t[asl]])

        def bfn(e, asl=asl):
            return e.scalar_tensor_tensor(
                out=ab_s[asl][:, 1024:1536], in0=ab_s[asl][:, 0:512],
                scalar=-1.0, in1=ab_s[asl][:, 512:1024],
                op0=ALU.mult, op1=ALU.mult)
        sch.op("vector", bfn, reads=[ab_t[asl]], writes=[ab_t[asl]])

        def bca(e, asl=asl):
            return e.matmul(ps_ab[:, 0:512], ones_f[:, :],
                            ab_s[asl][:, 512:1024], start=True, stop=True)
        sch.op("tensor", bca, reads=[ab_t[asl], ones_f_t], writes=[ps_ab_t])

        def bcb(e, asl=asl):
            return e.matmul(ps_ab[:, 512:1024], ones_f[:, :],
                            ab_s[asl][:, 1024:1536], start=True, stop=True)
        sch.op("tensor", bcb, reads=[ab_t[asl], ones_f_t], writes=[ps_ab_t])

        for cch in range(DC):
            def xnm(e, cch=cch, sl=sl):
                return e.tensor_tensor(
                    out=xn_sb[sl][:, cch * 512:(cch + 1) * 512],
                    in0=xb[sl][:, cch * 512:(cch + 1) * 512],
                    in1=ps_ab[:, 0:512], op=ALU.mult)
            sch.op("vector", xnm, reads=[xb_t[sl], ps_ab_t], writes=[xn_t[sl]])

            def xna(e, cch=cch, sl=sl):
                return e.tensor_tensor(
                    out=xn_sb[sl][:, cch * 512:(cch + 1) * 512],
                    in0=xn_sb[sl][:, cch * 512:(cch + 1) * 512],
                    in1=ps_ab[:, 512:1024], op=ALU.add)
            sch.op("vector", xna, reads=[xn_t[sl], ps_ab_t], writes=[xn_t[sl]])

        def xnw(e, tt=tt, sl=sl):
            return e.dma_start(
                out=xn_dram.ap()[:, :, tt * 512:(tt + 1) * 512]
                    .rearrange("c p n -> p c n"),
                in_=xn_sb[sl].ap().rearrange("p (c n) -> p c n", c=DC))
        sch.op("gpsimd", xnw, reads=[xn_t[sl]], writes=[xn_dram_t], dma=True)


    def emit_K1a(tt):
        sl = tt % 2
        for jc in range(H):
            kp = jc % 2

            for cch in range(DC):
                def mk(e, jc=jc, sl=sl, kp=kp, cch=cch):
                    return e.matmul(
                        ps_k[kp][:, :],
                        wkb[:, cch * D + jc * 128: cch * D + jc * 128 + 128],
                        xn_sb[sl][:, cch * 512:(cch + 1) * 512],
                        start=(cch == 0), stop=(cch == DC - 1))
                sch.op("tensor", mk, reads=[wkb_t, xn_t[sl]],
                       writes=[ps_k_t[kp]])

            def evk(e, jc=jc, kp=kp):
                return e.copy(out=kst[kp][:, :], in_=ps_k[kp][:, :])
            sch.op("scalar", evk, reads=[ps_k_t[kp]], writes=[kst_t[kp]])

            def kwh(e, jc=jc, kp=kp, tt=tt):
                return e.dma_start(
                    out=k_dram.ap()[jc, :, tt * 512:(tt + 1) * 512],
                    in_=kst[kp][:, :])
            sch.op("gpsimd", kwh, reads=[kst_t[kp]], writes=[k_dram_t],
                   dma=True)

            def k2f(e, jc=jc, kp=kp):
                return e.tensor_tensor(out=k2b[kp][:, :],
                                       in0=kst[kp][:, :],
                                       in1=kst[kp][:, :],
                                       op=ALU.mult)
            sch.op("vector", k2f, reads=[kst_t[kp]], writes=[k2b_t[kp]])

            def nmm(e, jc=jc, kp=kp):
                return e.matmul(ps_n[:, :], ones_b[:, 0:1], k2b[kp][:, :],
                                start=True, stop=True)
            sch.op("tensor", nmm, reads=[k2b_t[kp], ones_b_t], writes=[ps_n_t])

            def nred2(e, jc=jc):
                return e.tensor_reduce(out=ab_s[0][:, 0:1], in_=ps_n[:, :],
                                       axis=mybir.AxisListType.X, op=ALU.max)
            sch.op("vector", nred2, reads=[ps_n_t], writes=[ab_t[0]], tag="kn")

            def nacc(e, jc=jc):
                return e.tensor_tensor(out=kmax[:, jc:jc + 1],
                                       in0=kmax[:, jc:jc + 1],
                                       in1=ab_s[0][:, 0:1], op=ALU.max)
            sch.op("vector", nacc, reads=[ab_t[0], kmax_t[jc]],
                   writes=[kmax_t[jc]])



    emit_front1(0)
    emit_front2(0)
    for tt in range(NT):
        if tt + 1 < NT:
            emit_front1(tt + 1)
        emit_K1a(tt)
        if tt + 1 < NT:
            emit_front2(tt + 1)

    p.close()
    sch.barrier()

    p = ExitStack()
    xnv = [p.enter_context(nc.sbuf_tensor(f"xnv{s}", [128, DC * 512], BF16))
           for s in range(2)]
    wvb = p.enter_context(nc.sbuf_tensor("wvb", [128, DC * D], BF16))
    wst2 = p.enter_context(nc.sbuf_tensor("wst2", [128, 2 * D], F32))
    vst = p.enter_context(nc.sbuf_tensor("vst", [128, 4 * D], BF16))
    ps_v = [p.enter_context(nc.psum_tensor(f"ps_v{s}", [128, 512], F32))
            for s in range(2)]

    xnv_t = [TBuf() for _ in range(2)]
    wvb_t = TBuf("wvb")
    vst_t = TBuf("vst")
    ps_v_t = [TBuf() for _ in range(2)]
    wst2_t = [TBuf(), TBuf()]
    v_dram_t = TBuf("v_dram")

    for cch in range(DC):
        sl2 = cch % 2

        def ldw2(e, cch=cch, sl2=sl2):
            return e.dma_start(out=wst2[:, sl2 * D:(sl2 + 1) * D],
                               in_=wv_h.ap()[cch * 128:(cch + 1) * 128, :])
        sch.op("sync", ldw2, writes=[wst2_t[sl2]], dma=True)

        def cstw2(e, cch=cch, sl2=sl2):
            return e.tensor_copy(out=wvb[:, cch * D:(cch + 1) * D],
                                 in_=wst2[:, sl2 * D:(sl2 + 1) * D])
        sch.op("vector", cstw2, reads=[wst2_t[sl2]], writes=[wvb_t])

    NJ = D // 512
    for tt in range(NT):
        sl = tt % 2

        def ldxn(e, tt=tt, sl=sl):
            return e.dma_start(
                out=xnv[sl].ap().rearrange("p (c n) -> p c n", c=DC),
                in_=xn_dram.ap()[:, :, tt * 512:(tt + 1) * 512]
                    .rearrange("c p n -> p c n"))
        sch.op("sync", ldxn, reads=[xn_dram_t], writes=[xnv_t[sl]], dma=True)

        for tb in range(4):
            for js in range(NJ):
                vp = (tb * NJ + js) % 2
                for cch in range(DC):
                    def mv(e, tt=tt, sl=sl, tb=tb, js=js, vp=vp, cch=cch):
                        return e.matmul(
                            ps_v[vp][:, :],
                            xnv[sl][:, cch * 512 + tb * 128:
                                    cch * 512 + tb * 128 + 128],
                            wvb[:, cch * D + js * 512: cch * D + js * 512 + 512],
                            start=(cch == 0), stop=(cch == DC - 1))
                    sch.op("tensor", mv, reads=[xnv_t[sl], wvb_t],
                           writes=[ps_v_t[vp]])

                def evv(e, tb=tb, js=js, vp=vp):
                    return e.copy(out=vst[:, tb * D + js * 512:
                                          tb * D + js * 512 + 512],
                                  in_=ps_v[vp][:, :])
                sch.op("scalar", evv, reads=[ps_v_t[vp]], writes=[vst_t])

        def vw(e, tt=tt):
            return e.dma_start(
                out=v_dram.ap()[tt * 4:(tt + 1) * 4, :, :]
                    .rearrange("c p n -> p c n"),
                in_=vst.ap().rearrange("p (c n) -> p c n", c=4))
        sch.op("gpsimd", vw, reads=[vst_t], writes=[v_dram_t], dma=True)

    p.close()
    sch.barrier()

    p = ExitStack()
    xbq = [p.enter_context(nc.sbuf_tensor(f"xbq{s}", [128, DC * 512], BF16))
           for s in range(2)]
    x2q = [p.enter_context(nc.sbuf_tensor(f"x2q{s}", [128, 512], BF16))
           for s in range(2)]
    xnq_sb = [p.enter_context(nc.sbuf_tensor(f"xnq{s}", [128, DC * 512], BF16))
              for s in range(2)]
    wqb = p.enter_context(nc.sbuf_tensor("wqb", [128, DC * D], BF16))
    wstq = p.enter_context(nc.sbuf_tensor("wstq", [128, 2 * D], F32))
    qst = [p.enter_context(nc.sbuf_tensor(f"qst{s}", [128, 512], BF16))
           for s in range(2)]
    q2b = [p.enter_context(nc.sbuf_tensor(f"q2b{s}", [128, 512], BF16))
           for s in range(2)]
    abq = [p.enter_context(nc.sbuf_tensor(f"abq{s}", [1, 3 * 512], F32))
           for s in range(1)]
    ps_st2 = p.enter_context(nc.psum_tensor("ps_st2", [1, 1024], F32))
    ps_ab2 = p.enter_context(nc.psum_tensor("ps_ab2", [128, 1024], F32))
    ps_q = [p.enter_context(nc.psum_tensor(f"ps_q{s}", [128, 512], F32))
            for s in range(2)]
    ps_n2 = p.enter_context(nc.psum_tensor("ps_n2", [1, 512], F32))

    xbq_t = [TBuf() for _ in range(2)]
    x2q_t = [TBuf() for _ in range(2)]
    xnq_t = [TBuf() for _ in range(2)]
    wqb_t = TBuf()
    wstq_t = [TBuf(), TBuf(), TBuf(), TBuf()]
    qst_t = [TBuf(), TBuf()]
    q2b_t = [TBuf() for _ in range(2)]
    abq_t = [TBuf() for _ in range(1)]
    ps_st2_t = TBuf()
    ps_ab2_t = TBuf()
    ps_q_t = [TBuf() for _ in range(2)]
    ps_n2_t = TBuf()
    q_dram_t = TBuf()
    xnq_dram_t = TBuf()

    for cch in range(DC):
        sl2 = cch % 2

        def ldwq(e, cch=cch, sl2=sl2):
            return e.dma_start(out=wstq[:, sl2 * D:(sl2 + 1) * D],
                               in_=wq_h.ap()[cch * 128:(cch + 1) * 128, :])
        sch.op("sync", ldwq, writes=[wstq_t[sl2]], dma=True)

        def cstwq(e, cch=cch, sl2=sl2):
            return e.tensor_copy(out=wqb[:, cch * D:(cch + 1) * D],
                                 in_=wstq[:, sl2 * D:(sl2 + 1) * D])
        sch.op("vector", cstwq, reads=[wstq_t[sl2]], writes=[wqb_t])

    for tt in range(NQT):
        sl = tt % 2
        BCH = DC // 4
        for ph in range(4):
            xsl = (tt * 4 + ph) % 2

            def ldxq(e, tt=tt, ph=ph, xsl=xsl, BCH=BCH):
                return e.dma_start(
                    out=wstq.ap()[:, xsl * D: xsl * D + BCH * 512]
                        .rearrange("p (c n) -> p c n", c=BCH),
                    in_=xqT.ap()[ph * BCH * 128:(ph + 1) * BCH * 128,
                                 tt * 512:(tt + 1) * 512]
                        .rearrange("(c p) n -> p c n", p=128))
            sch.op("sync", ldxq, writes=[wstq_t[xsl]], dma=True)

            for i in range(BCH):
                cch = ph * BCH + i

                def cstxq(e, cch=cch, i=i, xsl=xsl, sl=sl):
                    return e.tensor_copy(
                        out=xbq[sl][:, cch * 512:(cch + 1) * 512],
                        in_=wstq[:, xsl * D + i * 512:
                                 xsl * D + i * 512 + 512])
                sch.op("vector", cstxq, reads=[wstq_t[xsl]],
                       writes=[xbq_t[sl]])

        for cch in range(DC):
            def sq2(e, cch=cch, sl=sl):
                return e.activation(
                    out=x2q[sl][:, :],
                    in_=xbq[sl][:, cch * 512:(cch + 1) * 512], func=AF.Square)
            sch.op("scalar", sq2, reads=[xbq_t[sl]], writes=[x2q_t[sl]])

            def mm_s2(e, cch=cch, sl=sl):
                return e.matmul(ps_st2[:, 0:512], ones_b[:, 0:1],
                                xbq[sl][:, cch * 512:(cch + 1) * 512],
                                start=(cch == 0), stop=(cch == DC - 1))
            sch.op("tensor", mm_s2, reads=[xbq_t[sl], ones_b_t],
                   writes=[ps_st2_t])

            def mm_q2(e, cch=cch, sl=sl):
                return e.matmul(ps_st2[:, 512:1024], ones_b[:, 0:1],
                                x2q[sl][:, :],
                                start=(cch == 0), stop=(cch == DC - 1))
            sch.op("tensor", mm_q2, reads=[x2q_t[sl], ones_b_t],
                   writes=[ps_st2_t])

        asl = 0

        def mmean2(e, asl=asl):
            return e.mul(out=abq[asl][:, 0:512], in_=ps_st2[:, 0:512],
                         mul=inv_d)
        sch.op("scalar", mmean2, reads=[ps_st2_t], writes=[abq_t[asl]])

        def m2f2(e, asl=asl):
            return e.tensor_tensor(out=abq[asl][:, 1024:1536],
                                   in0=abq[asl][:, 0:512],
                                   in1=abq[asl][:, 0:512], op=ALU.mult)
        sch.op("vector", m2f2, reads=[abq_t[asl]], writes=[abq_t[asl]])

        def mvar22(e, asl=asl):
            return e.scalar_tensor_tensor(
                out=abq[asl][:, 512:1024], in0=ps_st2[:, 512:1024],
                scalar=inv_d, in1=abq[asl][:, 1024:1536],
                op0=ALU.mult, op1=ALU.subtract)
        sch.op("vector", mvar22, reads=[ps_st2_t, abq_t[asl]],
               writes=[abq_t[asl]])

        def lnv2(e, asl=asl):
            return e.activation(out=abq[asl][:, 512:1024],
                                in_=abq[asl][:, 512:1024], func=AF.Ln,
                                bias=eps_sb[:, 0:1])
        sch.op("scalar", lnv2, reads=[abq_t[asl], eps_t], writes=[abq_t[asl]])

        def expv2(e, asl=asl):
            return e.activation(out=abq[asl][:, 512:1024],
                                in_=abq[asl][:, 512:1024], func=AF.Exp,
                                scale=-0.5)
        sch.op("scalar", expv2, reads=[abq_t[asl]], writes=[abq_t[asl]])

        def bfn2(e, asl=asl):
            return e.scalar_tensor_tensor(
                out=abq[asl][:, 1024:1536], in0=abq[asl][:, 0:512],
                scalar=-1.0, in1=abq[asl][:, 512:1024],
                op0=ALU.mult, op1=ALU.mult)
        sch.op("vector", bfn2, reads=[abq_t[asl]], writes=[abq_t[asl]])

        def bca2(e, asl=asl):
            return e.matmul(ps_ab2[:, 0:512], ones_f[:, :],
                            abq[asl][:, 512:1024], start=True, stop=True)
        sch.op("tensor", bca2, reads=[abq_t[asl], ones_f_t], writes=[ps_ab2_t])

        def bcb2(e, asl=asl):
            return e.matmul(ps_ab2[:, 512:1024], ones_f[:, :],
                            abq[asl][:, 1024:1536], start=True, stop=True)
        sch.op("tensor", bcb2, reads=[abq_t[asl], ones_f_t], writes=[ps_ab2_t])

        for cch in range(DC):
            def xnm2(e, cch=cch, sl=sl):
                return e.tensor_tensor(
                    out=xnq_sb[sl][:, cch * 512:(cch + 1) * 512],
                    in0=xbq[sl][:, cch * 512:(cch + 1) * 512],
                    in1=ps_ab2[:, 0:512], op=ALU.mult)
            sch.op("vector", xnm2, reads=[xbq_t[sl], ps_ab2_t],
                   writes=[xnq_t[sl]])

            def xna2(e, cch=cch, sl=sl):
                return e.tensor_tensor(
                    out=xnq_sb[sl][:, cch * 512:(cch + 1) * 512],
                    in0=xnq_sb[sl][:, cch * 512:(cch + 1) * 512],
                    in1=ps_ab2[:, 512:1024], op=ALU.add)
            sch.op("vector", xna2, reads=[xnq_t[sl], ps_ab2_t],
                   writes=[xnq_t[sl]])

        def xnqw(e, tt=tt, sl=sl):
            return e.dma_start(
                out=xnq_dram.ap()[:, :, tt * 512:(tt + 1) * 512]
                    .rearrange("c p n -> p c n"),
                in_=xnq_sb[sl].ap().rearrange("p (c n) -> p c n", c=DC))
        sch.op("gpsimd", xnqw, reads=[xnq_t[sl]], writes=[xnq_dram_t],
               dma=True)

        for jc in range(H):
            qp = jc % 2
            for cch in range(DC):
                def mq(e, jc=jc, sl=sl, qp=qp, cch=cch):
                    return e.matmul(
                        ps_q[qp][:, :],
                        wqb[:, cch * D + jc * 128: cch * D + jc * 128 + 128],
                        xnq_sb[sl][:, cch * 512:(cch + 1) * 512],
                        start=(cch == 0), stop=(cch == DC - 1))
                sch.op("tensor", mq, reads=[wqb_t, xnq_t[sl]],
                       writes=[ps_q_t[qp]])

            def evq(e, jc=jc, qp=qp):
                return e.copy(out=qst[qp][:, :], in_=ps_q[qp][:, :])
            sch.op("scalar", evq, reads=[ps_q_t[qp]], writes=[qst_t[qp]])

            def qwh(e, jc=jc, qp=qp, tt=tt):
                return e.dma_start(
                    out=q_dram.ap()[jc, :, tt * 512:(tt + 1) * 512],
                    in_=qst[qp][:, :])
            sch.op("gpsimd", qwh, reads=[qst_t[qp]], writes=[q_dram_t],
                   dma=True)

            def q2f(e, jc=jc, qp=qp):
                return e.tensor_tensor(out=q2b[qp][:, :],
                                       in0=qst[qp][:, :],
                                       in1=qst[qp][:, :],
                                       op=ALU.mult)
            sch.op("vector", q2f, reads=[qst_t[qp]], writes=[q2b_t[qp]])

            def nmm2(e, jc=jc, qp=qp):
                return e.matmul(ps_n2[:, :], ones_b[:, 0:1], q2b[qp][:, :],
                                start=True, stop=True)
            sch.op("tensor", nmm2, reads=[q2b_t[qp], ones_b_t],
                   writes=[ps_n2_t])

            def nred22(e, jc=jc):
                return e.tensor_reduce(out=abq[0][:, 0:1], in_=ps_n2[:, :],
                                       axis=mybir.AxisListType.X, op=ALU.max)
            sch.op("vector", nred22, reads=[ps_n2_t], writes=[abq_t[0]])

            def nacc2(e, jc=jc):
                return e.tensor_tensor(out=qmax[:, jc:jc + 1],
                                       in0=qmax[:, jc:jc + 1],
                                       in1=abq[0][:, 0:1], op=ALU.max)
            sch.op("vector", nacc2, reads=[abq_t[0], qmax_t[jc]],
                   writes=[qmax_t[jc]])


    p.close()
    sch.barrier()

    attn_bf_dram = nc.dram_tensor("attn_bf", [H, 128, TQ], BF16,
                                  kind="Internal")
    p = ExitStack()
    ksb = [p.enter_context(nc.sbuf_tensor(f"ksb{s}", [128, S], BF16))
           for s in range(2)]
    vsb = [p.enter_context(nc.sbuf_tensor(f"vsb{s}", [128, (S // 128) * 128],
                                          BF16)) for s in range(2)]
    qsb = [p.enter_context(nc.sbuf_tensor(f"qsb{s}", [128, TQ], BF16))
           for s in range(2)]
    masks_sb = p.enter_context(nc.sbuf_tensor("masks_sb", [128, CPW * WQ],
                                              BF16))
    mstage = p.enter_context(nc.sbuf_tensor("mstage", [128, WQ], F32))
    P_sb = [p.enter_context(nc.sbuf_tensor(f"P{s}", [128, 1024], BF16))
            for s in range(4)]
    Pm_sb = [p.enter_context(nc.sbuf_tensor(f"Pm{s}", [128, WQ], BF16))
             for s in range(2)]
    pacc_sb = [p.enter_context(nc.sbuf_tensor(f"pacc{s}", [128, WQ], F32))
               for s in range(2)]
    pcast_sb = [p.enter_context(nc.sbuf_tensor(f"pcast{s}", [128, WQ], BF16))
                for s in range(2)]
    bias_all = p.enter_context(nc.sbuf_tensor("bias_all", [128, max(H, 2)],
                                               F32))
    sc_sb = p.enter_context(nc.sbuf_tensor("sc_sb", [1, 2 * max(H, 2)], F32))
    recl_sb = [p.enter_context(nc.sbuf_tensor(f"recl{s}", [1, WQ], F32))
               for s in range(2)]
    rb_sb = [p.enter_context(nc.sbuf_tensor(f"rb{s}", [128, WQ], F32))
             for s in range(2)]
    attn_win = [p.enter_context(nc.sbuf_tensor(f"awin{s}", [128, WQ], BF16))
                for s in range(2)]
    ps_sc = [p.enter_context(nc.psum_tensor(f"ps_sc{s}", [128, 1024], F32))
             for s in range(2)]
    ps_o = [p.enter_context(nc.psum_tensor(f"ps_o{s}", [128, WQ], F32))
            for s in range(2)]
    ps_L = p.enter_context(nc.psum_tensor("ps_L", [1, WQ], F32))
    ps_rb = p.enter_context(nc.psum_tensor("ps_rb", [128, WQ], F32))

    ksb_t = [TBuf() for _ in range(2)]
    vsb_t = [TBuf() for _ in range(2)]
    qsb_t = [TBuf() for _ in range(2)]
    attn_win_t = [TBuf(), TBuf()]
    attn_bf_t = TBuf("attn_bf")
    masks_t = TBuf()
    mstage_t = TBuf()
    P_t = [TBuf() for _ in range(4)]
    Pm_t = [TBuf(), TBuf()]
    pacc_t = [TBuf(), TBuf()]
    pcast_t = [TBuf(), TBuf()]
    bias_all_t = TBuf()
    sc_t = TBuf()
    recl_t = [TBuf() for _ in range(2)]
    rb_t = [TBuf() for _ in range(2)]
    ps_sc_t = [TBuf() for _ in range(2)]
    ps_o_t = [TBuf() for _ in range(2)]
    ps_L_t = TBuf()
    ps_rb_t = TBuf()
    attn_dram_t = TBuf()

    for j in range(CPW):
        def ldm(e, j=j):
            return e.dma_start(out=mstage[:, :], in_=masks_h.ap()[j, :, :])
        sch.op("sync", ldm, writes=[mstage_t], dma=True)

        def cstm(e, j=j):
            return e.tensor_copy(out=masks_sb[:, j * WQ:(j + 1) * WQ],
                                 in_=mstage[:, :])
        sch.op("vector", cstm, reads=[mstage_t], writes=[masks_t])

    wdstg = [p.enter_context(nc.sbuf_tensor(f"wdstg{s}", [128, D], F32))
             for s in range(2)]
    wdbs = [p.enter_context(nc.sbuf_tensor(f"wdbs{s}", [128, D], BF16))
            for s in range(2)]
    wdstg_t = [TBuf(), TBuf()]
    wdbs_t = [TBuf(), TBuf()]
    wdb_dram_t = TBuf()

    def emit_wd_cast(ck):
        sl2 = ck % 2

        def ldwd(e, ck=ck, sl2=sl2):
            return e.dma_start(out=wdstg[sl2][:, :], in_=wdh.ap()[ck, :, :])
        sch.op("sync", ldwd, writes=[wdstg_t[sl2]], dma=True)

        def cstwd(e, ck=ck, sl2=sl2):
            return e.tensor_copy(out=wdbs[sl2][:, :], in_=wdstg[sl2][:, :])
        sch.op("vector", cstwd, reads=[wdstg_t[sl2]], writes=[wdbs_t[sl2]])

        def stwd(e, ck=ck, sl2=sl2):
            return e.dma_start(out=wdb_dram.ap()[ck, :, :], in_=wdbs[sl2][:, :])
        sch.op("gpsimd", stwd, reads=[wdbs_t[sl2]], writes=[wdb_dram_t],
               dma=True)

    def load_head(h):
        sl = h % 2

        def ldk(e, h=h):
            return e.dma_start(out=ksb[sl][:, :], in_=k_dram.ap()[h, :, :])
        sch.op("sync", ldk, reads=[k_dram_t], writes=[ksb_t[sl]], dma=True)

        def ldv(e, h=h):
            return e.dma_start(
                out=vsb[sl].ap().rearrange("p (c n) -> p c n", c=S // 128),
                in_=v_dram.ap()[:, :, h * 128:(h + 1) * 128]
                    .rearrange("c p n -> p c n"))
        sch.op("sync", ldv, reads=[v_dram_t], writes=[vsb_t[sl]], dma=True)

        def ldq(e, h=h):
            return e.dma_start(out=qsb[sl][:, :], in_=q_dram.ap()[h, :, :])
        sch.op("sync", ldq, reads=[q_dram_t], writes=[qsb_t[sl]], dma=True)

    def b_all_m(e):
        return e.tensor_tensor(out=sc_sb[:, 0:H], in0=qmax[:, 0:H],
                               in1=kmax[:, 0:H], op=ALU.mult)
    sch.op("vector", b_all_m, reads=qmax_t + kmax_t, writes=[sc_t])

    def b_all_ln(e):
        return e.activation(out=sc_sb[:, H:2 * H], in_=sc_sb[:, 0:H],
                            func=AF.Ln)
    sch.op("scalar", b_all_ln, reads=[sc_t], writes=[sc_t])

    def b_all_exp(e):
        return e.activation(out=sc_sb[:, H:2 * H], in_=sc_sb[:, H:2 * H],
                            func=AF.Exp, scale=0.5)
    sch.op("scalar", b_all_exp, reads=[sc_t], writes=[sc_t])

    def b_all_sc(e):
        return e.mul(out=sc_sb[:, H:2 * H], in_=sc_sb[:, H:2 * H],
                     mul=-(1.02 * scale))
    sch.op("scalar", b_all_sc, reads=[sc_t], writes=[sc_t])

    def b_all_bc(e):
        return e.matmul(ps_rb[:, 0:H], ones_f[:, :], sc_sb[:, H:2 * H],
                        start=True, stop=True)
    sch.op("tensor", b_all_bc, reads=[sc_t, ones_f_t], writes=[ps_rb_t])

    def b_all_cv(e):
        return e.copy(out=bias_all[:, :], in_=ps_rb[:, 0:H])
    sch.op("scalar", b_all_cv, reads=[ps_rb_t], writes=[bias_all_t])

    def lane_gen(lane):
        sl = lane
        for h in range(lane, H, 2):
            load_head(h)
            yield
            for w in range(NW):
                nk = CPW * (w + 1)
                npair = nk // 2

                def emit_scores(kp):
                    for half in (0, 1):
                        ck = 2 * kp + half

                        def mms(e, h=h, sl=sl, w=w, ck=ck, half=half):
                            return e.matmul(
                                ps_sc[sl][:, half * WQ:(half + 1) * WQ],
                                ksb[sl][:, ck * 128:(ck + 1) * 128],
                                qsb[sl][:, w * WQ:(w + 1) * WQ],
                                start=True, stop=True)
                        sch.op("tensor", mms, reads=[ksb_t[sl], qsb_t[sl]],
                               writes=[ps_sc_t[sl]])

                emit_scores(0)
                for kp in range(npair):
                    ck0, ck1 = 2 * kp, 2 * kp + 1
                    psl = lane * 2 + kp % 2

                    def mexp(e, sl=sl, psl=psl, h=h):
                        return e.activation(out=P_sb[psl][:, :],
                                            in_=ps_sc[sl][:, :], func=AF.Exp,
                                            scale=scale,
                                            bias=bias_all[:, h:h + 1])
                    sch.op("scalar", mexp, reads=[ps_sc_t[sl], bias_all_t],
                           writes=[P_t[psl]])

                    if kp + 1 < npair:
                        emit_scores(kp + 1)

                    for half, ck in ((0, ck0), (1, ck1)):
                        j = ck - CPW * w
                        if j >= 0:
                            def mmask(e, psl=psl, half=half, j=j):
                                return e.tensor_tensor(
                                    out=P_sb[psl][:, half * WQ:
                                                  (half + 1) * WQ],
                                    in0=P_sb[psl][:, half * WQ:
                                                  (half + 1) * WQ],
                                    in1=masks_sb[:, j * WQ:(j + 1) * WQ],
                                    op=ALU.mult)
                            sch.op("vector", mmask, reads=[P_t[psl], masks_t],
                                   writes=[P_t[psl]])

                    if kp == 0:
                        def macc0(e, psl=psl, lane=lane):
                            return e.tensor_tensor(out=pacc_sb[lane][:, :],
                                                   in0=P_sb[psl][:, 0:WQ],
                                                   in1=P_sb[psl][:, WQ:2 * WQ],
                                                   op=ALU.add)
                        sch.op("vector", macc0, reads=[P_t[psl]],
                               writes=[pacc_t[lane]])
                    else:
                        def mpadd(e, psl=psl, lane=lane):
                            return e.tensor_tensor(out=Pm_sb[lane][:, :],
                                                   in0=P_sb[psl][:, 0:WQ],
                                                   in1=P_sb[psl][:, WQ:2 * WQ],
                                                   op=ALU.add)
                        sch.op("vector", mpadd, reads=[P_t[psl]],
                               writes=[Pm_t[lane]])

                        def macc(e, lane=lane):
                            return e.tensor_tensor(out=pacc_sb[lane][:, :],
                                                   in0=pacc_sb[lane][:, :],
                                                   in1=Pm_sb[lane][:, :],
                                                   op=ALU.add)
                        sch.op("vector", macc, reads=[Pm_t[lane],
                                                      pacc_t[lane]],
                               writes=[pacc_t[lane]])

                    for half, ck in ((0, ck0), (1, ck1)):
                        def mmav(e, sl=sl, psl=psl, half=half, ck=ck, kp=kp,
                                 npair=npair):
                            return e.matmul(
                                ps_o[sl][:, :],
                                vsb[sl][:, ck * 128:(ck + 1) * 128],
                                P_sb[psl][:, half * WQ:(half + 1) * WQ],
                                start=(kp == 0 and half == 0),
                                stop=(kp == npair - 1 and half == 1))
                        sch.op("tensor", mmav, reads=[P_t[psl], vsb_t[sl]],
                               writes=[ps_o_t[sl]])
                    yield

                def wcast(e, lane=lane):
                    return e.tensor_copy(out=pcast_sb[lane][:, :],
                                         in_=pacc_sb[lane][:, :])
                sch.op("vector", wcast, reads=[pacc_t[lane]],
                       writes=[pcast_t[lane]])

                def wsum(e, lane=lane):
                    return e.matmul(ps_L[:, :], ones_b[:, 0:1],
                                    pcast_sb[lane][:, :],
                                    start=True, stop=True)
                sch.op("tensor", wsum, reads=[pcast_t[lane], ones_b_t],
                       writes=[ps_L_t])

                def wln(e, lane=lane):
                    return e.activation(out=recl_sb[lane][:, :],
                                        in_=ps_L[:, :], func=AF.Ln)
                sch.op("scalar", wln, reads=[ps_L_t], writes=[recl_t[lane]])

                def wexp(e, lane=lane):
                    return e.activation(out=recl_sb[lane][:, :],
                                        in_=recl_sb[lane][:, :], func=AF.Exp,
                                        scale=-1.0)
                sch.op("scalar", wexp, reads=[recl_t[lane]],
                       writes=[recl_t[lane]])

                def wbc(e, lane=lane):
                    return e.matmul(ps_rb[:, :], ones_f[:, :],
                                    recl_sb[lane][:, :],
                                    start=True, stop=True)
                sch.op("tensor", wbc, reads=[recl_t[lane], ones_f_t],
                       writes=[ps_rb_t])

                def wrbc(e, lane=lane):
                    return e.copy(out=rb_sb[lane][:, :], in_=ps_rb[:, :])
                sch.op("scalar", wrbc, reads=[ps_rb_t], writes=[rb_t[lane]])

                def wnorm(e, h=h, w=w, lane=lane):
                    return e.tensor_tensor(
                        out=attn_win[lane][:, :],
                        in0=ps_o[lane][:, :], in1=rb_sb[lane][:, :],
                        op=ALU.mult)
                sch.op("vector", wnorm, reads=[ps_o_t[lane], rb_t[lane]],
                       writes=[attn_win_t[lane]])

                def wst_a(e, h=h, w=w, lane=lane):
                    return e.dma_start(
                        out=attn_bf_dram.ap()[h, :, w * WQ:(w + 1) * WQ],
                        in_=attn_win[lane][:, :])
                sch.op("gpsimd", wst_a, reads=[attn_win_t[lane]],
                       writes=[attn_bf_t], dma=True)
                yield

            for i in range(NI // H):
                emit_wd_cast(h * (NI // H) + i)
            yield

    _gens = [lane_gen(0), lane_gen(1)]
    _alive = [True, True]
    _gi = 0
    while _alive[0] or _alive[1]:
        if _alive[_gi % 2]:
            try:
                next(_gens[_gi % 2])
            except StopIteration:
                _alive[_gi % 2] = False
        _gi += 1

    p.close()
    sch.barrier()

    p = ExitStack()
    wob = p.enter_context(nc.sbuf_tensor("wob", [128, H * D], BF16))
    atile = [p.enter_context(nc.sbuf_tensor(f"atile{s}", [128, H * 128],
                                            BF16)) for s in range(2)]
    wsto = [p.enter_context(nc.sbuf_tensor(f"wsto{s}", [128, D], F32))
            for s in range(2)]
    ost = [p.enter_context(nc.sbuf_tensor(f"ost{s}", [128, 512], F32))
           for s in range(2)]
    ps_wo = [p.enter_context(nc.psum_tensor(f"ps_wo{s}", [128, 512], F32))
             for s in range(2)]
    wob_t = TBuf()
    atile_t = [TBuf(), TBuf()]
    wsto_t = [TBuf(), TBuf()]
    ost_t = [TBuf(), TBuf()]
    ps_wo_t = [TBuf(), TBuf()]

    for f in range(H):
        sl2 = f % 2

        def ldwo(e, f=f, sl2=sl2):
            return e.dma_start(out=wsto[sl2][:, :],
                               in_=wo_h.ap()[f * 128:(f + 1) * 128, :])
        sch.op("sync", ldwo, writes=[wsto_t[sl2]], dma=True)

        def cstwo(e, f=f, sl2=sl2):
            return e.tensor_copy(out=wob[:, f * D:(f + 1) * D],
                                 in_=wsto[sl2][:, :])
        sch.op("vector", cstwo, reads=[wsto_t[sl2]], writes=[wob_t])

    NJT = D // 512
    for qb in range(TQ // 128):
        asl3 = qb % 2

        def lda3(e, qb=qb, asl3=asl3):
            return e.dma_start(
                out=atile[asl3].ap().rearrange("p (c n) -> p c n", c=H),
                in_=attn_bf_dram.ap()[:, :, qb * 128:(qb + 1) * 128]
                    .rearrange("c p n -> p c n"))
        sch.op("sync", lda3, reads=[attn_bf_t], writes=[atile_t[asl3]],
               dma=True)

        for jt in range(NJT):
            wp = (qb * NJT + jt) % 2
            for f in range(H):
                def mwo(e, qb=qb, jt=jt, f=f, wp=wp, asl3=asl3):
                    return e.matmul(
                        ps_wo[wp][:, :],
                        atile[asl3][:, f * 128:(f + 1) * 128],
                        wob[:, f * D + jt * 512: f * D + jt * 512 + 512],
                        start=(f == 0), stop=(f == H - 1))
                sch.op("tensor", mwo, reads=[atile_t[asl3], wob_t],
                       writes=[ps_wo_t[wp]])

            def evo(e, wp=wp):
                return e.copy(out=ost[wp][:, :], in_=ps_wo[wp][:, :])
            sch.op("scalar", evo, reads=[ps_wo_t[wp]], writes=[ost_t[wp]])

            def sto(e, qb=qb, jt=jt, wp=wp):
                return e.dma_start(
                    out=attn_dram.ap()[qb * 128:(qb + 1) * 128,
                                       jt * 512:(jt + 1) * 512],
                    in_=ost[wp][:, :])
            sch.op("gpsimd", sto, reads=[ost_t[wp]], writes=[attn_dram_t],
                   dma=True)

    p.close()
    sch.barrier()

    p = ExitStack()
    xnq_full = p.enter_context(nc.sbuf_tensor("xnq_full", [128, DC * TQ],
                                              BF16))
    wstg = [p.enter_context(nc.sbuf_tensor(f"wstg{s}", [128, D], F32))
            for s in range(2)]
    wgb = [p.enter_context(nc.sbuf_tensor(f"wgb{s}", [128, D], BF16))
           for s in range(4)]
    wub = [p.enter_context(nc.sbuf_tensor(f"wub{s}", [128, D], BF16))
           for s in range(4)]
    T4 = 512
    hX = [p.enter_context(nc.sbuf_tensor(f"hX{x}", [128, TQ], F32))
          for x in range(4)]
    tmpA = p.enter_context(nc.sbuf_tensor("tmpA", [128, T4], F32))
    tmpB = p.enter_context(nc.sbuf_tensor("tmpB", [128, T4], F32))
    m1_sb = p.enter_context(nc.sbuf_tensor("m1_sb", [128, T4], F32))
    n1_sb = p.enter_context(nc.sbuf_tensor("n1_sb", [128, T4], F32))
    t2_sb = p.enter_context(nc.sbuf_tensor("t2_sb", [128, T4], F32))
    hsp_sb = [p.enter_context(nc.sbuf_tensor(f"hsp{x}", [128, T4], BF16))
              for x in range(4)]
    NT4 = TQ // T4
    ps_g = [p.enter_context(nc.psum_tensor(f"ps_g{s}", [128, T4], F32))
            for s in range(NT4)]
    ps_u = [p.enter_context(nc.psum_tensor(f"ps_u{s}", [128, T4], F32))
            for s in range(NT4)]

    import os as _os
    KDBG = bool(_os.environ.get("KDBG"))
    if KDBG:
        dbg_h_dram = nc.dram_tensor("dbg_h", [NI, 128, T4], F32,
                                    kind="ExternalOutput")
        dbg_h_t = TBuf()

    xnq_full_t = TBuf()
    wstg_t = [TBuf(), TBuf()]
    wgb_t = [TBuf() for _ in range(4)]
    wub_t = [TBuf() for _ in range(4)]
    hX_t = [TBuf() for _ in range(4)]
    tmpA_t = TBuf()
    tmpB_t = TBuf()
    m1_t = TBuf()
    n1_t = TBuf()
    t2_t = TBuf()
    hsp_t = [TBuf() for _ in range(4)]
    ps_g_t = [TBuf() for _ in range(NT4)]
    ps_u_t = [TBuf() for _ in range(NT4)]
    hsp_dram_t = TBuf()

    def ldxnq(e):
        return e.dma_start(
            out=xnq_full.ap().rearrange("p (c n) -> p c n", c=DC),
            in_=xnq_dram.ap().rearrange("c p n -> p c n"))
    sch.op("sync", ldxnq, reads=[xnq_dram_t], writes=[xnq_full_t], dma=True)

    NH2 = T4 // 512
    for quad in range(QUADS):
        for X in range(4):
            ck = X * QUADS + quad
            s0 = (2 * X) % 2
            s1 = (2 * X + 1) % 2

            def ldwg(e, ck=ck, s0=s0):
                return e.dma_start(out=wstg[s0][:, :], in_=wgh.ap()[ck, :, :])
            sch.op("sync", ldwg, writes=[wstg_t[s0]], dma=True)

            def cwg(e, s0=s0, X=X):
                return e.copy(out=wgb[X][:, :], in_=wstg[s0][:, :])
            sch.op("scalar", cwg, reads=[wstg_t[s0]], writes=[wgb_t[X]])

            def ldwu(e, ck=ck, s1=s1):
                return e.dma_start(out=wstg[s1][:, :], in_=wuh.ap()[ck, :, :])
            sch.op("sync", ldwu, writes=[wstg_t[s1]], dma=True)

            def cwu(e, s1=s1, X=X):
                return e.copy(out=wub[X][:, :], in_=wstg[s1][:, :])
            sch.op("scalar", cwu, reads=[wstg_t[s1]], writes=[wub_t[X]])

        for X in range(4):
            for cch in range(DC):
                for tt in range(NT4):
                    def mg(e, X=X, cch=cch, tt=tt):
                        return e.matmul(
                            ps_g[tt][:, :],
                            wgb[X][:, cch * 128:(cch + 1) * 128],
                            xnq_full[:, cch * TQ + tt * T4:
                                     cch * TQ + tt * T4 + T4],
                            start=(cch == 0), stop=(cch == DC - 1))
                    sch.op("tensor", mg, reads=[wgb_t[X], xnq_full_t],
                           writes=[ps_g_t[tt]])
                for tt in range(NT4):
                    def mu(e, X=X, cch=cch, tt=tt):
                        return e.matmul(
                            ps_u[tt][:, :],
                            wub[X][:, cch * 128:(cch + 1) * 128],
                            xnq_full[:, cch * TQ + tt * T4:
                                     cch * TQ + tt * T4 + T4],
                            start=(cch == 0), stop=(cch == DC - 1))
                    sch.op("tensor", mu, reads=[wub_t[X], xnq_full_t],
                           writes=[ps_u_t[tt]])

            for tt in range(NT4):
                def msilu(e, X=X, tt=tt):
                    return e.activation(out=hX[X][:, tt * T4:(tt + 1) * T4],
                                        in_=ps_g[tt][:, :], func=AF.Silu)
                sch.op("scalar", msilu, reads=[ps_g_t[tt]], writes=[hX_t[X]])

                def mh(e, X=X, tt=tt):
                    return e.tensor_tensor(
                        out=hX[X][:, tt * T4:(tt + 1) * T4],
                        in0=hX[X][:, tt * T4:(tt + 1) * T4],
                        in1=ps_u[tt][:, :], op=ALU.mult)
                sch.op("vector", mh, reads=[hX_t[X], ps_u_t[tt]],
                       writes=[hX_t[X]])

        for tt in range(NT4):
            def sp(fn, reads, writes):
                sch.op("vector", fn, reads=reads, writes=writes)

            sp(lambda e, tt=tt: e.tensor_tensor(out=tmpA[:, :], in0=hX[0][:, tt * T4:(tt + 1) * T4],
                                         in1=hX[0][:, tt * T4:(tt + 1) * T4], op=ALU.mult),
               [hX_t[0]], [tmpA_t])
            sp(lambda e, tt=tt: e.tensor_tensor(out=tmpB[:, :], in0=hX[1][:, tt * T4:(tt + 1) * T4],
                                         in1=hX[1][:, tt * T4:(tt + 1) * T4], op=ALU.mult),
               [hX_t[1]], [tmpB_t])
            sp(lambda e, tt=tt: e.tensor_tensor(out=m1_sb[:, :], in0=tmpA[:, :],
                                         in1=tmpB[:, :], op=ALU.max),
               [tmpA_t, tmpB_t], [m1_t])
            sp(lambda e, tt=tt: e.tensor_tensor(out=n1_sb[:, :], in0=tmpA[:, :],
                                         in1=tmpB[:, :], op=ALU.min),
               [tmpA_t, tmpB_t], [n1_t])
            sp(lambda e, tt=tt: e.tensor_tensor(out=tmpA[:, :], in0=hX[2][:, tt * T4:(tt + 1) * T4],
                                         in1=hX[2][:, tt * T4:(tt + 1) * T4], op=ALU.mult),
               [hX_t[2]], [tmpA_t])
            sp(lambda e, tt=tt: e.tensor_tensor(out=tmpB[:, :], in0=hX[3][:, tt * T4:(tt + 1) * T4],
                                         in1=hX[3][:, tt * T4:(tt + 1) * T4], op=ALU.mult),
               [hX_t[3]], [tmpB_t])
            sp(lambda e, tt=tt: e.tensor_tensor(out=t2_sb[:, :], in0=tmpA[:, :],
                                         in1=tmpB[:, :], op=ALU.max),
               [tmpA_t, tmpB_t], [t2_t])
            sp(lambda e, tt=tt: e.tensor_tensor(out=tmpA[:, :], in0=tmpA[:, :],
                                         in1=tmpB[:, :], op=ALU.min),
               [tmpA_t, tmpB_t], [tmpA_t])
            sp(lambda e, tt=tt: e.tensor_tensor(out=m1_sb[:, :], in0=m1_sb[:, :],
                                         in1=t2_sb[:, :], op=ALU.min),
               [m1_t, t2_t], [m1_t])
            sp(lambda e, tt=tt: e.tensor_tensor(out=n1_sb[:, :], in0=n1_sb[:, :],
                                         in1=tmpA[:, :], op=ALU.max),
               [n1_t, tmpA_t], [n1_t])
            sp(lambda e, tt=tt: e.tensor_tensor(out=t2_sb[:, :], in0=m1_sb[:, :],
                                         in1=n1_sb[:, :], op=ALU.max),
               [m1_t, n1_t], [t2_t])

            for X in range(4):
                def f_sq(e, X=X, tt=tt):
                    return e.tensor_tensor(out=tmpB[:, :], in0=hX[X][:, tt * T4:(tt + 1) * T4],
                                           in1=hX[X][:, tt * T4:(tt + 1) * T4], op=ALU.mult)
                sch.op("vector", f_sq, reads=[hX_t[X]], writes=[tmpB_t])

                def f_keep(e, X=X, tt=tt):
                    return e.tensor_tensor(out=tmpB[:, :], in0=tmpB[:, :],
                                           in1=t2_sb[:, :], op=ALU.is_ge)
                sch.op("vector", f_keep, reads=[tmpB_t, t2_t],
                       writes=[tmpB_t])

                def f_app(e, X=X, tt=tt):
                    return e.tensor_tensor(out=hsp_sb[X][:, :],
                                           in0=hX[X][:, tt * T4:(tt + 1) * T4],
                                           in1=tmpB[:, :], op=ALU.mult)
                sch.op("vector", f_app, reads=[hX_t[X], tmpB_t],
                       writes=[hsp_t[X]])

                ck = X * QUADS + quad

                def f_st(e, X=X, ck=ck, tt=tt):
                    return e.dma_start(
                        out=hsp_dram.ap()[ck, :, tt * T4:(tt + 1) * T4],
                        in_=hsp_sb[X][:, :])
                sch.op("gpsimd", f_st, reads=[hsp_t[X]], writes=[hsp_dram_t],
                       dma=True)

    p.close()
    sch.barrier()

    p = ExitStack()
    TS5 = 256
    NTS = TQ // TS5
    hsp_c = [p.enter_context(nc.sbuf_tensor(f"hsp_c{s}", [128, NI * TS5],
                                            BF16)) for s in range(2)]
    wd_jt = p.enter_context(nc.sbuf_tensor("wd_jt", [128, NI * 512], BF16))
    x_tile = [p.enter_context(nc.sbuf_tensor(f"x_t{s}", [128, 512], F32))
              for s in range(2)]
    a_tile = [p.enter_context(nc.sbuf_tensor(f"a_t{s}", [128, 512], F32))
              for s in range(2)]
    o_tile = [p.enter_context(nc.sbuf_tensor(f"o_t{s}", [128, 512], F32))
              for s in range(2)]
    ps_dn = [p.enter_context(nc.psum_tensor(f"ps_dn{s}", [128, 512], F32))
             for s in range(2)]
    hsp_c_t = [TBuf(), TBuf()]
    wd_jt_t = TBuf()
    x_tile_t = [TBuf(), TBuf()]
    a_tile_t = [TBuf(), TBuf()]
    o_tile_t = [TBuf(), TBuf()]
    ps_dn_t = [TBuf(), TBuf()]
    out_t = TBuf()

    NJT5 = D // 512
    for jt in range(NJT5):
        def ldwj(e, jt=jt):
            return e.dma_start(
                out=wd_jt.ap().rearrange("p (c n) -> p c n", c=NI),
                in_=wdb_dram.ap()[:, :, jt * 512:(jt + 1) * 512]
                    .rearrange("c p n -> p c n"))
        sch.op("sync", ldwj, reads=[wdb_dram_t], writes=[wd_jt_t], dma=True)

        for tts in range(NTS):
            hsl = tts % 2

            def ldh(e, tts=tts, hsl=hsl):
                return e.dma_start(
                    out=hsp_c[hsl].ap().rearrange("p (c n) -> p c n", c=NI),
                    in_=hsp_dram.ap()[:, :, tts * TS5:(tts + 1) * TS5]
                        .rearrange("c p n -> p c n"))
            sch.op("sync", ldh, reads=[hsp_dram_t], writes=[hsp_c_t[hsl]],
                   dma=True)

            for tb in range(TS5 // 128):
                dp = tb % 2
                t0 = tts * TS5 + tb * 128

                def ldxt(e, t0=t0, jt=jt, dp=dp):
                    return e.dma_start(
                        out=x_tile[dp][:, :],
                        in_=xq_rows.ap()[t0:t0 + 128,
                                         jt * 512:(jt + 1) * 512])
                sch.op("sync", ldxt, writes=[x_tile_t[dp]], dma=True)

                def ldat(e, t0=t0, jt=jt, dp=dp):
                    return e.dma_start(
                        out=a_tile[dp][:, :],
                        in_=attn_dram.ap()[t0:t0 + 128,
                                           jt * 512:(jt + 1) * 512])
                sch.op("sync", ldat, reads=[attn_dram_t],
                       writes=[a_tile_t[dp]], dma=True)

                for ck in range(NI):
                    def mdn(e, ck=ck, tb=tb, dp=dp, hsl=hsl):
                        return e.matmul(
                            ps_dn[dp][:, :],
                            hsp_c[hsl][:, ck * TS5 + tb * 128:
                                       ck * TS5 + tb * 128 + 128],
                            wd_jt[:, ck * 512:(ck + 1) * 512],
                            start=(ck == 0), stop=(ck == NI - 1))
                    sch.op("tensor", mdn, reads=[hsp_c_t[hsl], wd_jt_t],
                           writes=[ps_dn_t[dp]])

                def madd1(e, dp=dp):
                    return e.tensor_tensor(out=o_tile[dp][:, :],
                                           in0=ps_dn[dp][:, :],
                                           in1=x_tile[dp][:, :], op=ALU.add)
                sch.op("vector", madd1, reads=[ps_dn_t[dp], x_tile_t[dp]],
                       writes=[o_tile_t[dp]])

                def madd2(e, dp=dp):
                    return e.tensor_tensor(out=o_tile[dp][:, :],
                                           in0=o_tile[dp][:, :],
                                           in1=a_tile[dp][:, :], op=ALU.add)
                sch.op("vector", madd2, reads=[o_tile_t[dp], a_tile_t[dp]],
                       writes=[o_tile_t[dp]])

                def stout(e, t0=t0, jt=jt, dp=dp):
                    return e.dma_start(
                        out=out_h.ap()[t0:t0 + 128, jt * 512:(jt + 1) * 512],
                        in_=o_tile[dp][:, :])
                sch.op("gpsimd", stout, reads=[o_tile_t[dp]], writes=[out_t],
                       dma=True)

    last_out = sch.last_by_eng["gpsimd"]

    def fence(e):
        return e.engine_nop()
    sch.op("gpsimd", fence, deps=[last_out])

    p.close()

    sch.emit(nc, ctx)
    return nc



_cached = {}


def _get_program(S, D, H, I):
    key = (S, D, H, I)
    if key not in _cached:
        _cached[key] = build_program(S, D, H, I)
    return _cached[key]


def run(x, wq, wk, wv, wo, wg, wu, wd):
    B, S, D = x.shape
    H = D // DH
    I = wg.shape[1]
    DC = D // 128
    NI = I // 128
    TQ = S // R
    QUADS = I // 512
    CPW = (R * WQ) // 128

    Xs = np.arange(I) // (I // 4)
    ms = np.arange(I) % (I // 4)
    perm = 4 * ms + Xs
    wg_p = np.ascontiguousarray(wg[:, perm])
    wu_p = np.ascontiguousarray(wu[:, perm])
    wd_p = np.ascontiguousarray(wd[perm, :])

    def mk_wh(w_p):
        t = w_p.reshape(DC, 128, 4, QUADS, 128)
        t = t.transpose(2, 3, 1, 0, 4)
        return np.ascontiguousarray(t.reshape(NI, 128, DC * 128))

    wgh = mk_wh(wg_p)
    wuh = mk_wh(wu_p)
    wdh = np.ascontiguousarray(wd_p.reshape(4, QUADS, 128, D)
                               .reshape(NI, 128, D))

    in_maps = []
    xT_cache = {}
    for c in range(8):
        b, l = c // 4, c % 4
        if b not in xT_cache:
            xT_cache[b] = np.ascontiguousarray(x[b].T)
        xq = np.ascontiguousarray(x[b, l::R, :])
        xqT = np.ascontiguousarray(xq.T)
        j = np.arange(CPW)[:, None, None]
        k = np.arange(128)[None, :, None]
        q = np.arange(WQ)[None, None, :]
        masks = ((128 * j + k) <= (R * q + l)).astype(np.float32)
        in_maps.append({
            "xT": xT_cache[b], "xqT": xqT, "xq_rows": xq,
            "wq": wq, "wk": wk, "wv": wv, "wo": wo,
            "wgh": wgh, "wuh": wuh, "wdh": wdh,
            "masks": np.ascontiguousarray(masks),
        })

    nc = _get_program(S, D, H, I)
    res = run_bass_kernel_spmd(nc, in_maps, core_ids=list(range(8)))

    out = np.empty((B, S, D), dtype=np.float32)
    for c in range(8):
        b, l = c // 4, c % 4
        out[b, l::R, :] = res.results[c]["out"]
    return out


def kernel(L0_attn_x, L0_attn_w_q, L0_attn_w_k, L0_attn_w_v, L0_attn_w_o,
           L0_ffn_w_gate, L0_ffn_w_up, L0_ffn_w_down):
    return run(np.asarray(L0_attn_x, dtype=np.float32),
               np.asarray(L0_attn_w_q, dtype=np.float32),
               np.asarray(L0_attn_w_k, dtype=np.float32),
               np.asarray(L0_attn_w_v, dtype=np.float32),
               np.asarray(L0_attn_w_o, dtype=np.float32),
               np.asarray(L0_ffn_w_gate, dtype=np.float32),
               np.asarray(L0_ffn_w_up, dtype=np.float32),
               np.asarray(L0_ffn_w_down, dtype=np.float32))


# version 11 (final)
# speedup vs baseline: 1.2389x; 1.0918x over previous
import numpy as np
import ml_dtypes
from contextlib import ExitStack

import concourse.bass as bass
import concourse.mybir as mybir
from concourse.bass_utils import run_bass_kernel_spmd

F32 = mybir.dt.float32
BF16 = mybir.dt.bfloat16
AF = mybir.ActivationFunctionType
ALU = mybir.AluOpType

S_FULL = 8192
D_FULL = 2048
H_FULL = 16
I_FULL = 8192
R = 4
WQ = 512
DH = 128

BF16NP = ml_dtypes.bfloat16



class Opr:
    __slots__ = ("eng", "fn", "dma", "deps", "needs_inc", "inc_val", "dma_val",
                 "tag")

    def __init__(self, eng, fn, dma, tag=""):
        self.eng = eng
        self.fn = fn
        self.dma = dma
        self.deps = []
        self.needs_inc = False
        self.inc_val = None
        self.dma_val = None
        self.tag = tag


class TBuf:
    __slots__ = ("last_w", "readers", "name")

    def __init__(self, name=""):
        self.last_w = None
        self.readers = []
        self.name = name

    def r(self, op):
        if self.last_w is not None:
            op.deps.append(self.last_w)
        self.readers.append(op)

    def w(self, op):
        if self.last_w is not None:
            op.deps.append(self.last_w)
        op.deps.extend(self.readers)
        self.last_w = op
        self.readers = []


class Sched:
    ENGINES = ("tensor", "scalar", "vector", "gpsimd", "sync")

    def __init__(self):
        self.ops = []
        self.last_by_eng = {}
        self.pending_barrier = None

    SERIALIZE = False

    def op(self, eng, fn, reads=(), writes=(), dma=False, deps=(), tag=""):
        o = Opr(eng, fn, dma, tag)
        if Sched.SERIALIZE and self.ops:
            o.deps.append(self.ops[-1])
        if self.pending_barrier is not None:
            o.deps.extend(self.pending_barrier.get("__all__", []))
            if not self.pending_barrier.get(eng, False):
                self.pending_barrier[eng] = True
            if all(self.pending_barrier.get(e, e not in self.last_by_eng)
                   for e in self.ENGINES):
                self.pending_barrier = None
        o.deps.extend(deps)
        for b in reads:
            b.r(o)
        for b in writes:
            b.w(o)
        self.ops.append(o)
        self.last_by_eng[eng] = o
        return o

    def barrier(self):
        lasts = [op for op in self.last_by_eng.values()]
        self.pending_barrier = {"__all__": lasts}

    NQSEM = 8

    def emit(self, nc, ctx):
        R = Sched.NQSEM
        qcount = {}
        for o in self.ops:
            if o.dma:
                n = qcount.get(o.eng, 0)
                o.dma_val = (n % R, 16 * (n // R + 1))
                qcount[o.eng] = n + 1
        for o in self.ops:
            for d in o.deps:
                if (not d.dma) and d.eng != o.eng:
                    d.needs_inc = True
        ecount = {}
        for o in self.ops:
            if (not o.dma) and o.needs_inc:
                ecount[o.eng] = ecount.get(o.eng, 0) + 1
                o.inc_val = ecount[o.eng]

        esem = {}
        qsem = {}
        for e in self.ENGINES:
            if ecount.get(e):
                esem[e] = ctx.enter_context(nc.semaphore(f"se_{e}"))
            if qcount.get(e):
                qsem[e] = [ctx.enter_context(nc.semaphore(f"sq_{e}{i}"))
                           for i in range(R)]

        by_eng = {e: [o for o in self.ops if o.eng == e] for e in self.ENGINES}

        def emit_engine(handle, ename):
            seen = {}
            for o in by_eng[ename]:
                waits = {}
                if o.dma:
                    sub, val = o.dma_val
                    if val > 16:
                        waits[("q", ename, sub)] = val - 16
                for d in o.deps:
                    if d.dma:
                        sub, val = d.dma_val
                        key = ("q", d.eng, sub)
                    else:
                        if d.eng == ename:
                            continue
                        if d.inc_val is None:
                            continue
                        key = ("e", d.eng, 0)
                        val = d.inc_val
                    if waits.get(key, 0) < val:
                        waits[key] = val
                for (kind, de, sub), val in sorted(waits.items()):
                    sem = qsem[de][sub] if kind == "q" else esem[de]
                    if seen.get((kind, de, sub), 0) < val:
                        handle.wait_ge(sem, val)
                        seen[(kind, de, sub)] = val
                ins = o.fn(handle)
                if o.dma:
                    ins.then_inc(qsem[ename][o.dma_val[0]], 16)
                elif o.needs_inc:
                    ins.then_inc(esem[ename], 1)

        with nc.Block() as block:
            if by_eng["sync"]:
                @block.sync
                def _(e):
                    emit_engine(e, "sync")
            if by_eng["tensor"]:
                @block.tensor
                def _(e):
                    emit_engine(e, "tensor")
            if by_eng["scalar"]:
                @block.scalar
                def _(e):
                    emit_engine(e, "scalar")
            if by_eng["vector"]:
                @block.vector
                def _(e):
                    emit_engine(e, "vector")
            if by_eng["gpsimd"]:
                @block.gpsimd
                def _(e):
                    emit_engine(e, "gpsimd")



def build_program(S, D, H, I):
    DC = D // 128
    TQ = S // R
    NW = TQ // WQ
    CPW = (R * WQ) // 128
    NT = S // 512
    NQT = TQ // 512
    NI = I // 128
    QUADS = I // 512
    scale = 1.0 / np.sqrt(DH)

    nc = bass.Bass("TRN2", target_bir_lowering=False)
    ctx = ExitStack()

    xT = nc.dram_tensor("xT", [D, S], F32, kind="ExternalInput")
    xqT = nc.dram_tensor("xqT", [D, TQ], F32, kind="ExternalInput")
    xq_rows = nc.dram_tensor("xq_rows", [TQ, D], F32, kind="ExternalInput")
    wq_h = nc.dram_tensor("wq", [D, D], F32, kind="ExternalInput")
    wk_h = nc.dram_tensor("wk", [D, D], F32, kind="ExternalInput")
    wv_h = nc.dram_tensor("wv", [D, D], F32, kind="ExternalInput")
    wo_h = nc.dram_tensor("wo", [D, D], F32, kind="ExternalInput")
    wgh = nc.dram_tensor("wgh", [NI, 128, D], F32, kind="ExternalInput")
    wuh = nc.dram_tensor("wuh", [NI, 128, D], F32, kind="ExternalInput")
    wdh = nc.dram_tensor("wdh", [NI, 128, D], F32, kind="ExternalInput")
    masks_h = nc.dram_tensor("masks", [CPW, 128, WQ], F32, kind="ExternalInput")

    out_h = nc.dram_tensor("out", [TQ, D], F32, kind="ExternalOutput")

    xn_dram = nc.dram_tensor("xn_dram", [DC, 128, S], BF16, kind="Internal")
    k_dram = nc.dram_tensor("k_dram", [H, 128, S], BF16, kind="Internal")
    v_dram = nc.dram_tensor("v_dram", [S // 128, 128, D], BF16, kind="Internal")
    q_dram = nc.dram_tensor("q_dram", [H, 128, TQ], BF16, kind="Internal")
    xnq_dram = nc.dram_tensor("xnq_dram", [DC, 128, TQ], BF16, kind="Internal")
    attn_dram = nc.dram_tensor("attn_dram", [TQ, D], F32, kind="Internal")
    hsp_dram = nc.dram_tensor("hsp_dram", [NI, 128, TQ], BF16, kind="Internal")
    wdb_dram = nc.dram_tensor("wdb_dram", [NI, 128, D], BF16, kind="Internal")

    sch = Sched()

    g = ExitStack()
    ones_b = g.enter_context(nc.sbuf_tensor("ones_b", [128, 2], BF16))
    ones_f = g.enter_context(nc.sbuf_tensor("ones_f", [1, 128], F32))
    ones_b_t = TBuf("ones_b")
    ones_f_t = TBuf("ones_f")
    sch.op("vector", lambda e: e.memset(ones_b[:, :], 1.0), writes=[ones_b_t])
    sch.op("vector", lambda e: e.memset(ones_f[:, :], 1.0), writes=[ones_f_t])
    eps_sb = g.enter_context(nc.sbuf_tensor("eps_sb", [1, 2], F32))
    eps_t = TBuf("eps")
    sch.op("vector", lambda e: e.memset(eps_sb[:, :], 1e-5), writes=[eps_t])

    kmax = g.enter_context(nc.sbuf_tensor("kmax", [1, max(H, 2)], F32))
    qmax = g.enter_context(nc.sbuf_tensor("qmax", [1, max(H, 2)], F32))
    kmax_t = [TBuf(f"kmax{h}") for h in range(H)]
    qmax_t = [TBuf(f"qmax{h}") for h in range(H)]
    sch.op("vector", lambda e: e.memset(kmax[:, :], 0.0), writes=kmax_t)
    sch.op("vector", lambda e: e.memset(qmax[:, :], 0.0), writes=qmax_t)

    p = ExitStack()
    xb = [p.enter_context(nc.sbuf_tensor(f"xb{s}", [128, DC * 512], BF16))
          for s in range(2)]
    x2b = [p.enter_context(nc.sbuf_tensor(f"x2b{s}", [128, 512], BF16))
           for s in range(2)]
    xn_sb = [p.enter_context(nc.sbuf_tensor(f"xn{s}", [128, DC * 512], BF16))
             for s in range(2)]
    wkb = p.enter_context(nc.sbuf_tensor("wkb", [128, DC * D], BF16))
    wst = p.enter_context(nc.sbuf_tensor("wst", [128, 2 * D], F32))
    kst = [p.enter_context(nc.sbuf_tensor(f"kst{s}", [128, 512], BF16))
           for s in range(2)]
    k2b = [p.enter_context(nc.sbuf_tensor(f"k2b{s}", [128, 512], BF16))
           for s in range(2)]
    ab_s = [p.enter_context(nc.sbuf_tensor(f"ab{s}", [1, 3 * 512], F32))
            for s in range(1)]
    ps_st = p.enter_context(nc.psum_tensor("ps_st", [1, 1024], F32))
    ps_ab = p.enter_context(nc.psum_tensor("ps_ab", [128, 1024], F32))
    ps_k = [p.enter_context(nc.psum_tensor(f"ps_k{s}", [128, 512], F32))
            for s in range(2)]
    ps_n = p.enter_context(nc.psum_tensor("ps_n", [1, 512], F32))

    xb_t = [TBuf() for _ in range(2)]
    x2b_t = [TBuf() for _ in range(2)]
    xn_t = [TBuf() for _ in range(2)]
    wkb_t = TBuf("wkb")
    kst_t = [TBuf(), TBuf()]
    k2b_t = [TBuf() for _ in range(2)]
    ab_t = [TBuf() for _ in range(1)]
    ps_st_t = TBuf("ps_st")
    ps_ab_t = TBuf("ps_ab")
    ps_k_t = [TBuf() for _ in range(2)]
    ps_n_t = TBuf("ps_n")
    xn_dram_t = TBuf("xn_dram")
    k_dram_t = TBuf("k_dram")

    wstage = [wst[:, 0:D], wst[:, D:2 * D]]
    wstage_t = [TBuf(), TBuf()]

    def load_weight_cast(w_h, dst_sb, dst_t):
        for cch in range(DC):
            sl = cch % 2

            def ldf(e, cch=cch, sl=sl):
                return e.dma_start(
                    out=wstage[sl],
                    in_=w_h.ap()[cch * 128:(cch + 1) * 128, :])
            o = sch.op("sync", ldf, writes=[wstage_t[sl]], dma=True)

            def cst(e, cch=cch, sl=sl):
                return e.tensor_copy(
                    out=dst_sb[:, cch * D:(cch + 1) * D], in_=wstage[sl])
            sch.op("vector", cst, reads=[wstage_t[sl]], writes=[dst_t])

    load_weight_cast(wk_h, wkb, wkb_t)

    inv_d = 1.0 / D

    def emit_front1(tt):
        sl = tt % 2
        BCH = DC // 4
        for ph in range(4):
            xsl = (tt * 4 + ph) % 2

            def ldx(e, tt=tt, ph=ph, xsl=xsl, BCH=BCH):
                return e.dma_start(
                    out=wst.ap()[:, xsl * D: xsl * D + BCH * 512]
                        .rearrange("p (c n) -> p c n", c=BCH),
                    in_=xT.ap()[ph * BCH * 128:(ph + 1) * BCH * 128,
                                tt * 512:(tt + 1) * 512]
                        .rearrange("(c p) n -> p c n", p=128))
            sch.op("sync", ldx, writes=[wstage_t[xsl]], dma=True)

            for i in range(BCH):
                cch = ph * BCH + i

                def cstx(e, cch=cch, i=i, xsl=xsl, sl=sl):
                    return e.copy(
                        out=xb[sl][:, cch * 512:(cch + 1) * 512],
                        in_=wst[:, xsl * D + i * 512: xsl * D + i * 512 + 512])
                sch.op("scalar", cstx, reads=[wstage_t[xsl]],
                       writes=[xb_t[sl]])

        for cch in range(DC):
            def sq(e, cch=cch, sl=sl):
                return e.activation(
                    out=x2b[sl][:, :],
                    in_=xb[sl][:, cch * 512:(cch + 1) * 512], func=AF.Square)
            sch.op("scalar", sq, reads=[xb_t[sl]], writes=[x2b_t[sl]])

            def mm_s(e, cch=cch, sl=sl):
                return e.matmul(ps_st[:, 0:512], ones_b[:, 0:1],
                                xb[sl][:, cch * 512:(cch + 1) * 512],
                                start=(cch == 0), stop=(cch == DC - 1))
            sch.op("tensor", mm_s, reads=[xb_t[sl], ones_b_t],
                   writes=[ps_st_t])

            def mm_q(e, cch=cch, sl=sl):
                return e.matmul(ps_st[:, 512:1024], ones_b[:, 0:1],
                                x2b[sl][:, :],
                                start=(cch == 0), stop=(cch == DC - 1))
            sch.op("tensor", mm_q, reads=[x2b_t[sl], ones_b_t],
                   writes=[ps_st_t])


    def emit_front2(tt):
        sl = tt % 2
        asl = 0
        m_v = ab_s[asl][:, 0:512]
        A_v = ab_s[asl][:, 512:1024]
        B_v = ab_s[asl][:, 1024:1536]

        def mmean(e, asl=asl):
            return e.mul(out=ab_s[asl][:, 0:512], in_=ps_st[:, 0:512], mul=inv_d)
        sch.op("scalar", mmean, reads=[ps_st_t], writes=[ab_t[asl]])

        def m2f(e, asl=asl):
            return e.tensor_tensor(out=ab_s[asl][:, 1024:1536],
                                   in0=ab_s[asl][:, 0:512],
                                   in1=ab_s[asl][:, 0:512], op=ALU.mult)
        sch.op("vector", m2f, reads=[ab_t[asl]], writes=[ab_t[asl]])

        def mvar2(e, asl=asl):
            return e.scalar_tensor_tensor(
                out=ab_s[asl][:, 512:1024], in0=ps_st[:, 512:1024],
                scalar=inv_d, in1=ab_s[asl][:, 1024:1536],
                op0=ALU.mult, op1=ALU.subtract)
        sch.op("vector", mvar2, reads=[ps_st_t, ab_t[asl]], writes=[ab_t[asl]])

        def lnv(e, asl=asl):
            return e.activation(out=ab_s[asl][:, 512:1024],
                                in_=ab_s[asl][:, 512:1024],
                                func=AF.Ln, bias=eps_sb[:, 0:1])
        sch.op("scalar", lnv, reads=[ab_t[asl], eps_t], writes=[ab_t[asl]])

        def expv(e, asl=asl):
            return e.activation(out=ab_s[asl][:, 512:1024],
                                in_=ab_s[asl][:, 512:1024],
                                func=AF.Exp, scale=-0.5)
        sch.op("scalar", expv, reads=[ab_t[asl]], writes=[ab_t[asl]])

        def bfn(e, asl=asl):
            return e.scalar_tensor_tensor(
                out=ab_s[asl][:, 1024:1536], in0=ab_s[asl][:, 0:512],
                scalar=-1.0, in1=ab_s[asl][:, 512:1024],
                op0=ALU.mult, op1=ALU.mult)
        sch.op("vector", bfn, reads=[ab_t[asl]], writes=[ab_t[asl]])

        def bca(e, asl=asl):
            return e.matmul(ps_ab[:, 0:512], ones_f[:, :],
                            ab_s[asl][:, 512:1024], start=True, stop=True)
        sch.op("tensor", bca, reads=[ab_t[asl], ones_f_t], writes=[ps_ab_t])

        def bcb(e, asl=asl):
            return e.matmul(ps_ab[:, 512:1024], ones_f[:, :],
                            ab_s[asl][:, 1024:1536], start=True, stop=True)
        sch.op("tensor", bcb, reads=[ab_t[asl], ones_f_t], writes=[ps_ab_t])

        for cch in range(DC):
            def xnm(e, cch=cch, sl=sl):
                return e.tensor_tensor(
                    out=xn_sb[sl][:, cch * 512:(cch + 1) * 512],
                    in0=xb[sl][:, cch * 512:(cch + 1) * 512],
                    in1=ps_ab[:, 0:512], op=ALU.mult)
            sch.op("vector", xnm, reads=[xb_t[sl], ps_ab_t], writes=[xn_t[sl]])

            def xna(e, cch=cch, sl=sl):
                return e.tensor_tensor(
                    out=xn_sb[sl][:, cch * 512:(cch + 1) * 512],
                    in0=xn_sb[sl][:, cch * 512:(cch + 1) * 512],
                    in1=ps_ab[:, 512:1024], op=ALU.add)
            sch.op("vector", xna, reads=[xn_t[sl], ps_ab_t], writes=[xn_t[sl]])

        def xnw(e, tt=tt, sl=sl):
            return e.dma_start(
                out=xn_dram.ap()[:, :, tt * 512:(tt + 1) * 512]
                    .rearrange("c p n -> p c n"),
                in_=xn_sb[sl].ap().rearrange("p (c n) -> p c n", c=DC))
        sch.op("gpsimd", xnw, reads=[xn_t[sl]], writes=[xn_dram_t], dma=True)


    def emit_K1a(tt):
        sl = tt % 2
        for jc in range(H):
            kp = jc % 2

            for cch in range(DC):
                def mk(e, jc=jc, sl=sl, kp=kp, cch=cch):
                    return e.matmul(
                        ps_k[kp][:, :],
                        wkb[:, cch * D + jc * 128: cch * D + jc * 128 + 128],
                        xn_sb[sl][:, cch * 512:(cch + 1) * 512],
                        start=(cch == 0), stop=(cch == DC - 1))
                sch.op("tensor", mk, reads=[wkb_t, xn_t[sl]],
                       writes=[ps_k_t[kp]])

            def evk(e, jc=jc, kp=kp):
                return e.copy(out=kst[kp][:, :], in_=ps_k[kp][:, :])
            sch.op("scalar", evk, reads=[ps_k_t[kp]], writes=[kst_t[kp]])

            def kwh(e, jc=jc, kp=kp, tt=tt):
                return e.dma_start(
                    out=k_dram.ap()[jc, :, tt * 512:(tt + 1) * 512],
                    in_=kst[kp][:, :])
            sch.op("gpsimd", kwh, reads=[kst_t[kp]], writes=[k_dram_t],
                   dma=True)

            def k2f(e, jc=jc, kp=kp):
                return e.tensor_tensor(out=k2b[kp][:, :],
                                       in0=kst[kp][:, :],
                                       in1=kst[kp][:, :],
                                       op=ALU.mult)
            sch.op("vector", k2f, reads=[kst_t[kp]], writes=[k2b_t[kp]])

            def nmm(e, jc=jc, kp=kp):
                return e.matmul(ps_n[:, :], ones_b[:, 0:1], k2b[kp][:, :],
                                start=True, stop=True)
            sch.op("tensor", nmm, reads=[k2b_t[kp], ones_b_t], writes=[ps_n_t])

            def nred2(e, jc=jc):
                return e.tensor_reduce(out=ab_s[0][:, 0:1], in_=ps_n[:, :],
                                       axis=mybir.AxisListType.X, op=ALU.max)
            sch.op("vector", nred2, reads=[ps_n_t], writes=[ab_t[0]], tag="kn")

            def nacc(e, jc=jc):
                return e.tensor_tensor(out=kmax[:, jc:jc + 1],
                                       in0=kmax[:, jc:jc + 1],
                                       in1=ab_s[0][:, 0:1], op=ALU.max)
            sch.op("vector", nacc, reads=[ab_t[0], kmax_t[jc]],
                   writes=[kmax_t[jc]])



    emit_front1(0)
    emit_front2(0)
    for tt in range(NT):
        if tt + 1 < NT:
            emit_front1(tt + 1)
        emit_K1a(tt)
        if tt + 1 < NT:
            emit_front2(tt + 1)

    p.close()
    sch.barrier()

    p = ExitStack()
    xnv = [p.enter_context(nc.sbuf_tensor(f"xnv{s}", [128, DC * 512], BF16))
           for s in range(2)]
    wvb = p.enter_context(nc.sbuf_tensor("wvb", [128, DC * D], BF16))
    wst2 = p.enter_context(nc.sbuf_tensor("wst2", [128, 2 * D], F32))
    vst = p.enter_context(nc.sbuf_tensor("vst", [128, 4 * D], BF16))
    ps_v = [p.enter_context(nc.psum_tensor(f"ps_v{s}", [128, 512], F32))
            for s in range(2)]

    xnv_t = [TBuf() for _ in range(2)]
    wvb_t = TBuf("wvb")
    vst_t = TBuf("vst")
    ps_v_t = [TBuf() for _ in range(2)]
    wst2_t = [TBuf(), TBuf()]
    v_dram_t = TBuf("v_dram")

    for cch in range(DC):
        sl2 = cch % 2

        def ldw2(e, cch=cch, sl2=sl2):
            return e.dma_start(out=wst2[:, sl2 * D:(sl2 + 1) * D],
                               in_=wv_h.ap()[cch * 128:(cch + 1) * 128, :])
        sch.op("sync", ldw2, writes=[wst2_t[sl2]], dma=True)

        def cstw2(e, cch=cch, sl2=sl2):
            return e.tensor_copy(out=wvb[:, cch * D:(cch + 1) * D],
                                 in_=wst2[:, sl2 * D:(sl2 + 1) * D])
        sch.op("vector", cstw2, reads=[wst2_t[sl2]], writes=[wvb_t])

    NJ = D // 512
    for tt in range(NT):
        sl = tt % 2

        def ldxn(e, tt=tt, sl=sl):
            return e.dma_start(
                out=xnv[sl].ap().rearrange("p (c n) -> p c n", c=DC),
                in_=xn_dram.ap()[:, :, tt * 512:(tt + 1) * 512]
                    .rearrange("c p n -> p c n"))
        sch.op("sync", ldxn, reads=[xn_dram_t], writes=[xnv_t[sl]], dma=True)

        for tb in range(4):
            for js in range(NJ):
                vp = (tb * NJ + js) % 2
                for cch in range(DC):
                    def mv(e, tt=tt, sl=sl, tb=tb, js=js, vp=vp, cch=cch):
                        return e.matmul(
                            ps_v[vp][:, :],
                            xnv[sl][:, cch * 512 + tb * 128:
                                    cch * 512 + tb * 128 + 128],
                            wvb[:, cch * D + js * 512: cch * D + js * 512 + 512],
                            start=(cch == 0), stop=(cch == DC - 1))
                    sch.op("tensor", mv, reads=[xnv_t[sl], wvb_t],
                           writes=[ps_v_t[vp]])

                def evv(e, tb=tb, js=js, vp=vp):
                    return e.copy(out=vst[:, tb * D + js * 512:
                                          tb * D + js * 512 + 512],
                                  in_=ps_v[vp][:, :])
                sch.op("scalar", evv, reads=[ps_v_t[vp]], writes=[vst_t])

        def vw(e, tt=tt):
            return e.dma_start(
                out=v_dram.ap()[tt * 4:(tt + 1) * 4, :, :]
                    .rearrange("c p n -> p c n"),
                in_=vst.ap().rearrange("p (c n) -> p c n", c=4))
        sch.op("gpsimd", vw, reads=[vst_t], writes=[v_dram_t], dma=True)

    p.close()
    sch.barrier()

    p = ExitStack()
    xbq = [p.enter_context(nc.sbuf_tensor(f"xbq{s}", [128, DC * 512], BF16))
           for s in range(2)]
    x2q = [p.enter_context(nc.sbuf_tensor(f"x2q{s}", [128, 512], BF16))
           for s in range(2)]
    xnq_sb = [p.enter_context(nc.sbuf_tensor(f"xnq{s}", [128, DC * 512], BF16))
              for s in range(2)]
    wqb = p.enter_context(nc.sbuf_tensor("wqb", [128, DC * D], BF16))
    wstq = p.enter_context(nc.sbuf_tensor("wstq", [128, 2 * D], F32))
    qst = [p.enter_context(nc.sbuf_tensor(f"qst{s}", [128, 512], BF16))
           for s in range(2)]
    q2b = [p.enter_context(nc.sbuf_tensor(f"q2b{s}", [128, 512], BF16))
           for s in range(2)]
    abq = [p.enter_context(nc.sbuf_tensor(f"abq{s}", [1, 3 * 512], F32))
           for s in range(1)]
    ps_st2 = p.enter_context(nc.psum_tensor("ps_st2", [1, 1024], F32))
    ps_ab2 = p.enter_context(nc.psum_tensor("ps_ab2", [128, 1024], F32))
    ps_q = [p.enter_context(nc.psum_tensor(f"ps_q{s}", [128, 512], F32))
            for s in range(2)]
    ps_n2 = p.enter_context(nc.psum_tensor("ps_n2", [1, 512], F32))

    xbq_t = [TBuf() for _ in range(2)]
    x2q_t = [TBuf() for _ in range(2)]
    xnq_t = [TBuf() for _ in range(2)]
    wqb_t = TBuf()
    wstq_t = [TBuf(), TBuf(), TBuf(), TBuf()]
    qst_t = [TBuf(), TBuf()]
    q2b_t = [TBuf() for _ in range(2)]
    abq_t = [TBuf() for _ in range(1)]
    ps_st2_t = TBuf()
    ps_ab2_t = TBuf()
    ps_q_t = [TBuf() for _ in range(2)]
    ps_n2_t = TBuf()
    q_dram_t = TBuf()
    xnq_dram_t = TBuf()

    for cch in range(DC):
        sl2 = cch % 2

        def ldwq(e, cch=cch, sl2=sl2):
            return e.dma_start(out=wstq[:, sl2 * D:(sl2 + 1) * D],
                               in_=wq_h.ap()[cch * 128:(cch + 1) * 128, :])
        sch.op("sync", ldwq, writes=[wstq_t[sl2]], dma=True)

        def cstwq(e, cch=cch, sl2=sl2):
            return e.tensor_copy(out=wqb[:, cch * D:(cch + 1) * D],
                                 in_=wstq[:, sl2 * D:(sl2 + 1) * D])
        sch.op("vector", cstwq, reads=[wstq_t[sl2]], writes=[wqb_t])

    for tt in range(NQT):
        sl = tt % 2
        BCH = DC // 4
        for ph in range(4):
            xsl = (tt * 4 + ph) % 2

            def ldxq(e, tt=tt, ph=ph, xsl=xsl, BCH=BCH):
                return e.dma_start(
                    out=wstq.ap()[:, xsl * D: xsl * D + BCH * 512]
                        .rearrange("p (c n) -> p c n", c=BCH),
                    in_=xqT.ap()[ph * BCH * 128:(ph + 1) * BCH * 128,
                                 tt * 512:(tt + 1) * 512]
                        .rearrange("(c p) n -> p c n", p=128))
            sch.op("sync", ldxq, writes=[wstq_t[xsl]], dma=True)

            for i in range(BCH):
                cch = ph * BCH + i

                def cstxq(e, cch=cch, i=i, xsl=xsl, sl=sl):
                    return e.copy(
                        out=xbq[sl][:, cch * 512:(cch + 1) * 512],
                        in_=wstq[:, xsl * D + i * 512:
                                 xsl * D + i * 512 + 512])
                sch.op("scalar", cstxq, reads=[wstq_t[xsl]],
                       writes=[xbq_t[sl]])

        for cch in range(DC):
            def sq2(e, cch=cch, sl=sl):
                return e.activation(
                    out=x2q[sl][:, :],
                    in_=xbq[sl][:, cch * 512:(cch + 1) * 512], func=AF.Square)
            sch.op("scalar", sq2, reads=[xbq_t[sl]], writes=[x2q_t[sl]])

            def mm_s2(e, cch=cch, sl=sl):
                return e.matmul(ps_st2[:, 0:512], ones_b[:, 0:1],
                                xbq[sl][:, cch * 512:(cch + 1) * 512],
                                start=(cch == 0), stop=(cch == DC - 1))
            sch.op("tensor", mm_s2, reads=[xbq_t[sl], ones_b_t],
                   writes=[ps_st2_t])

            def mm_q2(e, cch=cch, sl=sl):
                return e.matmul(ps_st2[:, 512:1024], ones_b[:, 0:1],
                                x2q[sl][:, :],
                                start=(cch == 0), stop=(cch == DC - 1))
            sch.op("tensor", mm_q2, reads=[x2q_t[sl], ones_b_t],
                   writes=[ps_st2_t])

        asl = 0

        def mmean2(e, asl=asl):
            return e.mul(out=abq[asl][:, 0:512], in_=ps_st2[:, 0:512],
                         mul=inv_d)
        sch.op("scalar", mmean2, reads=[ps_st2_t], writes=[abq_t[asl]])

        def m2f2(e, asl=asl):
            return e.tensor_tensor(out=abq[asl][:, 1024:1536],
                                   in0=abq[asl][:, 0:512],
                                   in1=abq[asl][:, 0:512], op=ALU.mult)
        sch.op("vector", m2f2, reads=[abq_t[asl]], writes=[abq_t[asl]])

        def mvar22(e, asl=asl):
            return e.scalar_tensor_tensor(
                out=abq[asl][:, 512:1024], in0=ps_st2[:, 512:1024],
                scalar=inv_d, in1=abq[asl][:, 1024:1536],
                op0=ALU.mult, op1=ALU.subtract)
        sch.op("vector", mvar22, reads=[ps_st2_t, abq_t[asl]],
               writes=[abq_t[asl]])

        def lnv2(e, asl=asl):
            return e.activation(out=abq[asl][:, 512:1024],
                                in_=abq[asl][:, 512:1024], func=AF.Ln,
                                bias=eps_sb[:, 0:1])
        sch.op("scalar", lnv2, reads=[abq_t[asl], eps_t], writes=[abq_t[asl]])

        def expv2(e, asl=asl):
            return e.activation(out=abq[asl][:, 512:1024],
                                in_=abq[asl][:, 512:1024], func=AF.Exp,
                                scale=-0.5)
        sch.op("scalar", expv2, reads=[abq_t[asl]], writes=[abq_t[asl]])

        def bfn2(e, asl=asl):
            return e.scalar_tensor_tensor(
                out=abq[asl][:, 1024:1536], in0=abq[asl][:, 0:512],
                scalar=-1.0, in1=abq[asl][:, 512:1024],
                op0=ALU.mult, op1=ALU.mult)
        sch.op("vector", bfn2, reads=[abq_t[asl]], writes=[abq_t[asl]])

        def bca2(e, asl=asl):
            return e.matmul(ps_ab2[:, 0:512], ones_f[:, :],
                            abq[asl][:, 512:1024], start=True, stop=True)
        sch.op("tensor", bca2, reads=[abq_t[asl], ones_f_t], writes=[ps_ab2_t])

        def bcb2(e, asl=asl):
            return e.matmul(ps_ab2[:, 512:1024], ones_f[:, :],
                            abq[asl][:, 1024:1536], start=True, stop=True)
        sch.op("tensor", bcb2, reads=[abq_t[asl], ones_f_t], writes=[ps_ab2_t])

        for cch in range(DC):
            def xnm2(e, cch=cch, sl=sl):
                return e.tensor_tensor(
                    out=xnq_sb[sl][:, cch * 512:(cch + 1) * 512],
                    in0=xbq[sl][:, cch * 512:(cch + 1) * 512],
                    in1=ps_ab2[:, 0:512], op=ALU.mult)
            sch.op("vector", xnm2, reads=[xbq_t[sl], ps_ab2_t],
                   writes=[xnq_t[sl]])

            def xna2(e, cch=cch, sl=sl):
                return e.tensor_tensor(
                    out=xnq_sb[sl][:, cch * 512:(cch + 1) * 512],
                    in0=xnq_sb[sl][:, cch * 512:(cch + 1) * 512],
                    in1=ps_ab2[:, 512:1024], op=ALU.add)
            sch.op("vector", xna2, reads=[xnq_t[sl], ps_ab2_t],
                   writes=[xnq_t[sl]])

        def xnqw(e, tt=tt, sl=sl):
            return e.dma_start(
                out=xnq_dram.ap()[:, :, tt * 512:(tt + 1) * 512]
                    .rearrange("c p n -> p c n"),
                in_=xnq_sb[sl].ap().rearrange("p (c n) -> p c n", c=DC))
        sch.op("gpsimd", xnqw, reads=[xnq_t[sl]], writes=[xnq_dram_t],
               dma=True)

        for jc in range(H):
            qp = jc % 2
            for cch in range(DC):
                def mq(e, jc=jc, sl=sl, qp=qp, cch=cch):
                    return e.matmul(
                        ps_q[qp][:, :],
                        wqb[:, cch * D + jc * 128: cch * D + jc * 128 + 128],
                        xnq_sb[sl][:, cch * 512:(cch + 1) * 512],
                        start=(cch == 0), stop=(cch == DC - 1))
                sch.op("tensor", mq, reads=[wqb_t, xnq_t[sl]],
                       writes=[ps_q_t[qp]])

            def evq(e, jc=jc, qp=qp):
                return e.copy(out=qst[qp][:, :], in_=ps_q[qp][:, :])
            sch.op("scalar", evq, reads=[ps_q_t[qp]], writes=[qst_t[qp]])

            def qwh(e, jc=jc, qp=qp, tt=tt):
                return e.dma_start(
                    out=q_dram.ap()[jc, :, tt * 512:(tt + 1) * 512],
                    in_=qst[qp][:, :])
            sch.op("gpsimd", qwh, reads=[qst_t[qp]], writes=[q_dram_t],
                   dma=True)

            def q2f(e, jc=jc, qp=qp):
                return e.tensor_tensor(out=q2b[qp][:, :],
                                       in0=qst[qp][:, :],
                                       in1=qst[qp][:, :],
                                       op=ALU.mult)
            sch.op("vector", q2f, reads=[qst_t[qp]], writes=[q2b_t[qp]])

            def nmm2(e, jc=jc, qp=qp):
                return e.matmul(ps_n2[:, :], ones_b[:, 0:1], q2b[qp][:, :],
                                start=True, stop=True)
            sch.op("tensor", nmm2, reads=[q2b_t[qp], ones_b_t],
                   writes=[ps_n2_t])

            def nred22(e, jc=jc):
                return e.tensor_reduce(out=abq[0][:, 0:1], in_=ps_n2[:, :],
                                       axis=mybir.AxisListType.X, op=ALU.max)
            sch.op("vector", nred22, reads=[ps_n2_t], writes=[abq_t[0]])

            def nacc2(e, jc=jc):
                return e.tensor_tensor(out=qmax[:, jc:jc + 1],
                                       in0=qmax[:, jc:jc + 1],
                                       in1=abq[0][:, 0:1], op=ALU.max)
            sch.op("vector", nacc2, reads=[abq_t[0], qmax_t[jc]],
                   writes=[qmax_t[jc]])


    p.close()
    sch.barrier()

    attn_bf_dram = nc.dram_tensor("attn_bf", [H, 128, TQ], BF16,
                                  kind="Internal")
    p = ExitStack()
    ksb = [p.enter_context(nc.sbuf_tensor(f"ksb{s}", [128, S], BF16))
           for s in range(2)]
    vsb = [p.enter_context(nc.sbuf_tensor(f"vsb{s}", [128, (S // 128) * 128],
                                          BF16)) for s in range(2)]
    qsb = [p.enter_context(nc.sbuf_tensor(f"qsb{s}", [128, TQ], BF16))
           for s in range(2)]
    masks_sb = p.enter_context(nc.sbuf_tensor("masks_sb", [128, CPW * WQ],
                                              BF16))
    mstage = p.enter_context(nc.sbuf_tensor("mstage", [128, WQ], F32))
    P_sb = [p.enter_context(nc.sbuf_tensor(f"P{s}", [128, 1024], BF16))
            for s in range(4)]
    Pm_sb = [p.enter_context(nc.sbuf_tensor(f"Pm{s}", [128, WQ], BF16))
             for s in range(2)]
    pacc_sb = [p.enter_context(nc.sbuf_tensor(f"pacc{s}", [128, WQ], F32))
               for s in range(2)]
    pcast_sb = [p.enter_context(nc.sbuf_tensor(f"pcast{s}", [128, WQ], BF16))
                for s in range(2)]
    bias_all = p.enter_context(nc.sbuf_tensor("bias_all", [128, max(H, 2)],
                                               F32))
    sc_sb = p.enter_context(nc.sbuf_tensor("sc_sb", [1, 2 * max(H, 2)], F32))
    recl_sb = [p.enter_context(nc.sbuf_tensor(f"recl{s}", [1, WQ], F32))
               for s in range(2)]
    rb_sb = [p.enter_context(nc.sbuf_tensor(f"rb{s}", [128, WQ], F32))
             for s in range(2)]
    attn_win = [p.enter_context(nc.sbuf_tensor(f"awin{s}", [128, WQ], BF16))
                for s in range(2)]
    ps_sc = [p.enter_context(nc.psum_tensor(f"ps_sc{s}", [128, 1024], F32))
             for s in range(2)]
    ps_o = [p.enter_context(nc.psum_tensor(f"ps_o{s}", [128, WQ], F32))
            for s in range(2)]
    ps_L = p.enter_context(nc.psum_tensor("ps_L", [1, WQ], F32))
    ps_rb = p.enter_context(nc.psum_tensor("ps_rb", [128, WQ], F32))

    ksb_t = [TBuf() for _ in range(2)]
    vsb_t = [TBuf() for _ in range(2)]
    qsb_t = [TBuf() for _ in range(2)]
    attn_win_t = [TBuf(), TBuf()]
    attn_bf_t = TBuf("attn_bf")
    masks_t = TBuf()
    mstage_t = TBuf()
    P_t = [TBuf() for _ in range(4)]
    Pm_t = [TBuf(), TBuf()]
    pacc_t = [TBuf(), TBuf()]
    pcast_t = [TBuf(), TBuf()]
    bias_all_t = TBuf()
    sc_t = TBuf()
    recl_t = [TBuf() for _ in range(2)]
    rb_t = [TBuf() for _ in range(2)]
    ps_sc_t = [TBuf() for _ in range(2)]
    ps_o_t = [TBuf() for _ in range(2)]
    ps_L_t = TBuf()
    ps_rb_t = TBuf()
    attn_dram_t = TBuf()

    for j in range(CPW):
        def ldm(e, j=j):
            return e.dma_start(out=mstage[:, :], in_=masks_h.ap()[j, :, :])
        sch.op("sync", ldm, writes=[mstage_t], dma=True)

        def cstm(e, j=j):
            return e.tensor_copy(out=masks_sb[:, j * WQ:(j + 1) * WQ],
                                 in_=mstage[:, :])
        sch.op("vector", cstm, reads=[mstage_t], writes=[masks_t])

    wdstg = [p.enter_context(nc.sbuf_tensor(f"wdstg{s}", [128, D], F32))
             for s in range(2)]
    wdbs = [p.enter_context(nc.sbuf_tensor(f"wdbs{s}", [128, D], BF16))
            for s in range(2)]
    wdstg_t = [TBuf(), TBuf()]
    wdbs_t = [TBuf(), TBuf()]
    wdb_dram_t = TBuf()

    def emit_wd_cast(ck):
        sl2 = ck % 2

        def ldwd(e, ck=ck, sl2=sl2):
            return e.dma_start(out=wdstg[sl2][:, :], in_=wdh.ap()[ck, :, :])
        sch.op("sync", ldwd, writes=[wdstg_t[sl2]], dma=True)

        def cstwd(e, ck=ck, sl2=sl2):
            return e.tensor_copy(out=wdbs[sl2][:, :], in_=wdstg[sl2][:, :])
        sch.op("vector", cstwd, reads=[wdstg_t[sl2]], writes=[wdbs_t[sl2]])

        def stwd(e, ck=ck, sl2=sl2):
            return e.dma_start(out=wdb_dram.ap()[ck, :, :], in_=wdbs[sl2][:, :])
        sch.op("gpsimd", stwd, reads=[wdbs_t[sl2]], writes=[wdb_dram_t],
               dma=True)

    def load_head(h):
        sl = h % 2

        def ldk(e, h=h):
            return e.dma_start(out=ksb[sl][:, :], in_=k_dram.ap()[h, :, :])
        sch.op("sync", ldk, reads=[k_dram_t], writes=[ksb_t[sl]], dma=True)

        def ldv(e, h=h):
            return e.dma_start(
                out=vsb[sl].ap().rearrange("p (c n) -> p c n", c=S // 128),
                in_=v_dram.ap()[:, :, h * 128:(h + 1) * 128]
                    .rearrange("c p n -> p c n"))
        sch.op("sync", ldv, reads=[v_dram_t], writes=[vsb_t[sl]], dma=True)

        def ldq(e, h=h):
            return e.dma_start(out=qsb[sl][:, :], in_=q_dram.ap()[h, :, :])
        sch.op("sync", ldq, reads=[q_dram_t], writes=[qsb_t[sl]], dma=True)

    def b_all_m(e):
        return e.tensor_tensor(out=sc_sb[:, 0:H], in0=qmax[:, 0:H],
                               in1=kmax[:, 0:H], op=ALU.mult)
    sch.op("vector", b_all_m, reads=qmax_t + kmax_t, writes=[sc_t])

    def b_all_ln(e):
        return e.activation(out=sc_sb[:, H:2 * H], in_=sc_sb[:, 0:H],
                            func=AF.Ln)
    sch.op("scalar", b_all_ln, reads=[sc_t], writes=[sc_t])

    def b_all_exp(e):
        return e.activation(out=sc_sb[:, H:2 * H], in_=sc_sb[:, H:2 * H],
                            func=AF.Exp, scale=0.5)
    sch.op("scalar", b_all_exp, reads=[sc_t], writes=[sc_t])

    def b_all_sc(e):
        return e.mul(out=sc_sb[:, H:2 * H], in_=sc_sb[:, H:2 * H],
                     mul=-(1.02 * scale))
    sch.op("scalar", b_all_sc, reads=[sc_t], writes=[sc_t])

    def b_all_bc(e):
        return e.matmul(ps_rb[:, 0:H], ones_f[:, :], sc_sb[:, H:2 * H],
                        start=True, stop=True)
    sch.op("tensor", b_all_bc, reads=[sc_t, ones_f_t], writes=[ps_rb_t])

    def b_all_cv(e):
        return e.copy(out=bias_all[:, :], in_=ps_rb[:, 0:H])
    sch.op("scalar", b_all_cv, reads=[ps_rb_t], writes=[bias_all_t])

    def lane_gen(lane):
        sl = lane
        for h in range(lane, H, 2):
            load_head(h)
            yield
            for w in range(NW):
                nk = CPW * (w + 1)
                npair = nk // 2

                def emit_scores(kp):
                    for half in (0, 1):
                        ck = 2 * kp + half

                        def mms(e, h=h, sl=sl, w=w, ck=ck, half=half):
                            return e.matmul(
                                ps_sc[sl][:, half * WQ:(half + 1) * WQ],
                                ksb[sl][:, ck * 128:(ck + 1) * 128],
                                qsb[sl][:, w * WQ:(w + 1) * WQ],
                                start=True, stop=True)
                        sch.op("tensor", mms, reads=[ksb_t[sl], qsb_t[sl]],
                               writes=[ps_sc_t[sl]])

                emit_scores(0)
                for kp in range(npair):
                    ck0, ck1 = 2 * kp, 2 * kp + 1
                    psl = lane * 2 + kp % 2

                    def mexp(e, sl=sl, psl=psl, h=h):
                        return e.activation(out=P_sb[psl][:, :],
                                            in_=ps_sc[sl][:, :], func=AF.Exp,
                                            scale=scale,
                                            bias=bias_all[:, h:h + 1])
                    sch.op("scalar", mexp, reads=[ps_sc_t[sl], bias_all_t],
                           writes=[P_t[psl]])

                    if kp + 1 < npair:
                        emit_scores(kp + 1)

                    for half, ck in ((0, ck0), (1, ck1)):
                        j = ck - CPW * w
                        if j >= 0:
                            def mmask(e, psl=psl, half=half, j=j):
                                return e.tensor_tensor(
                                    out=P_sb[psl][:, half * WQ:
                                                  (half + 1) * WQ],
                                    in0=P_sb[psl][:, half * WQ:
                                                  (half + 1) * WQ],
                                    in1=masks_sb[:, j * WQ:(j + 1) * WQ],
                                    op=ALU.mult)
                            sch.op("vector", mmask, reads=[P_t[psl], masks_t],
                                   writes=[P_t[psl]])

                    if kp == 0:
                        def macc0(e, psl=psl, lane=lane):
                            return e.tensor_tensor(out=pacc_sb[lane][:, :],
                                                   in0=P_sb[psl][:, 0:WQ],
                                                   in1=P_sb[psl][:, WQ:2 * WQ],
                                                   op=ALU.add)
                        sch.op("vector", macc0, reads=[P_t[psl]],
                               writes=[pacc_t[lane]])
                    else:
                        def mpadd(e, psl=psl, lane=lane):
                            return e.tensor_tensor(out=Pm_sb[lane][:, :],
                                                   in0=P_sb[psl][:, 0:WQ],
                                                   in1=P_sb[psl][:, WQ:2 * WQ],
                                                   op=ALU.add)
                        sch.op("vector", mpadd, reads=[P_t[psl]],
                               writes=[Pm_t[lane]])

                        def macc(e, lane=lane):
                            return e.tensor_tensor(out=pacc_sb[lane][:, :],
                                                   in0=pacc_sb[lane][:, :],
                                                   in1=Pm_sb[lane][:, :],
                                                   op=ALU.add)
                        sch.op("vector", macc, reads=[Pm_t[lane],
                                                      pacc_t[lane]],
                               writes=[pacc_t[lane]])

                    for half, ck in ((0, ck0), (1, ck1)):
                        def mmav(e, sl=sl, psl=psl, half=half, ck=ck, kp=kp,
                                 npair=npair):
                            return e.matmul(
                                ps_o[sl][:, :],
                                vsb[sl][:, ck * 128:(ck + 1) * 128],
                                P_sb[psl][:, half * WQ:(half + 1) * WQ],
                                start=(kp == 0 and half == 0),
                                stop=(kp == npair - 1 and half == 1))
                        sch.op("tensor", mmav, reads=[P_t[psl], vsb_t[sl]],
                               writes=[ps_o_t[sl]])
                    yield

                def wcast(e, lane=lane):
                    return e.tensor_copy(out=pcast_sb[lane][:, :],
                                         in_=pacc_sb[lane][:, :])
                sch.op("vector", wcast, reads=[pacc_t[lane]],
                       writes=[pcast_t[lane]])

                def wsum(e, lane=lane):
                    return e.matmul(ps_L[:, :], ones_b[:, 0:1],
                                    pcast_sb[lane][:, :],
                                    start=True, stop=True)
                sch.op("tensor", wsum, reads=[pcast_t[lane], ones_b_t],
                       writes=[ps_L_t])

                def wln(e, lane=lane):
                    return e.activation(out=recl_sb[lane][:, :],
                                        in_=ps_L[:, :], func=AF.Ln)
                sch.op("scalar", wln, reads=[ps_L_t], writes=[recl_t[lane]])

                def wexp(e, lane=lane):
                    return e.activation(out=recl_sb[lane][:, :],
                                        in_=recl_sb[lane][:, :], func=AF.Exp,
                                        scale=-1.0)
                sch.op("scalar", wexp, reads=[recl_t[lane]],
                       writes=[recl_t[lane]])

                def wbc(e, lane=lane):
                    return e.matmul(ps_rb[:, :], ones_f[:, :],
                                    recl_sb[lane][:, :],
                                    start=True, stop=True)
                sch.op("tensor", wbc, reads=[recl_t[lane], ones_f_t],
                       writes=[ps_rb_t])

                def wrbc(e, lane=lane):
                    return e.copy(out=rb_sb[lane][:, :], in_=ps_rb[:, :])
                sch.op("scalar", wrbc, reads=[ps_rb_t], writes=[rb_t[lane]])

                def wnorm(e, h=h, w=w, lane=lane):
                    return e.tensor_tensor(
                        out=attn_win[lane][:, :],
                        in0=ps_o[lane][:, :], in1=rb_sb[lane][:, :],
                        op=ALU.mult)
                sch.op("vector", wnorm, reads=[ps_o_t[lane], rb_t[lane]],
                       writes=[attn_win_t[lane]])

                def wst_a(e, h=h, w=w, lane=lane):
                    return e.dma_start(
                        out=attn_bf_dram.ap()[h, :, w * WQ:(w + 1) * WQ],
                        in_=attn_win[lane][:, :])
                sch.op("gpsimd", wst_a, reads=[attn_win_t[lane]],
                       writes=[attn_bf_t], dma=True)
                yield

            for i in range(NI // H):
                emit_wd_cast(h * (NI // H) + i)
            yield

    _gens = [lane_gen(0), lane_gen(1)]
    _alive = [True, True]
    _gi = 0
    while _alive[0] or _alive[1]:
        if _alive[_gi % 2]:
            try:
                next(_gens[_gi % 2])
            except StopIteration:
                _alive[_gi % 2] = False
        _gi += 1

    p.close()
    sch.barrier()

    p = ExitStack()
    wob = p.enter_context(nc.sbuf_tensor("wob", [128, H * D], BF16))
    atile = [p.enter_context(nc.sbuf_tensor(f"atile{s}", [128, H * 128],
                                            BF16)) for s in range(2)]
    wsto = [p.enter_context(nc.sbuf_tensor(f"wsto{s}", [128, D], F32))
            for s in range(2)]
    ost = [p.enter_context(nc.sbuf_tensor(f"ost{s}", [128, 512], F32))
           for s in range(2)]
    ps_wo = [p.enter_context(nc.psum_tensor(f"ps_wo{s}", [128, 512], F32))
             for s in range(2)]
    wob_t = TBuf()
    atile_t = [TBuf(), TBuf()]
    wsto_t = [TBuf(), TBuf()]
    ost_t = [TBuf(), TBuf()]
    ps_wo_t = [TBuf(), TBuf()]

    for f in range(H):
        sl2 = f % 2

        def ldwo(e, f=f, sl2=sl2):
            return e.dma_start(out=wsto[sl2][:, :],
                               in_=wo_h.ap()[f * 128:(f + 1) * 128, :])
        sch.op("sync", ldwo, writes=[wsto_t[sl2]], dma=True)

        def cstwo(e, f=f, sl2=sl2):
            return e.tensor_copy(out=wob[:, f * D:(f + 1) * D],
                                 in_=wsto[sl2][:, :])
        sch.op("vector", cstwo, reads=[wsto_t[sl2]], writes=[wob_t])

    NJT = D // 512
    for qb in range(TQ // 128):
        asl3 = qb % 2

        def lda3(e, qb=qb, asl3=asl3):
            return e.dma_start(
                out=atile[asl3].ap().rearrange("p (c n) -> p c n", c=H),
                in_=attn_bf_dram.ap()[:, :, qb * 128:(qb + 1) * 128]
                    .rearrange("c p n -> p c n"))
        sch.op("sync", lda3, reads=[attn_bf_t], writes=[atile_t[asl3]],
               dma=True)

        for jt in range(NJT):
            wp = (qb * NJT + jt) % 2
            for f in range(H):
                def mwo(e, qb=qb, jt=jt, f=f, wp=wp, asl3=asl3):
                    return e.matmul(
                        ps_wo[wp][:, :],
                        atile[asl3][:, f * 128:(f + 1) * 128],
                        wob[:, f * D + jt * 512: f * D + jt * 512 + 512],
                        start=(f == 0), stop=(f == H - 1))
                sch.op("tensor", mwo, reads=[atile_t[asl3], wob_t],
                       writes=[ps_wo_t[wp]])

            def evo(e, wp=wp):
                return e.copy(out=ost[wp][:, :], in_=ps_wo[wp][:, :])
            sch.op("scalar", evo, reads=[ps_wo_t[wp]], writes=[ost_t[wp]])

            def sto(e, qb=qb, jt=jt, wp=wp):
                return e.dma_start(
                    out=attn_dram.ap()[qb * 128:(qb + 1) * 128,
                                       jt * 512:(jt + 1) * 512],
                    in_=ost[wp][:, :])
            sch.op("gpsimd", sto, reads=[ost_t[wp]], writes=[attn_dram_t],
                   dma=True)

    p.close()
    sch.barrier()

    p = ExitStack()
    xnq_full = p.enter_context(nc.sbuf_tensor("xnq_full", [128, DC * TQ],
                                              BF16))
    wstg = [p.enter_context(nc.sbuf_tensor(f"wstg{s}", [128, D], F32))
            for s in range(2)]
    wgb = [p.enter_context(nc.sbuf_tensor(f"wgb{s}", [128, D], BF16))
           for s in range(4)]
    wub = [p.enter_context(nc.sbuf_tensor(f"wub{s}", [128, D], BF16))
           for s in range(4)]
    T4 = 512
    hX = [p.enter_context(nc.sbuf_tensor(f"hX{x}", [128, TQ], F32))
          for x in range(4)]
    tmpA = p.enter_context(nc.sbuf_tensor("tmpA", [128, T4], F32))
    tmpB = p.enter_context(nc.sbuf_tensor("tmpB", [128, T4], F32))
    m1_sb = p.enter_context(nc.sbuf_tensor("m1_sb", [128, T4], F32))
    n1_sb = p.enter_context(nc.sbuf_tensor("n1_sb", [128, T4], F32))
    t2_sb = p.enter_context(nc.sbuf_tensor("t2_sb", [128, T4], F32))
    hsp_sb = [p.enter_context(nc.sbuf_tensor(f"hsp{x}", [128, T4], BF16))
              for x in range(4)]
    NT4 = TQ // T4
    ps_g = [p.enter_context(nc.psum_tensor(f"ps_g{s}", [128, T4], F32))
            for s in range(NT4)]
    ps_u = [p.enter_context(nc.psum_tensor(f"ps_u{s}", [128, T4], F32))
            for s in range(NT4)]

    import os as _os
    KDBG = bool(_os.environ.get("KDBG"))
    if KDBG:
        dbg_h_dram = nc.dram_tensor("dbg_h", [NI, 128, T4], F32,
                                    kind="ExternalOutput")
        dbg_h_t = TBuf()

    xnq_full_t = TBuf()
    wstg_t = [TBuf(), TBuf()]
    wgb_t = [TBuf() for _ in range(4)]
    wub_t = [TBuf() for _ in range(4)]
    hX_t = [TBuf() for _ in range(4)]
    tmpA_t = TBuf()
    tmpB_t = TBuf()
    m1_t = TBuf()
    n1_t = TBuf()
    t2_t = TBuf()
    hsp_t = [TBuf() for _ in range(4)]
    ps_g_t = [TBuf() for _ in range(NT4)]
    ps_u_t = [TBuf() for _ in range(NT4)]
    hsp_dram_t = TBuf()

    def ldxnq(e):
        return e.dma_start(
            out=xnq_full.ap().rearrange("p (c n) -> p c n", c=DC),
            in_=xnq_dram.ap().rearrange("c p n -> p c n"))
    sch.op("sync", ldxnq, reads=[xnq_dram_t], writes=[xnq_full_t], dma=True)

    NH2 = T4 // 512
    for quad in range(QUADS):
        for X in range(4):
            ck = X * QUADS + quad
            s0 = (2 * X) % 2
            s1 = (2 * X + 1) % 2

            def ldwg(e, ck=ck, s0=s0):
                return e.dma_start(out=wstg[s0][:, :], in_=wgh.ap()[ck, :, :])
            sch.op("sync", ldwg, writes=[wstg_t[s0]], dma=True)

            def cwg(e, s0=s0, X=X):
                return e.copy(out=wgb[X][:, :], in_=wstg[s0][:, :])
            sch.op("scalar", cwg, reads=[wstg_t[s0]], writes=[wgb_t[X]])

            def ldwu(e, ck=ck, s1=s1):
                return e.dma_start(out=wstg[s1][:, :], in_=wuh.ap()[ck, :, :])
            sch.op("sync", ldwu, writes=[wstg_t[s1]], dma=True)

            def cwu(e, s1=s1, X=X):
                return e.copy(out=wub[X][:, :], in_=wstg[s1][:, :])
            sch.op("scalar", cwu, reads=[wstg_t[s1]], writes=[wub_t[X]])

        for X in range(4):
            for cch in range(DC):
                for tt in range(NT4):
                    def mg(e, X=X, cch=cch, tt=tt):
                        return e.matmul(
                            ps_g[tt][:, :],
                            wgb[X][:, cch * 128:(cch + 1) * 128],
                            xnq_full[:, cch * TQ + tt * T4:
                                     cch * TQ + tt * T4 + T4],
                            start=(cch == 0), stop=(cch == DC - 1))
                    sch.op("tensor", mg, reads=[wgb_t[X], xnq_full_t],
                           writes=[ps_g_t[tt]])
                for tt in range(NT4):
                    def mu(e, X=X, cch=cch, tt=tt):
                        return e.matmul(
                            ps_u[tt][:, :],
                            wub[X][:, cch * 128:(cch + 1) * 128],
                            xnq_full[:, cch * TQ + tt * T4:
                                     cch * TQ + tt * T4 + T4],
                            start=(cch == 0), stop=(cch == DC - 1))
                    sch.op("tensor", mu, reads=[wub_t[X], xnq_full_t],
                           writes=[ps_u_t[tt]])

            for tt in range(NT4):
                def msilu(e, X=X, tt=tt):
                    return e.activation(out=hX[X][:, tt * T4:(tt + 1) * T4],
                                        in_=ps_g[tt][:, :], func=AF.Silu)
                sch.op("scalar", msilu, reads=[ps_g_t[tt]], writes=[hX_t[X]])

                def mh(e, X=X, tt=tt):
                    return e.tensor_tensor(
                        out=hX[X][:, tt * T4:(tt + 1) * T4],
                        in0=hX[X][:, tt * T4:(tt + 1) * T4],
                        in1=ps_u[tt][:, :], op=ALU.mult)
                sch.op("vector", mh, reads=[hX_t[X], ps_u_t[tt]],
                       writes=[hX_t[X]])

        for tt in range(NT4):
            def sp(fn, reads, writes):
                sch.op("vector", fn, reads=reads, writes=writes)

            sp(lambda e, tt=tt: e.tensor_tensor(out=tmpA[:, :], in0=hX[0][:, tt * T4:(tt + 1) * T4],
                                         in1=hX[0][:, tt * T4:(tt + 1) * T4], op=ALU.mult),
               [hX_t[0]], [tmpA_t])
            sp(lambda e, tt=tt: e.tensor_tensor(out=tmpB[:, :], in0=hX[1][:, tt * T4:(tt + 1) * T4],
                                         in1=hX[1][:, tt * T4:(tt + 1) * T4], op=ALU.mult),
               [hX_t[1]], [tmpB_t])
            sp(lambda e, tt=tt: e.tensor_tensor(out=m1_sb[:, :], in0=tmpA[:, :],
                                         in1=tmpB[:, :], op=ALU.max),
               [tmpA_t, tmpB_t], [m1_t])
            sp(lambda e, tt=tt: e.tensor_tensor(out=n1_sb[:, :], in0=tmpA[:, :],
                                         in1=tmpB[:, :], op=ALU.min),
               [tmpA_t, tmpB_t], [n1_t])
            sp(lambda e, tt=tt: e.tensor_tensor(out=tmpA[:, :], in0=hX[2][:, tt * T4:(tt + 1) * T4],
                                         in1=hX[2][:, tt * T4:(tt + 1) * T4], op=ALU.mult),
               [hX_t[2]], [tmpA_t])
            sp(lambda e, tt=tt: e.tensor_tensor(out=tmpB[:, :], in0=hX[3][:, tt * T4:(tt + 1) * T4],
                                         in1=hX[3][:, tt * T4:(tt + 1) * T4], op=ALU.mult),
               [hX_t[3]], [tmpB_t])
            sp(lambda e, tt=tt: e.tensor_tensor(out=t2_sb[:, :], in0=tmpA[:, :],
                                         in1=tmpB[:, :], op=ALU.max),
               [tmpA_t, tmpB_t], [t2_t])
            sp(lambda e, tt=tt: e.tensor_tensor(out=tmpA[:, :], in0=tmpA[:, :],
                                         in1=tmpB[:, :], op=ALU.min),
               [tmpA_t, tmpB_t], [tmpA_t])
            sp(lambda e, tt=tt: e.tensor_tensor(out=m1_sb[:, :], in0=m1_sb[:, :],
                                         in1=t2_sb[:, :], op=ALU.min),
               [m1_t, t2_t], [m1_t])
            sp(lambda e, tt=tt: e.tensor_tensor(out=n1_sb[:, :], in0=n1_sb[:, :],
                                         in1=tmpA[:, :], op=ALU.max),
               [n1_t, tmpA_t], [n1_t])
            sp(lambda e, tt=tt: e.tensor_tensor(out=t2_sb[:, :], in0=m1_sb[:, :],
                                         in1=n1_sb[:, :], op=ALU.max),
               [m1_t, n1_t], [t2_t])

            for X in range(4):
                def f_sq(e, X=X, tt=tt):
                    return e.tensor_tensor(out=tmpB[:, :], in0=hX[X][:, tt * T4:(tt + 1) * T4],
                                           in1=hX[X][:, tt * T4:(tt + 1) * T4], op=ALU.mult)
                sch.op("vector", f_sq, reads=[hX_t[X]], writes=[tmpB_t])

                def f_keep(e, X=X, tt=tt):
                    return e.tensor_tensor(out=tmpB[:, :], in0=tmpB[:, :],
                                           in1=t2_sb[:, :], op=ALU.is_ge)
                sch.op("vector", f_keep, reads=[tmpB_t, t2_t],
                       writes=[tmpB_t])

                def f_app(e, X=X, tt=tt):
                    return e.tensor_tensor(out=hsp_sb[X][:, :],
                                           in0=hX[X][:, tt * T4:(tt + 1) * T4],
                                           in1=tmpB[:, :], op=ALU.mult)
                sch.op("vector", f_app, reads=[hX_t[X], tmpB_t],
                       writes=[hsp_t[X]])

                ck = X * QUADS + quad

                def f_st(e, X=X, ck=ck, tt=tt):
                    return e.dma_start(
                        out=hsp_dram.ap()[ck, :, tt * T4:(tt + 1) * T4],
                        in_=hsp_sb[X][:, :])
                sch.op("gpsimd", f_st, reads=[hsp_t[X]], writes=[hsp_dram_t],
                       dma=True)

    p.close()
    sch.barrier()

    p = ExitStack()
    TS5 = 256
    NTS = TQ // TS5
    hsp_c = [p.enter_context(nc.sbuf_tensor(f"hsp_c{s}", [128, NI * TS5],
                                            BF16)) for s in range(2)]
    wd_jt = p.enter_context(nc.sbuf_tensor("wd_jt", [128, NI * 512], BF16))
    x_tile = [p.enter_context(nc.sbuf_tensor(f"x_t{s}", [128, 512], F32))
              for s in range(2)]
    a_tile = [p.enter_context(nc.sbuf_tensor(f"a_t{s}", [128, 512], F32))
              for s in range(2)]
    o_tile = [p.enter_context(nc.sbuf_tensor(f"o_t{s}", [128, 512], F32))
              for s in range(2)]
    ps_dn = [p.enter_context(nc.psum_tensor(f"ps_dn{s}", [128, 512], F32))
             for s in range(2)]
    hsp_c_t = [TBuf(), TBuf()]
    wd_jt_t = TBuf()
    x_tile_t = [TBuf(), TBuf()]
    a_tile_t = [TBuf(), TBuf()]
    o_tile_t = [TBuf(), TBuf()]
    ps_dn_t = [TBuf(), TBuf()]
    out_t = TBuf()

    NJT5 = D // 512
    for jt in range(NJT5):
        def ldwj(e, jt=jt):
            return e.dma_start(
                out=wd_jt.ap().rearrange("p (c n) -> p c n", c=NI),
                in_=wdb_dram.ap()[:, :, jt * 512:(jt + 1) * 512]
                    .rearrange("c p n -> p c n"))
        sch.op("sync", ldwj, reads=[wdb_dram_t], writes=[wd_jt_t], dma=True)

        for tts in range(NTS):
            hsl = tts % 2

            def ldh(e, tts=tts, hsl=hsl):
                return e.dma_start(
                    out=hsp_c[hsl].ap().rearrange("p (c n) -> p c n", c=NI),
                    in_=hsp_dram.ap()[:, :, tts * TS5:(tts + 1) * TS5]
                        .rearrange("c p n -> p c n"))
            sch.op("sync", ldh, reads=[hsp_dram_t], writes=[hsp_c_t[hsl]],
                   dma=True)

            for tb in range(TS5 // 128):
                dp = tb % 2
                t0 = tts * TS5 + tb * 128

                def ldxt(e, t0=t0, jt=jt, dp=dp):
                    return e.dma_start(
                        out=x_tile[dp][:, :],
                        in_=xq_rows.ap()[t0:t0 + 128,
                                         jt * 512:(jt + 1) * 512])
                sch.op("sync", ldxt, writes=[x_tile_t[dp]], dma=True)

                def ldat(e, t0=t0, jt=jt, dp=dp):
                    return e.dma_start(
                        out=a_tile[dp][:, :],
                        in_=attn_dram.ap()[t0:t0 + 128,
                                           jt * 512:(jt + 1) * 512])
                sch.op("sync", ldat, reads=[attn_dram_t],
                       writes=[a_tile_t[dp]], dma=True)

                for ck in range(NI):
                    def mdn(e, ck=ck, tb=tb, dp=dp, hsl=hsl):
                        return e.matmul(
                            ps_dn[dp][:, :],
                            hsp_c[hsl][:, ck * TS5 + tb * 128:
                                       ck * TS5 + tb * 128 + 128],
                            wd_jt[:, ck * 512:(ck + 1) * 512],
                            start=(ck == 0), stop=(ck == NI - 1))
                    sch.op("tensor", mdn, reads=[hsp_c_t[hsl], wd_jt_t],
                           writes=[ps_dn_t[dp]])

                def madd1(e, dp=dp):
                    return e.tensor_tensor(out=o_tile[dp][:, :],
                                           in0=ps_dn[dp][:, :],
                                           in1=x_tile[dp][:, :], op=ALU.add)
                sch.op("vector", madd1, reads=[ps_dn_t[dp], x_tile_t[dp]],
                       writes=[o_tile_t[dp]])

                def madd2(e, dp=dp):
                    return e.tensor_tensor(out=o_tile[dp][:, :],
                                           in0=o_tile[dp][:, :],
                                           in1=a_tile[dp][:, :], op=ALU.add)
                sch.op("vector", madd2, reads=[o_tile_t[dp], a_tile_t[dp]],
                       writes=[o_tile_t[dp]])

                def stout(e, t0=t0, jt=jt, dp=dp):
                    return e.dma_start(
                        out=out_h.ap()[t0:t0 + 128, jt * 512:(jt + 1) * 512],
                        in_=o_tile[dp][:, :])
                sch.op("gpsimd", stout, reads=[o_tile_t[dp]], writes=[out_t],
                       dma=True)

    last_out = sch.last_by_eng["gpsimd"]

    def fence(e):
        return e.engine_nop()
    sch.op("gpsimd", fence, deps=[last_out])

    p.close()

    sch.emit(nc, ctx)
    return nc



_cached = {}


def _get_program(S, D, H, I):
    key = (S, D, H, I)
    if key not in _cached:
        _cached[key] = build_program(S, D, H, I)
    return _cached[key]


def run(x, wq, wk, wv, wo, wg, wu, wd):
    B, S, D = x.shape
    H = D // DH
    I = wg.shape[1]
    DC = D // 128
    NI = I // 128
    TQ = S // R
    QUADS = I // 512
    CPW = (R * WQ) // 128

    Xs = np.arange(I) // (I // 4)
    ms = np.arange(I) % (I // 4)
    perm = 4 * ms + Xs
    wg_p = np.ascontiguousarray(wg[:, perm])
    wu_p = np.ascontiguousarray(wu[:, perm])
    wd_p = np.ascontiguousarray(wd[perm, :])

    def mk_wh(w_p):
        t = w_p.reshape(DC, 128, 4, QUADS, 128)
        t = t.transpose(2, 3, 1, 0, 4)
        return np.ascontiguousarray(t.reshape(NI, 128, DC * 128))

    wgh = mk_wh(wg_p)
    wuh = mk_wh(wu_p)
    wdh = np.ascontiguousarray(wd_p.reshape(4, QUADS, 128, D)
                               .reshape(NI, 128, D))

    in_maps = []
    xT_cache = {}
    for c in range(8):
        b, l = c // 4, c % 4
        if b not in xT_cache:
            xT_cache[b] = np.ascontiguousarray(x[b].T)
        xq = np.ascontiguousarray(x[b, l::R, :])
        xqT = np.ascontiguousarray(xq.T)
        j = np.arange(CPW)[:, None, None]
        k = np.arange(128)[None, :, None]
        q = np.arange(WQ)[None, None, :]
        masks = ((128 * j + k) <= (R * q + l)).astype(np.float32)
        in_maps.append({
            "xT": xT_cache[b], "xqT": xqT, "xq_rows": xq,
            "wq": wq, "wk": wk, "wv": wv, "wo": wo,
            "wgh": wgh, "wuh": wuh, "wdh": wdh,
            "masks": np.ascontiguousarray(masks),
        })

    nc = _get_program(S, D, H, I)
    res = run_bass_kernel_spmd(nc, in_maps, core_ids=list(range(8)))

    out = np.empty((B, S, D), dtype=np.float32)
    for c in range(8):
        b, l = c // 4, c % 4
        out[b, l::R, :] = res.results[c]["out"]
    return out


def kernel(L0_attn_x, L0_attn_w_q, L0_attn_w_k, L0_attn_w_v, L0_attn_w_o,
           L0_ffn_w_gate, L0_ffn_w_up, L0_ffn_w_down):
    return run(np.asarray(L0_attn_x, dtype=np.float32),
               np.asarray(L0_attn_w_q, dtype=np.float32),
               np.asarray(L0_attn_w_k, dtype=np.float32),
               np.asarray(L0_attn_w_v, dtype=np.float32),
               np.asarray(L0_attn_w_o, dtype=np.float32),
               np.asarray(L0_ffn_w_gate, dtype=np.float32),
               np.asarray(L0_ffn_w_up, dtype=np.float32),
               np.asarray(L0_ffn_w_down, dtype=np.float32))
